# version 03
# baseline (speedup 1.0000x reference)
import numpy as np
import concourse.bass as bass
import concourse.mybir as mybir
from concourse.bass_utils import run_bass_kernel_spmd

F32 = mybir.dt.float32
BF16 = mybir.dt.bfloat16
AF = mybir.ActivationFunctionType
ALU = mybir.AluOpType
AX = mybir.AxisListType

CFG = dict(B=2, T=4096, D=1024, E=1536, H=4096, C=512)
N_CORES = 8


class _S:
    def __init__(self, h):
        self.h = h
        self.n = 0


def _build(cfg):
    B, T, D, E, H, C = cfg["B"], cfg["T"], cfg["D"], cfg["E"], cfg["H"], cfg["C"]
    DT, ET, HT = D // 128, E // 128, H // 128
    PH = T // C
    TI = C // 128
    NPH = B * C
    assert C <= 512 and C % 128 == 0

    nc = bass.Bass("TRN2", target_bir_lowering=False, debug=False,
                   detect_race_conditions=False)

    xd = nc.dram_tensor("x", [B, T, D], F32, kind="ExternalInput").ap()
    yd = nc.dram_tensor("y", [B, T, D], F32, kind="ExternalOutput").ap()
    xv = xd.rearrange("b t d -> (b t) d")
    yv = yd.rearrange("b t d -> (b t) d")

    w_hg = nc.dram_tensor("w_hg", [2 * E, D], F32, kind="ExternalInput").ap()
    w_out = nc.dram_tensor("w_out", [D, E], F32, kind="ExternalInput").ap()
    w1 = nc.dram_tensor("w1", [H, D], F32, kind="ExternalInput").ap()
    w2 = nc.dram_tensor("w2", [H, D], F32, kind="ExternalInput").ap()
    h1 = nc.dram_tensor("h1", [H, H], F32, kind="ExternalInput").ap()
    h2 = nc.dram_tensor("h2", [H, H], F32, kind="ExternalInput").ap()
    w3 = nc.dram_tensor("w3", [D, H], F32, kind="ExternalInput").ap()
    s1d = nc.dram_tensor("s1", [128, HT], F32, kind="ExternalInput").ap()
    s2d = nc.dram_tensor("s2", [128, HT], F32, kind="ExternalInput").ap()
    h1sd = nc.dram_tensor("h1s", [128, HT], F32, kind="ExternalInput").ap()
    h2sd = nc.dram_tensor("h2s", [128, HT], F32, kind="ExternalInput").ap()
    w3sd = nc.dram_tensor("w3s", [128, DT], F32, kind="ExternalInput").ap()
    cId = nc.dram_tensor("cI", [128, 128], F32, kind="ExternalInput").ap()
    cJd = nc.dram_tensor("cJ", [128, 128], F32, kind="ExternalInput").ap()

    dbg = cfg.get("dbg", False)
    if dbg:
        d_xnfm = nc.dram_tensor("d_xnfm", [128, DT * NPH], BF16, kind="ExternalOutput").ap()
        d_xnrev = nc.dram_tensor("d_xnrev", [128, DT * NPH], BF16, kind="ExternalOutput").ap()
        d_states = nc.dram_tensor("d_states", [128, ET * NPH], BF16, kind="ExternalOutput").ap()
        d_ha = nc.dram_tensor("d_ha", [128, HT * C], BF16, kind="ExternalOutput").ap()
        d_acc = nc.dram_tensor("d_acc", [128, DT * C], F32, kind="ExternalOutput").ap()

    def scratch(name, MT, KT):
        t = nc.dram_tensor(name, [MT, 128, KT * 128], BF16, kind="Internal").ap()
        return (t, MT, KT)
    whg_s = scratch("whg_s", 2 * ET, DT)
    wout_s = scratch("wout_s", DT, ET)
    w1_s = scratch("w1_s", HT, DT)
    w2_s = scratch("w2_s", HT, DT)
    h1_s = scratch("h1_s", HT, HT)
    h2_s = scratch("h2_s", HT, HT)
    w3_s = scratch("w3_s", DT, HT)

    from contextlib import ExitStack
    stk = ExitStack()
    sb = lambda name, shape, dt=F32: stk.enter_context(nc.sbuf_tensor(name, shape, dt))

    cI = sb("cIt", [128, 128]); cJ = sb("cJt", [128, 128])
    s1t = sb("s1t", [128, HT]); s2t = sb("s2t", [128, HT])
    h1st = sb("h1st", [128, HT]); h2st = sb("h2st", [128, HT])
    w3st = sb("w3st", [128, DT])
    epsT = sb("epsT", [128, 1])
    x_tm = [sb(f"x_tm{i}", [128, D]) for i in range(2)]
    xn_tm = sb("xn_tm", [128, D])
    stat = sb("stat", [128, 8])
    xn_fm = sb("xn_fm", [128, DT * NPH], BF16)
    xn_rev = sb("xn_rev", [128, DT * NPH], BF16)
    states = sb("states", [128, ET * NPH], BF16)
    carry = sb("carry", [128, 4 * ET])
    zb = sb("zb", [128, C]); cbuf = sb("cbuf", [128, C]); sgb = sb("sgb", [128, C])
    tb = sb("tb", [128, C]); gbuf = sb("gbuf", [128, C]); vbuf = sb("vbuf", [128, C])
    fbA = sb("fbA", [128, C]); fbB = sb("fbB", [128, C])
    Bh1 = sb("Bh1", [128, HT * C], BF16)
    Bh2 = sb("Bh2", [128, HT * C], BF16)
    acc = sb("acc", [128, DT * C])
    obuf = [sb(f"obuf{i}", [128, D]) for i in range(2)]
    slab_big = [sb(f"slabB{i}", [128, HT * 128], BF16) for i in range(2)]
    slab_sml = [sb(f"slabS{i}", [128, max(DT, ET) * 128], BF16) for i in range(3)]
    tsil = sb("tsil", [128, 2 * C])

    ps = [stk.enter_context(nc.psum_tensor(f"ps{i}", [128, 512], F32)) for i in range(8)]

    mksem = lambda name: _S(stk.enter_context(nc.semaphore(name)))
    SXC = mksem("SXC")
    SXT = [mksem(f"SXT{i}") for i in range(2)]
    SN = [mksem(f"SN{i}") for i in range(2)]
    SC = [mksem(f"SC{i}") for i in range(2)]
    SBG = [mksem(f"SBG{i}") for i in range(3)]
    SSL = [mksem(f"SSL{i}") for i in range(3)]
    SO = [mksem(f"SO{i}") for i in range(2)]
    SPE = mksem("SPE")
    SACT = mksem("SACT")
    SDVE = mksem("SDVE")
    SDBG = mksem("SDBG")

    Q = dict(sync=[], scalar=[], vector=[], tensor=[], gpsimd=[])
    _last_wait = {}

    def w(eng, sem, val):
        if val <= 0:
            return
        key = (eng, id(sem))
        if _last_wait.get(key, 0) >= val:
            return
        _last_wait[key] = val
        Q[eng].append(lambda e, s=sem, v=val: e.wait_ge(s.h, v))

    def op(eng, f, inc=None, n=1):
        if inc is not None:
            inc.n += n
            Q[eng].append(lambda e, f=f, s=inc, n=n: f(e).then_inc(s.h, n))
            return inc.n
        Q[eng].append(lambda e, f=f: f(e))
        return None

    bank_free = {i: (SPE, 0) for i in range(8)}

    def wait_banks(eng, banks):
        for b in banks:
            sm, v = bank_free[b]
            w(eng, sm, v)

    for dst, src in ((cI, cId), (cJ, cJd), (s1t, s1d), (s2t, s2d),
                     (h1st, h1sd), (h2st, h2sd), (w3st, w3sd)):
        op("sync", lambda e, d=dst, s=src: e.dma_start(out=d[:, :], in_=s), SXC, 16)
    setup_loads = SXC.n
    op("vector", lambda e: e.memset(carry[:, :], 0.0), SDVE)
    op("vector", lambda e: e.memset(epsT[:, :], 1e-5), SDVE)

    maxK = max(D, E, H)
    if HT * C // 2 >= 2 * maxK and 2 * DT * C >= 2 * maxK:
        natw = [Bh1[:].bitcast(F32)[:, 0:maxK], Bh1[:].bitcast(F32)[:, maxK:2 * maxK]]
        natws = [Bh2[:].bitcast(F32)[:, 0:maxK], Bh2[:].bitcast(F32)[:, maxK:2 * maxK]]
        staging = [acc[:].bitcast(BF16)[:, 0:maxK], acc[:].bitcast(BF16)[:, maxK:2 * maxK]]
    else:
        natw_t = sb("natw_t", [128, 2 * maxK])
        natws_t = sb("natws_t", [128, 2 * maxK])
        staging_t = sb("staging_t", [128, 2 * maxK], BF16)
        natw = [natw_t[:, 0:maxK], natw_t[:, maxK:2 * maxK]]
        natws = [natws_t[:, 0:maxK], natws_t[:, maxK:2 * maxK]]
        staging = [staging_t[:, 0:maxK], staging_t[:, maxK:2 * maxK]]
    rp = dict(pe_done=[0, 0], act_done=[0, 0], sc_done=[0, 0])

    w("scalar", SXC, setup_loads)
    w("tensor", SXC, setup_loads)

    def repack(wdram, scr3, scale_t):
        scr, MT, KT = scr3
        K = KT * 128
        for mt in range(MT):
            slot = mt % 2
            w("sync", SPE, rp["pe_done"][slot])
            w("sync", SACT, rp["act_done"][slot])
            w("sync", SC[slot], rp["sc_done"][slot])
            op("sync", lambda e, sl=slot, mt=mt, K=K, wd=wdram:
               e.dma_start(out=natw[sl][:, 0:K], in_=wd[mt * 128:(mt + 1) * 128, 0:K]),
               SN[slot], 16)
            ldn = SN[slot].n
            if scale_t is not None:
                w("scalar", SN[slot], ldn)
                w("scalar", SPE, rp["pe_done"][slot])
                op("scalar", lambda e, sl=slot, K=K, st_=scale_t, mt=mt:
                   e.activation(out=natws[sl][:, 0:K], in_=natw[sl][:, 0:K],
                                func=AF.Copy, scale=st_[:, mt:mt + 1]), SACT)
                rp["act_done"][slot] = SACT.n
                w("tensor", SACT, SACT.n)
                src = natws[slot]
            else:
                w("tensor", SN[slot], ldn)
                src = natw[slot]
            evs = []
            for kt in range(KT):
                bq = (kt // 4) % 2
                if kt % 4 == 0:
                    wait_banks("tensor", [bq])
                last = op("tensor", lambda e, kt=kt, bq=bq, src=src:
                          e.transpose(ps[bq][:, (kt % 4) * 128:(kt % 4) * 128 + 128],
                                      src[:, kt * 128:(kt + 1) * 128], cI[:, :]),
                          SPE)
                if kt % 4 == 3 or kt == KT - 1:
                    nt = (kt % 4) + 1
                    w("scalar", SPE, last)
                    if kt < 4:
                        w("scalar", SC[slot], rp["sc_done"][slot])
                    evn = op("scalar", lambda e, sl=slot, kt=kt, bq=bq, nt=nt:
                             e.copy(out=staging[sl][:, (kt - nt + 1) * 128:(kt + 1) * 128],
                                    in_=ps[bq][:, 0:nt * 128]), SACT)
                    bank_free[bq] = (SACT, evn)
                    evs.append(evn)
            rp["pe_done"][slot] = SPE.n
            w("sync", SACT, evs[-1])
            op("sync", lambda e, sl=slot, mt=mt, KT=KT, scr=scr:
               e.dma_start(out=scr[mt:mt + 1, :, :].rearrange("m p j -> (m p) j"),
                           in_=staging[sl][:, 0:KT * 128]),
               SC[slot], 16)
            rp["sc_done"][slot] = SC[slot].n

    repack(w_hg, whg_s, None)
    repack(w_out, wout_s, None)
    repack(w1, w1_s, s1t)
    repack(w2, w2_s, s2t)
    repack(h1, h1_s, h1st)
    repack(h2, h2_s, h2st)
    repack(w3, w3_s, w3st)
    w("sync", SC[0], SC[0].n)
    w("sync", SC[1], SC[1].n)

    slab_free = {"big": [(SPE, 0)] * 3, "sml": [(SPE, 0)] * 3}
    slab_idx = {"big": 0, "sml": 0}

    def load_slab(scr3, mt, big):
        scr, MT, KT = scr3
        kind = "big" if big else "sml"
        pool = slab_big if big else slab_sml
        sgrp = SBG if big else SSL
        sl = slab_idx[kind] % len(pool)
        slab_idx[kind] += 1
        sm, v = slab_free[kind][sl]
        w("sync", sm, v)
        op("sync", lambda e, sl=sl, mt=mt, KT=KT, scr=scr, pool=pool:
           e.dma_start(out=pool[sl][:, 0:KT * 128],
                       in_=scr[mt:mt + 1, :, :].rearrange("m p j -> (m p) j")),
           sgrp[sl], 16)
        return pool[sl], (kind, sl), sgrp[sl], sgrp[sl].n

    x_slot_free = [(SDVE, 0), (SDVE, 0)]
    o_slot_free = [0, 0]
    xnfm_reader = 0
    xnrev_reader = 0
    states_reader = 0
    Bh1_pe_reader = 0
    Bh2_pe_reader = 0
    acc_pe_reader = 0
    tsil_reader = [0, 0]
    prev_scanbufs_free = 0
    prev_stat_done = 0
    prev_xn_tr = 0
    n_groups = 0

    for phn in range(PH):
        ch_f, ch_b = phn, PH - 1 - phn

        w("scalar", SPE, max(xnfm_reader, xnrev_reader))
        tt = 0
        for kind, chunk in (("f", ch_f), ("b", ch_b)):
            for seq in range(B):
                for ti in range(TI):
                    tok0 = chunk * C + ti * 128
                    sl = tt % 2
                    sm, v = x_slot_free[sl]
                    w("sync", sm, v)
                    op("sync", lambda e, sl=sl, seq=seq, tok0=tok0:
                       e.dma_start(out=x_tm[sl][:, :],
                                   in_=xv[seq * T + tok0:seq * T + tok0 + 128, :]),
                       SXT[sl], 16)
                    ldn = SXT[sl].n
                    w("vector", SXT[sl], ldn)
                    op("vector", lambda e, sl=sl:
                       e.reduce_sum(stat[:, 0:1], x_tm[sl][:, :], axis=AX.X), SDVE)
                    osl = tt % 2
                    w("scalar", SXT[sl], ldn)
                    w("scalar", SO[osl], o_slot_free[osl])
                    w("scalar", SDVE, prev_stat_done)
                    sqn = op("scalar", lambda e, sl=sl, osl=osl:
                             e.activation(out=obuf[osl][:, :], in_=x_tm[sl][:, :],
                                          func=AF.Square), SACT)
                    w("vector", SACT, sqn)
                    s1n = op("vector", lambda e, osl=osl:
                             e.reduce_sum(stat[:, 1:2], obuf[osl][:, :], axis=AX.X), SDVE)
                    w("scalar", SDVE, s1n)
                    op("scalar", lambda e: e.activation(out=stat[:, 2:3], in_=stat[:, 0:1],
                                                        func=AF.Square, scale=1.0 / D), SACT)
                    op("scalar", lambda e: e.activation(out=stat[:, 3:4], in_=stat[:, 0:1],
                                                        func=AF.Copy, scale=1.0 / D), SACT)
                    msn = op("scalar", lambda e: e.activation(out=stat[:, 4:5], in_=stat[:, 1:2],
                                                              func=AF.Copy, scale=1.0 / D), SACT)
                    w("vector", SACT, msn)
                    vn = op("vector", lambda e: e.tensor_tensor(out=stat[:, 5:6], in0=stat[:, 4:5],
                                                                in1=stat[:, 2:3], op=ALU.subtract), SDVE)
                    w("scalar", SDVE, vn)
                    sqn2 = op("scalar", lambda e: e.activation(out=stat[:, 6:7], in_=stat[:, 5:6],
                                                               func=AF.Sqrt, bias=epsT[:, 0:1]), SACT)
                    w("vector", SACT, sqn2)
                    rcn = op("vector", lambda e: e.reciprocal(out=stat[:, 7:8], in_=stat[:, 6:7]), SDVE)
                    w("scalar", SDVE, rcn)
                    rbn = op("scalar", lambda e: e.activation(out=stat[:, 5:6], in_=stat[:, 7:8],
                                                              func=AF.Copy), SACT)
                    w("vector", SACT, rbn)
                    w("vector", SPE, prev_xn_tr)
                    xn_done = op("vector", lambda e, sl=sl: e.tensor_scalar(
                        out=xn_tm[:, :], in0=x_tm[sl][:, :], scalar1=stat[:, 3:4],
                        scalar2=stat[:, 5:6], op0=ALU.subtract, op1=ALU.mult), SDVE)
                    prev_stat_done = xn_done
                    x_slot_free[sl] = (SDVE, xn_done)
                    bq = tt % 2
                    b0, b1 = (4, 5) if bq == 0 else (6, 7)
                    w("tensor", SDVE, xn_done)
                    wait_banks("tensor", [b0, b1])
                    idmat = cI if kind == "f" else cJ
                    half = DT // 2
                    for dt in range(DT):
                        bank = b0 if dt < half else b1
                        colp = (dt % half) * 128
                        op("tensor", lambda e, bank=bank, colp=colp, dt=dt, idmat=idmat:
                           e.transpose(ps[bank][:, colp:colp + 128],
                                       xn_tm[:, dt * 128:(dt + 1) * 128], idmat[:, :]), SPE)
                    tr_done = SPE.n
                    prev_xn_tr = tr_done
                    dst_t = xn_fm if kind == "f" else xn_rev
                    cti = ti if kind == "f" else (TI - 1 - ti)
                    col = seq * C + cti * 128
                    w("scalar", SPE, tr_done)
                    ev1 = op("scalar", lambda e, b0=b0, dst_t=dst_t, col=col, half=half:
                             e.copy(out=dst_t[:, :].rearrange("p (dt n) -> p dt n", n=NPH)[:, 0:half, col:col + 128],
                                    in_=ps[b0][:, 0:half * 128].rearrange("p (dt j) -> p dt j", j=128)),
                             SACT)
                    ev2 = op("scalar", lambda e, b1=b1, dst_t=dst_t, col=col, half=half:
                             e.copy(out=dst_t[:, :].rearrange("p (dt n) -> p dt n", n=NPH)[:, half:DT, col:col + 128],
                                    in_=ps[b1][:, 0:(DT - half) * 128].rearrange("p (dt j) -> p dt j", j=128)),
                             SACT)
                    bank_free[b0] = (SACT, ev1)
                    bank_free[b1] = (SACT, ev2)
                    tt += 1
        xn_ready = SACT.n
        if dbg and phn == 0:
            w("gpsimd", SACT, xn_ready)
            op("gpsimd", lambda e: e.dma_start(out=d_xnfm, in_=xn_fm[:, :]), SDBG, 16)
            op("gpsimd", lambda e: e.dma_start(out=d_xnrev, in_=xn_rev[:, :]), SDBG, 16)

        w("vector", SPE, states_reader)
        for pr in range(ET):
            hid_slab, hid_slot, hid_sm, hid_ld = load_slab(whg_s, pr, big=False)
            gat_slab, gat_slot, gat_sm, gat_ld = load_slab(whg_s, ET + pr, big=False)
            for dir_, seq in ((0, 0), (1, 0), (0, 1), (1, 1)):
                qq = n_groups % 2
                p_hid, p_gat = ps[2 * qq], ps[2 * qq + 1]
                src = xn_fm if dir_ == 0 else xn_rev
                w("tensor", hid_sm, hid_ld)
                w("tensor", gat_sm, gat_ld)
                w("tensor", SACT, xn_ready)
                wait_banks("tensor", [2 * qq, 2 * qq + 1])
                hg_done = None
                for which, pps, slabb in ((0, p_hid, hid_slab), (1, p_gat, gat_slab)):
                    for kt in range(DT):
                        is_last = (which == 1 and kt == DT - 1)
                        hg_done = op("tensor", lambda e, pps=pps, slabb=slabb, kt=kt, src=src, seq=seq:
                                     e.matmul(pps[:, 0:C],
                                              slabb[:, kt * 128:(kt + 1) * 128],
                                              src[:, kt * NPH + seq * C:kt * NPH + seq * C + C],
                                              start=(kt == 0), stop=(kt == DT - 1)),
                                     SPE if is_last else None)
                if dir_ == 0:
                    xnfm_reader = max(xnfm_reader, SPE.n)
                else:
                    xnrev_reader = max(xnrev_reader, SPE.n)
                if (dir_, seq) == (1, 1):
                    slab_free[hid_slot[0]][hid_slot[1]] = (SPE, SPE.n)
                    slab_free[gat_slot[0]][gat_slot[1]] = (SPE, SPE.n)
                w("scalar", SPE, hg_done)
                w("scalar", SDVE, prev_scanbufs_free)
                op("scalar", lambda e, p_gat=p_gat: e.activation(out=zb[:, 0:C], in_=p_gat[:, 0:C],
                                                                 func=AF.Sigmoid), SACT)
                op("scalar", lambda e, p_gat=p_gat: e.activation(out=cbuf[:, 0:C], in_=p_gat[:, 0:C],
                                                                 func=AF.Sigmoid, scale=-1.0), SACT)
                prep = op("scalar", lambda e, p_hid=p_hid: e.activation(out=sgb[:, 0:C], in_=p_hid[:, 0:C],
                                                                        func=AF.Sigmoid), SACT)
                w("vector", SACT, prep)
                tbn = op("vector", lambda e, p_hid=p_hid: e.scalar_tensor_tensor(
                    out=tb[:, 0:C], in0=p_hid[:, 0:C], scalar=0.5, in1=sgb[:, 0:C],
                    op0=ALU.add, op1=ALU.subtract), SDVE)
                bank_free[2 * qq] = (SDVE, tbn)
                bank_free[2 * qq + 1] = (SDVE, tbn)
                op("vector", lambda e: e.scalar_tensor_tensor(
                    out=gbuf[:, 0:C], in0=tb[:, 0:C], scalar=0.0, in1=sgb[:, 0:C],
                    op0=ALU.max, op1=ALU.add))
                op("vector", lambda e: e.tensor_tensor(out=vbuf[:, 0:C], in0=zb[:, 0:C],
                                                       in1=gbuf[:, 0:C], op=ALU.mult))
                fb = fbA if dir_ == 0 else fbB
                cc = (dir_ * B + seq) * ET + pr
                scn = op("vector", lambda e, fb=fb, cc=cc: e.tensor_tensor_scan(
                    out=fb[:, 0:C], data0=cbuf[:, 0:C], data1=vbuf[:, 0:C],
                    initial=carry[:, cc:cc + 1], op0=ALU.mult, op1=ALU.add), SDVE)
                prev_scanbufs_free = scn
                if dir_ == 1:
                    op("vector", lambda e, pr=pr, seq=seq: e.tensor_tensor(
                        out=states[:, pr * NPH + seq * C:pr * NPH + seq * C + C],
                        in0=fbA[:, 0:C], in1=fbB[:, 0:C], op=ALU.add), SDVE)
                    ccf = (0 * B + seq) * ET + pr
                    op("vector", lambda e, ccf=ccf:
                       e.tensor_copy(out=carry[:, ccf:ccf + 1], in_=fbA[:, C - 1:C]))
                    op("vector", lambda e, cc=cc:
                       e.tensor_copy(out=carry[:, cc:cc + 1], in_=fbB[:, C - 1:C]))
                n_groups += 1
        states_ready = SDVE.n
        if dbg and phn == 0:
            w("gpsimd", SDVE, states_ready)
            op("gpsimd", lambda e: e.dma_start(out=d_states, in_=states[:, :]), SDBG, 16)

        for seq in range(B):
            for mt in range(HT):
                s1b, s1slot, s1sm, ld1 = load_slab(w1_s, mt, big=False)
                s2b, s2slot, s2sm, ld2 = load_slab(w2_s, mt, big=False)
                qq = mt % 2
                pw1, pw2 = ps[2 * qq], ps[2 * qq + 1]
                w("tensor", s1sm, ld1)
                w("tensor", s2sm, ld2)
                wait_banks("tensor", [2 * qq, 2 * qq + 1])
                w1_done = w2_done = None
                for pps, slabb in ((pw1, s1b), (pw2, s2b)):
                    for kt in range(DT):
                        mk = op("tensor", lambda e, pps=pps, slabb=slabb, kt=kt, seq=seq:
                                e.matmul(pps[:, 0:C], slabb[:, kt * 128:(kt + 1) * 128],
                                         xn_fm[:, kt * NPH + seq * C:kt * NPH + seq * C + C],
                                         start=(kt == 0), stop=(kt == DT - 1)),
                                SPE if kt == DT - 1 else None)
                        if kt == DT - 1:
                            if pps is pw1:
                                w1_done = mk
                            else:
                                w2_done = mk
                xnfm_reader = max(xnfm_reader, SPE.n)
                slab_free[s1slot[0]][s1slot[1]] = (SPE, SPE.n)
                slab_free[s2slot[0]][s2slot[1]] = (SPE, SPE.n)
                tsl = mt % 2
                w("scalar", SPE, w1_done)
                w("scalar", SDVE, tsil_reader[tsl])
                sil = op("scalar", lambda e, pw1=pw1, tsl=tsl:
                         e.activation(out=tsil[:, tsl * C:tsl * C + C], in_=pw1[:, 0:C],
                                      func=AF.Silu), SACT)
                bank_free[2 * qq] = (SACT, sil)
                w("vector", SACT, sil)
                w("vector", SPE, max(w2_done, Bh1_pe_reader))
                ha = op("vector", lambda e, pw2=pw2, mt=mt, tsl=tsl: e.tensor_tensor(
                    out=Bh1[:, mt * C:(mt + 1) * C], in0=tsil[:, tsl * C:tsl * C + C],
                    in1=pw2[:, 0:C], op=ALU.mult), SDVE)
                tsil_reader[tsl] = ha
                bank_free[2 * qq + 1] = (SDVE, ha)
            ha_ready = SDVE.n
            if dbg and phn == 0 and seq == 0:
                w("gpsimd", SDVE, ha_ready)
                op("gpsimd", lambda e: e.dma_start(out=d_ha, in_=Bh1[:, :]), SDBG, 16)

            for mt in range(HT):
                hb, hslot, hsm, ldh = load_slab(h1_s, mt, big=True)
                qq = mt % 2
                pp = ps[qq]
                w("tensor", hsm, ldh)
                w("tensor", SDVE, ha_ready)
                wait_banks("tensor", [qq])
                mm_done = None
                for kt in range(HT):
                    mm_done = op("tensor", lambda e, pp=pp, hb=hb, kt=kt:
                                 e.matmul(pp[:, 0:C], hb[:, kt * 128:(kt + 1) * 128],
                                          Bh1[:, kt * C:(kt + 1) * C],
                                          start=(kt == 0), stop=(kt == HT - 1)),
                                 SPE if kt == HT - 1 else None)
                Bh1_pe_reader = max(Bh1_pe_reader, SPE.n)
                slab_free[hslot[0]][hslot[1]] = (SPE, SPE.n)
                w("scalar", SPE, max(mm_done, Bh2_pe_reader))
                evn = op("scalar", lambda e, pp=pp, mt=mt:
                         e.activation(out=Bh2[:, mt * C:(mt + 1) * C], in_=pp[:, 0:C],
                                      func=AF.Gelu), SACT)
                bank_free[qq] = (SACT, evn)
            hb_ready = SACT.n

            h1_reader_spe = Bh1_pe_reader
            for mt in range(HT):
                hb, hslot, hsm, ldh = load_slab(h2_s, mt, big=True)
                qq = mt % 2
                pp = ps[2 + qq]
                w("tensor", hsm, ldh)
                w("tensor", SACT, hb_ready)
                wait_banks("tensor", [2 + qq])
                mm_done = None
                for kt in range(HT):
                    mm_done = op("tensor", lambda e, pp=pp, hb=hb, kt=kt:
                                 e.matmul(pp[:, 0:C], hb[:, kt * 128:(kt + 1) * 128],
                                          Bh2[:, kt * C:(kt + 1) * C],
                                          start=(kt == 0), stop=(kt == HT - 1)),
                                 SPE if kt == HT - 1 else None)
                Bh2_pe_reader = max(Bh2_pe_reader, SPE.n)
                slab_free[hslot[0]][hslot[1]] = (SPE, SPE.n)
                w("scalar", SPE, max(mm_done, h1_reader_spe))
                evn = op("scalar", lambda e, pp=pp, mt=mt:
                         e.activation(out=Bh1[:, mt * C:(mt + 1) * C], in_=pp[:, 0:C],
                                      func=AF.Gelu), SACT)
                bank_free[2 + qq] = (SACT, evn)
            hc_ready = SACT.n

            for mt in range(DT):
                wob, woslot, wosm, ldo = load_slab(wout_s, mt, big=False)
                w3b, w3slot, w3sm, ld3 = load_slab(w3_s, mt, big=True)
                qq = mt % 2
                pp = ps[qq]
                w("tensor", wosm, ldo)
                w("tensor", w3sm, ld3)
                w("tensor", SDVE, states_ready)
                w("tensor", SACT, hc_ready)
                wait_banks("tensor", [qq])
                mm_done = None
                for kt in range(ET):
                    op("tensor", lambda e, pp=pp, wob=wob, kt=kt, seq=seq:
                       e.matmul(pp[:, 0:C], wob[:, kt * 128:(kt + 1) * 128],
                                states[:, kt * NPH + seq * C:kt * NPH + seq * C + C],
                                start=(kt == 0), stop=False))
                for kt in range(HT):
                    mm_done = op("tensor", lambda e, pp=pp, w3b=w3b, kt=kt:
                                 e.matmul(pp[:, 0:C], w3b[:, kt * 128:(kt + 1) * 128],
                                          Bh1[:, kt * C:(kt + 1) * C],
                                          start=False, stop=(kt == HT - 1)),
                                 SPE if kt == HT - 1 else None)
                states_reader = max(states_reader, SPE.n)
                Bh1_pe_reader = max(Bh1_pe_reader, SPE.n)
                slab_free[woslot[0]][woslot[1]] = (SPE, SPE.n)
                slab_free[w3slot[0]][w3slot[1]] = (SPE, SPE.n)
                w("scalar", SPE, max(mm_done, acc_pe_reader))
                evn = op("scalar", lambda e, pp=pp, mt=mt:
                         e.copy(out=acc[:, mt * C:(mt + 1) * C], in_=pp[:, 0:C]), SACT)
                bank_free[qq] = (SACT, evn)
            acc_ready = SACT.n
            if dbg and phn == 0 and seq == 0:
                w("gpsimd", SACT, acc_ready)
                op("gpsimd", lambda e: e.dma_start(out=d_acc, in_=acc[:, :]), SDBG, 16)

            for ti in range(TI):
                tok0 = ch_f * C + ti * 128
                bq = ti % 2
                b0, b1 = (4, 5) if bq == 0 else (6, 7)
                half = DT // 2
                w("tensor", SACT, acc_ready)
                wait_banks("tensor", [b0, b1])
                tr_done = None
                for dt in range(DT):
                    bank = b0 if dt < half else b1
                    colp = (dt % half) * 128
                    tr_done = op("tensor", lambda e, bank=bank, colp=colp, dt=dt, ti=ti:
                                 e.transpose(ps[bank][:, colp:colp + 128],
                                             acc[:, dt * C + ti * 128:dt * C + ti * 128 + 128],
                                             cI[:, :]),
                                 SPE if dt == DT - 1 else None)
                acc_pe_reader = max(acc_pe_reader, SPE.n)
                xsl = ti % 2
                sm, v = x_slot_free[xsl]
                w("sync", sm, v)
                op("sync", lambda e, xsl=xsl, seq=seq, tok0=tok0:
                   e.dma_start(out=x_tm[xsl][:, :],
                               in_=xv[seq * T + tok0:seq * T + tok0 + 128, :]),
                   SXT[xsl], 16)
                xldn = SXT[xsl].n
                osl = ti % 2
                w("vector", SPE, tr_done)
                w("vector", SXT[xsl], xldn)
                w("vector", SO[osl], o_slot_free[osl])
                op("vector", lambda e, osl=osl, xsl=xsl, b0=b0, half=half: e.tensor_tensor(
                    out=obuf[osl][:, 0:half * 128], in0=x_tm[xsl][:, 0:half * 128],
                    in1=ps[b0][:, 0:half * 128], op=ALU.add))
                addn = op("vector", lambda e, osl=osl, xsl=xsl, b1=b1, half=half: e.tensor_tensor(
                    out=obuf[osl][:, half * 128:D], in0=x_tm[xsl][:, half * 128:D],
                    in1=ps[b1][:, 0:(DT - half) * 128], op=ALU.add), SDVE)
                bank_free[b0] = (SDVE, addn)
                bank_free[b1] = (SDVE, addn)
                x_slot_free[xsl] = (SDVE, addn)
                w("gpsimd", SDVE, addn)
                op("gpsimd", lambda e, osl=osl, seq=seq, tok0=tok0:
                   e.dma_start(out=yv[seq * T + tok0:seq * T + tok0 + 128, :],
                               in_=obuf[osl][:, :]),
                   SO[osl], 16)
                o_slot_free[osl] = SO[osl].n

    w("gpsimd", SO[0], SO[0].n)
    w("gpsimd", SO[1], SO[1].n)
    if dbg:
        w("gpsimd", SDBG, SDBG.n)

    with nc.Block() as block:
        @block.sync
        def _(e):
            for f in Q["sync"]:
                f(e)

        @block.scalar
        def _(e):
            for f in Q["scalar"]:
                f(e)

        @block.vector
        def _(e):
            for f in Q["vector"]:
                f(e)

        @block.tensor
        def _(e):
            for f in Q["tensor"]:
                f(e)

        @block.gpsimd
        def _(e):
            for f in Q["gpsimd"]:
                f(e)

    stk.close()
    return nc


def _layout_scale(s, MT):
    s = np.asarray(s, dtype=np.float32)
    return np.ascontiguousarray(s.reshape(MT, 128).T)


def kernel(L0_attn_x, L0_attn_W_hg, L0_attn_W_out, L0_ffn_w1_w, L0_ffn_w1_s,
           L0_ffn_w2_w, L0_ffn_w2_s, L0_ffn_h1_w, L0_ffn_h1_s, L0_ffn_h2_w,
           L0_ffn_h2_s, L0_ffn_w3_w, L0_ffn_w3_s):
    cfg = CFG
    B, T, D, E, H = cfg["B"], cfg["T"], cfg["D"], cfg["E"], cfg["H"]
    DT, HT = D // 128, H // 128

    x = np.ascontiguousarray(np.asarray(L0_attn_x, dtype=np.float32))
    w_hg = np.ascontiguousarray(np.asarray(L0_attn_W_hg, dtype=np.float32))
    w_out = np.ascontiguousarray(np.asarray(L0_attn_W_out, dtype=np.float32))
    w1 = np.ascontiguousarray(np.asarray(L0_ffn_w1_w, dtype=np.float32))
    w2 = np.ascontiguousarray(np.asarray(L0_ffn_w2_w, dtype=np.float32))
    h1 = np.ascontiguousarray(np.asarray(L0_ffn_h1_w, dtype=np.float32))
    h2 = np.ascontiguousarray(np.asarray(L0_ffn_h2_w, dtype=np.float32))
    w3 = np.ascontiguousarray(np.asarray(L0_ffn_w3_w, dtype=np.float32))
    s1 = _layout_scale(L0_ffn_w1_s, HT)
    s2 = _layout_scale(L0_ffn_w2_s, HT)
    h1s = _layout_scale(L0_ffn_h1_s, HT)
    h2s = _layout_scale(L0_ffn_h2_s, HT)
    w3s = _layout_scale(L0_ffn_w3_s, DT)
    cIm = np.eye(128, dtype=np.float32)
    cJm = np.ascontiguousarray(cIm[::-1, :])

    import gc
    nc = _build(cfg)
    gc.collect()
    in_maps = []
    for c in range(N_CORES):
        in_maps.append(dict(
            x=np.ascontiguousarray(x[c * B:(c + 1) * B]),
            w_hg=w_hg, w_out=w_out, w1=w1, w2=w2, h1=h1, h2=h2, w3=w3,
            s1=s1, s2=s2, h1s=h1s, h2s=h2s, w3s=w3s, cI=cIm, cJ=cJm,
        ))
    res = run_bass_kernel_spmd(nc, in_maps, core_ids=list(range(N_CORES)))
    return np.concatenate([r["y"] for r in res.results], axis=0)


# version 05
# speedup vs baseline: 1.1517x; 1.1517x over previous
import numpy as np
import concourse.bass as bass
import concourse.mybir as mybir
from concourse.bass_utils import run_bass_kernel_spmd

F32 = mybir.dt.float32
BF16 = mybir.dt.bfloat16
AF = mybir.ActivationFunctionType
ALU = mybir.AluOpType
AX = mybir.AxisListType

CFG = dict(B=2, T=4096, D=1024, E=1536, H=4096, C=512)
N_CORES = 8


class _S:
    def __init__(self, h):
        self.h = h
        self.n = 0


def _build(cfg):
    B, T, D, E, H, C = cfg["B"], cfg["T"], cfg["D"], cfg["E"], cfg["H"], cfg["C"]
    DT, ET, HT = D // 128, E // 128, H // 128
    PH = T // C
    TI = C // 128
    NPH = B * C
    assert C <= 512 and C % 128 == 0

    nc = bass.Bass("TRN2", target_bir_lowering=False, debug=False,
                   detect_race_conditions=False)

    xd = nc.dram_tensor("x", [B, T, D], F32, kind="ExternalInput").ap()
    yd = nc.dram_tensor("y", [B, T, D], F32, kind="ExternalOutput").ap()
    xv = xd.rearrange("b t d -> (b t) d")
    yv = yd.rearrange("b t d -> (b t) d")

    w_hg = nc.dram_tensor("w_hg", [2 * E, D], F32, kind="ExternalInput").ap()
    w_out = nc.dram_tensor("w_out", [D, E], F32, kind="ExternalInput").ap()
    w1 = nc.dram_tensor("w1", [H, D], F32, kind="ExternalInput").ap()
    w2 = nc.dram_tensor("w2", [H, D], F32, kind="ExternalInput").ap()
    h1 = nc.dram_tensor("h1", [H, H], F32, kind="ExternalInput").ap()
    h2 = nc.dram_tensor("h2", [H, H], F32, kind="ExternalInput").ap()
    w3 = nc.dram_tensor("w3", [D, H], F32, kind="ExternalInput").ap()
    s1d = nc.dram_tensor("s1", [128, HT], F32, kind="ExternalInput").ap()
    s2d = nc.dram_tensor("s2", [128, HT], F32, kind="ExternalInput").ap()
    h1sd = nc.dram_tensor("h1s", [128, HT], F32, kind="ExternalInput").ap()
    h2sd = nc.dram_tensor("h2s", [128, HT], F32, kind="ExternalInput").ap()
    w3sd = nc.dram_tensor("w3s", [128, DT], F32, kind="ExternalInput").ap()
    cId = nc.dram_tensor("cI", [128, 128], F32, kind="ExternalInput").ap()
    cJd = nc.dram_tensor("cJ", [128, 128], F32, kind="ExternalInput").ap()

    dbg = cfg.get("dbg", False)
    if dbg:
        d_xnfm = nc.dram_tensor("d_xnfm", [128, DT * NPH], BF16, kind="ExternalOutput").ap()
        d_xnrev = nc.dram_tensor("d_xnrev", [128, DT * NPH], BF16, kind="ExternalOutput").ap()
        d_states = nc.dram_tensor("d_states", [128, ET * NPH], BF16, kind="ExternalOutput").ap()
        d_ha = nc.dram_tensor("d_ha", [128, HT * C], BF16, kind="ExternalOutput").ap()
        d_acc = nc.dram_tensor("d_acc", [128, DT * C], F32, kind="ExternalOutput").ap()

    def scratch(name, MT, KT):
        t = nc.dram_tensor(name, [MT, 128, KT * 128], BF16, kind="Internal").ap()
        return (t, MT, KT)
    whgp_s = scratch("whgp_s", ET, 2 * DT)
    w12_s = scratch("w12_s", HT, 2 * DT)
    wout_s = scratch("wout_s", DT, ET)
    h1_s = scratch("h1_s", HT, HT)
    h2_s = scratch("h2_s", HT, HT)
    w3_s = scratch("w3_s", DT, HT)

    from contextlib import ExitStack
    stk = ExitStack()
    sb = lambda name, shape, dt=F32: stk.enter_context(nc.sbuf_tensor(name, shape, dt))

    cI = sb("cIt", [128, 128]); cJ = sb("cJt", [128, 128])
    s1t = sb("s1t", [128, HT]); s2t = sb("s2t", [128, HT])
    h1st = sb("h1st", [128, HT]); h2st = sb("h2st", [128, HT])
    w3st = sb("w3st", [128, DT])
    epsT = sb("epsT", [128, 1])
    x_tm = [sb(f"x_tm{i}", [128, D]) for i in range(3)]
    diagb = [sb(f"diagb{i}", [128, 128]) for i in range(2)]
    xn_tm = sb("xn_tm", [128, D])
    stat = sb("stat", [128, 8])
    xn_fm = sb("xn_fm", [128, DT * NPH], BF16)
    xn_rev = sb("xn_rev", [128, DT * NPH], BF16)
    states = sb("states", [128, ET * NPH], BF16)
    carry = sb("carry", [128, 4 * ET])
    zb = sb("zb", [128, C]); cbuf = sb("cbuf", [128, C]); sgb = sb("sgb", [128, C])
    tb = sb("tb", [128, C]); gbuf = sb("gbuf", [128, C])
    vbuf = sgb
    fbA = sb("fbA", [128, C]); fbB = sb("fbB", [128, C])
    Bh1 = sb("Bh1", [128, HT * C], BF16)
    Bh2 = sb("Bh2", [128, HT * C], BF16)
    acc = sb("acc", [128, DT * C])
    obuf = [sb(f"obuf{i}", [128, D]) for i in range(2)]
    slab_big = [sb(f"slabB{i}", [128, HT * 128], BF16) for i in range(2)]
    slab_sml = [sb(f"slabS{i}", [128, max(2 * DT, ET) * 128], BF16) for i in range(3)]
    tsil = sb("tsil", [128, C])

    ps = [stk.enter_context(nc.psum_tensor(f"ps{i}", [128, 512], F32)) for i in range(8)]

    mksem = lambda name: _S(stk.enter_context(nc.semaphore(name)))
    SXC = mksem("SXC")
    SXT = [mksem(f"SXT{i}") for i in range(4)]
    SN = [mksem(f"SN{i}") for i in range(4)]
    SC = [mksem(f"SC{i}") for i in range(2)]
    SBG = [mksem(f"SBG{i}") for i in range(3)]
    SSL = [mksem(f"SSL{i}") for i in range(3)]
    SO = [mksem(f"SO{i}") for i in range(2)]
    SPE = mksem("SPE")
    SACT = mksem("SACT")
    SDVE = mksem("SDVE")
    SDBG = mksem("SDBG")

    Q = dict(sync=[], scalar=[], vector=[], tensor=[], gpsimd=[])
    _last_wait = {}

    def w(eng, sem, val):
        if val <= 0:
            return
        key = (eng, id(sem))
        if _last_wait.get(key, 0) >= val:
            return
        _last_wait[key] = val
        Q[eng].append(lambda e, s=sem, v=val: e.wait_ge(s.h, v))

    def op(eng, f, inc=None, n=1):
        if inc is not None:
            inc.n += n
            Q[eng].append(lambda e, f=f, s=inc, n=n: f(e).then_inc(s.h, n))
            return inc.n
        Q[eng].append(lambda e, f=f: f(e))
        return None

    bank_free = {i: (SPE, 0) for i in range(8)}

    def wait_banks(eng, banks):
        for b in banks:
            sm, v = bank_free[b]
            w(eng, sm, v)

    for dst, src in ((cI, cId), (cJ, cJd), (s1t, s1d), (s2t, s2d),
                     (h1st, h1sd), (h2st, h2sd), (w3st, w3sd)):
        op("sync", lambda e, d=dst, s=src: e.dma_start(out=d[:, :], in_=s), SXC, 16)
    setup_loads = SXC.n
    op("vector", lambda e: e.memset(carry[:, :], 0.0), SDVE)
    op("vector", lambda e: e.memset(epsT[:, :], 1e-5), SDVE)

    maxK = max(D, E, H)
    if HT * C // 2 >= 2 * maxK and 2 * DT * C >= 2 * maxK:
        natw = [Bh1[:].bitcast(F32)[:, 0:maxK], Bh1[:].bitcast(F32)[:, maxK:2 * maxK],
                Bh2[:].bitcast(F32)[:, 0:maxK], Bh2[:].bitcast(F32)[:, maxK:2 * maxK]]
        staging = [acc[:].bitcast(BF16)[:, 0:maxK], acc[:].bitcast(BF16)[:, maxK:2 * maxK]]
    else:
        natw_t = sb("natw_t", [128, 2 * maxK])
        natw_t2 = sb("natw_t2", [128, 2 * maxK])
        staging_t = sb("staging_t", [128, 2 * maxK], BF16)
        natw = [natw_t[:, 0:maxK], natw_t[:, maxK:2 * maxK],
                natw_t2[:, 0:maxK], natw_t2[:, maxK:2 * maxK]]
        staging = [staging_t[:, 0:maxK], staging_t[:, maxK:2 * maxK]]
    rp = dict(pe_done=[0, 0, 0, 0], sc_done=[0, 0], diag_free=[0, 0], i=0)

    w("scalar", SXC, setup_loads)
    w("tensor", SXC, setup_loads)
    w("vector", SXC, setup_loads)

    def repack(wdram, scr3, KTm, scale_t, row0=0, col_off=0):
        scr, MT, KTtot = scr3
        K = KTm * 128
        for mt in range(MT):
            it = rp["i"]
            rp["i"] += 1
            slot = it % 4
            stg = it % 2
            w("sync", SPE, rp["pe_done"][slot])
            w("sync", SC[stg], rp["sc_done"][stg])
            op("sync", lambda e, sl=slot, mt=mt, K=K, wd=wdram, row0=row0:
               e.dma_start(out=natw[sl][:, 0:K],
                           in_=wd[row0 + mt * 128:row0 + (mt + 1) * 128, 0:K]),
               SN[slot], 16)
            ldn = SN[slot].n
            if scale_t is not None:
                dq = it % 2
                w("vector", SPE, rp["diag_free"][dq])
                dgn = op("vector", lambda e, dq=dq, st_=scale_t, mt=mt:
                         e.tensor_scalar(out=diagb[dq][:, :], in0=cI[:, :],
                                         scalar1=st_[:, mt:mt + 1], scalar2=None,
                                         op0=ALU.mult), SDVE)
                w("tensor", SDVE, dgn)
                idop = diagb[dq]
            else:
                idop = cI
            w("tensor", SN[slot], ldn)
            src = natw[slot]
            evs = []
            for kt in range(KTm):
                bq = (kt // 4) % 2
                if kt % 4 == 0:
                    wait_banks("tensor", [bq])
                if scale_t is not None:
                    last = op("tensor", lambda e, kt=kt, bq=bq, src=src, idop=idop:
                              e.matmul(ps[bq][:, (kt % 4) * 128:(kt % 4) * 128 + 128],
                                       src[:, kt * 128:(kt + 1) * 128], idop[:, :],
                                       start=True, stop=True),
                              SPE)
                else:
                    last = op("tensor", lambda e, kt=kt, bq=bq, src=src, idop=idop:
                              e.transpose(ps[bq][:, (kt % 4) * 128:(kt % 4) * 128 + 128],
                                          src[:, kt * 128:(kt + 1) * 128], idop[:, :]),
                              SPE)
                if kt % 4 == 3 or kt == KTm - 1:
                    nt = (kt % 4) + 1
                    w("scalar", SPE, last)
                    if kt < 4:
                        w("scalar", SC[stg], rp["sc_done"][stg])
                    evn = op("scalar", lambda e, sl=stg, kt=kt, bq=bq, nt=nt:
                             e.copy(out=staging[sl][:, (kt - nt + 1) * 128:(kt + 1) * 128],
                                    in_=ps[bq][:, 0:nt * 128]), SACT)
                    bank_free[bq] = (SACT, evn)
                    evs.append(evn)
            rp["pe_done"][slot] = SPE.n
            if scale_t is not None:
                rp["diag_free"][it % 2] = SPE.n
            w("sync", SACT, evs[-1])
            op("sync", lambda e, sl=stg, mt=mt, K=K, scr=scr, co=col_off * 128:
               e.dma_start(out=scr[mt:mt + 1, :, co:co + K].rearrange("m p j -> (m p) j"),
                           in_=staging[sl][:, 0:K]),
               SC[stg], 16)
            rp["sc_done"][stg] = SC[stg].n

    repack(w_hg, whgp_s, DT, None, row0=0, col_off=0)
    repack(w_hg, whgp_s, DT, None, row0=E, col_off=DT)
    repack(w1, w12_s, DT, s1t, col_off=0)
    repack(w2, w12_s, DT, s2t, col_off=DT)
    repack(w_out, wout_s, ET, None)
    repack(h1, h1_s, HT, h1st)
    repack(h2, h2_s, HT, h2st)
    repack(w3, w3_s, HT, w3st)
    w("sync", SC[0], SC[0].n)
    w("sync", SC[1], SC[1].n)

    slab_free = {"big": [(SPE, 0)] * 3, "sml": [(SPE, 0)] * 3}
    slab_idx = {"big": 0, "sml": 0}

    def load_slab(scr3, mt, big):
        scr, MT, KT = scr3
        kind = "big" if big else "sml"
        pool = slab_big if big else slab_sml
        sgrp = SBG if big else SSL
        sl = slab_idx[kind] % len(pool)
        slab_idx[kind] += 1
        sm, v = slab_free[kind][sl]
        w("sync", sm, v)
        op("sync", lambda e, sl=sl, mt=mt, KT=KT, scr=scr, pool=pool:
           e.dma_start(out=pool[sl][:, 0:KT * 128],
                       in_=scr[mt:mt + 1, :, :].rearrange("m p j -> (m p) j")),
           sgrp[sl], 16)
        return pool[sl], (kind, sl), sgrp[sl], sgrp[sl].n

    x_slot_free = [(SDVE, 0)] * 4
    o_slot_free = [0, 0]
    xnfm_reader = 0
    xnrev_reader = 0
    states_reader = 0
    Bh1_pe_reader = 0
    Bh2_pe_reader = 0
    acc_pe_reader = 0
    tsil_reader = [0, 0]
    prev_scanbufs_free = 0
    prev_stat_done = 0
    prev_xn_tr = 0
    n_groups = 0
    x_rr = dict(i=0)

    def load_x(seq, tok0):
        sl = x_rr["i"] % len(x_tm)
        x_rr["i"] += 1
        sm, v = x_slot_free[sl]
        w("sync", sm, v)
        op("sync", lambda e, sl=sl, seq=seq, tok0=tok0:
           e.dma_start(out=x_tm[sl][:, :],
                       in_=xv[seq * T + tok0:seq * T + tok0 + 128, :]),
           SXT[sl], 16)
        return sl, SXT[sl].n

    for phn in range(PH):
        ch_f, ch_b = phn, PH - 1 - phn

        w("scalar", SPE, max(xnfm_reader, xnrev_reader))
        chunk_ready = {}
        tt = 0
        for kind, chunk in (("f", ch_f), ("b", ch_b)):
            for seq in range(B):
                for ti in range(TI):
                    tok0 = chunk * C + ti * 128
                    sl, ldn = load_x(seq, tok0)
                    w("vector", SXT[sl], ldn)
                    op("vector", lambda e, sl=sl:
                       e.reduce_sum(stat[:, 0:1], x_tm[sl][:, :], axis=AX.X), SDVE)
                    osl = tt % 2
                    w("scalar", SXT[sl], ldn)
                    w("scalar", SO[osl], o_slot_free[osl])
                    w("scalar", SDVE, prev_stat_done)
                    sqn = op("scalar", lambda e, sl=sl, osl=osl:
                             e.activation(out=obuf[osl][:, :], in_=x_tm[sl][:, :],
                                          func=AF.Square), SACT)
                    w("vector", SACT, sqn)
                    s1n = op("vector", lambda e, osl=osl:
                             e.reduce_sum(stat[:, 1:2], obuf[osl][:, :], axis=AX.X), SDVE)
                    w("scalar", SDVE, s1n)
                    op("scalar", lambda e: e.activation(out=stat[:, 2:3], in_=stat[:, 0:1],
                                                        func=AF.Square, scale=1.0 / D), SACT)
                    op("scalar", lambda e: e.activation(out=stat[:, 3:4], in_=stat[:, 0:1],
                                                        func=AF.Copy, scale=1.0 / D), SACT)
                    msn = op("scalar", lambda e: e.activation(out=stat[:, 4:5], in_=stat[:, 1:2],
                                                              func=AF.Copy, scale=1.0 / D), SACT)
                    w("vector", SACT, msn)
                    vn = op("vector", lambda e: e.tensor_tensor(out=stat[:, 5:6], in0=stat[:, 4:5],
                                                                in1=stat[:, 2:3], op=ALU.subtract), SDVE)
                    w("scalar", SDVE, vn)
                    sqn2 = op("scalar", lambda e: e.activation(out=stat[:, 6:7], in_=stat[:, 5:6],
                                                               func=AF.Sqrt, bias=epsT[:, 0:1]), SACT)
                    w("vector", SACT, sqn2)
                    rcn = op("vector", lambda e: e.reciprocal(out=stat[:, 7:8], in_=stat[:, 6:7]), SDVE)
                    w("scalar", SDVE, rcn)
                    rbn = op("scalar", lambda e: e.activation(out=stat[:, 5:6], in_=stat[:, 7:8],
                                                              func=AF.Copy), SACT)
                    w("vector", SACT, rbn)
                    w("vector", SPE, prev_xn_tr)
                    xn_done = op("vector", lambda e, sl=sl: e.tensor_scalar(
                        out=xn_tm[:, :], in0=x_tm[sl][:, :], scalar1=stat[:, 3:4],
                        scalar2=stat[:, 5:6], op0=ALU.subtract, op1=ALU.mult), SDVE)
                    prev_stat_done = xn_done
                    x_slot_free[sl] = (SDVE, xn_done)
                    bq = tt % 2
                    b0, b1 = (4, 5) if bq == 0 else (6, 7)
                    w("tensor", SDVE, xn_done)
                    wait_banks("tensor", [b0, b1])
                    idmat = cI if kind == "f" else cJ
                    half = DT // 2
                    for dt in range(DT):
                        bank = b0 if dt < half else b1
                        colp = (dt % half) * 128
                        op("tensor", lambda e, bank=bank, colp=colp, dt=dt, idmat=idmat:
                           e.transpose(ps[bank][:, colp:colp + 128],
                                       xn_tm[:, dt * 128:(dt + 1) * 128], idmat[:, :]), SPE)
                    tr_done = SPE.n
                    prev_xn_tr = tr_done
                    dst_t = xn_fm if kind == "f" else xn_rev
                    cti = ti if kind == "f" else (TI - 1 - ti)
                    col = seq * C + cti * 128
                    w("scalar", SPE, tr_done)
                    ev1 = op("scalar", lambda e, b0=b0, dst_t=dst_t, col=col, half=half:
                             e.copy(out=dst_t[:, :].rearrange("p (dt n) -> p dt n", n=NPH)[:, 0:half, col:col + 128],
                                    in_=ps[b0][:, 0:half * 128].rearrange("p (dt j) -> p dt j", j=128)),
                             SACT)
                    ev2 = op("scalar", lambda e, b1=b1, dst_t=dst_t, col=col, half=half:
                             e.copy(out=dst_t[:, :].rearrange("p (dt n) -> p dt n", n=NPH)[:, half:DT, col:col + 128],
                                    in_=ps[b1][:, 0:(DT - half) * 128].rearrange("p (dt j) -> p dt j", j=128)),
                             SACT)
                    bank_free[b0] = (SACT, ev1)
                    bank_free[b1] = (SACT, ev2)
                    tt += 1
                chunk_ready[(kind, seq)] = SACT.n
        xn_ready = SACT.n
        if dbg and phn == 0:
            w("gpsimd", SACT, xn_ready)
            op("gpsimd", lambda e: e.dma_start(out=d_xnfm, in_=xn_fm[:, :]), SDBG, 16)
            op("gpsimd", lambda e: e.dma_start(out=d_xnrev, in_=xn_rev[:, :]), SDBG, 16)

        w("vector", SPE, states_reader)
        for pr in range(ET):
            pair_slab, pair_slot, pair_sm, pair_ld = load_slab(whgp_s, pr, big=False)
            for dir_, seq in ((0, 0), (1, 0), (0, 1), (1, 1)):
                qq = n_groups % 2
                p_hid, p_gat = ps[2 * qq], ps[2 * qq + 1]
                src = xn_fm if dir_ == 0 else xn_rev
                w("tensor", pair_sm, pair_ld)
                w("tensor", SACT, chunk_ready[("f" if dir_ == 0 else "b", seq)])
                wait_banks("tensor", [2 * qq, 2 * qq + 1])
                hg_done = None
                for which, pps in ((0, p_hid), (1, p_gat)):
                    for kt in range(DT):
                        is_last = (which == 1 and kt == DT - 1)
                        ko = which * DT + kt
                        hg_done = op("tensor", lambda e, pps=pps, slabb=pair_slab, ko=ko, kt=kt, src=src, seq=seq:
                                     e.matmul(pps[:, 0:C],
                                              slabb[:, ko * 128:(ko + 1) * 128],
                                              src[:, kt * NPH + seq * C:kt * NPH + seq * C + C],
                                              start=(kt == 0), stop=(kt == DT - 1)),
                                     SPE if is_last else None)
                if dir_ == 0:
                    xnfm_reader = max(xnfm_reader, SPE.n)
                else:
                    xnrev_reader = max(xnrev_reader, SPE.n)
                if (dir_, seq) == (1, 1):
                    slab_free[pair_slot[0]][pair_slot[1]] = (SPE, SPE.n)
                w("scalar", SPE, hg_done)
                w("scalar", SDVE, prev_scanbufs_free)
                op("scalar", lambda e, p_gat=p_gat: e.activation(out=zb[:, 0:C], in_=p_gat[:, 0:C],
                                                                 func=AF.Sigmoid), SACT)
                op("scalar", lambda e, p_gat=p_gat: e.activation(out=cbuf[:, 0:C], in_=p_gat[:, 0:C],
                                                                 func=AF.Sigmoid, scale=-1.0), SACT)
                prep = op("scalar", lambda e, p_hid=p_hid: e.activation(out=sgb[:, 0:C], in_=p_hid[:, 0:C],
                                                                        func=AF.Sigmoid), SACT)
                w("vector", SACT, prep)
                tbn = op("vector", lambda e, p_hid=p_hid: e.scalar_tensor_tensor(
                    out=tb[:, 0:C], in0=p_hid[:, 0:C], scalar=0.5, in1=sgb[:, 0:C],
                    op0=ALU.add, op1=ALU.subtract), SDVE)
                bank_free[2 * qq] = (SDVE, tbn)
                bank_free[2 * qq + 1] = (SDVE, tbn)
                op("vector", lambda e: e.scalar_tensor_tensor(
                    out=gbuf[:, 0:C], in0=tb[:, 0:C], scalar=0.0, in1=sgb[:, 0:C],
                    op0=ALU.max, op1=ALU.add))
                op("vector", lambda e: e.tensor_tensor(out=vbuf[:, 0:C], in0=zb[:, 0:C],
                                                       in1=gbuf[:, 0:C], op=ALU.mult))
                fb = fbA if dir_ == 0 else fbB
                cc = (dir_ * B + seq) * ET + pr
                scn = op("vector", lambda e, fb=fb, cc=cc: e.tensor_tensor_scan(
                    out=fb[:, 0:C], data0=cbuf[:, 0:C], data1=vbuf[:, 0:C],
                    initial=carry[:, cc:cc + 1], op0=ALU.mult, op1=ALU.add), SDVE)
                prev_scanbufs_free = scn
                if dir_ == 1:
                    op("vector", lambda e, pr=pr, seq=seq: e.tensor_tensor(
                        out=states[:, pr * NPH + seq * C:pr * NPH + seq * C + C],
                        in0=fbA[:, 0:C], in1=fbB[:, 0:C], op=ALU.add), SDVE)
                    ccf = (0 * B + seq) * ET + pr
                    op("vector", lambda e, ccf=ccf:
                       e.tensor_copy(out=carry[:, ccf:ccf + 1], in_=fbA[:, C - 1:C]))
                    op("vector", lambda e, cc=cc:
                       e.tensor_copy(out=carry[:, cc:cc + 1], in_=fbB[:, C - 1:C]))
                n_groups += 1
        states_ready = SDVE.n
        if dbg and phn == 0:
            w("gpsimd", SDVE, states_ready)
            op("gpsimd", lambda e: e.dma_start(out=d_states, in_=states[:, :]), SDBG, 16)

        for seq in range(B):
            for mt in range(HT):
                s12b, s12slot, s12sm, ld12 = load_slab(w12_s, mt, big=False)
                qq = mt % 2
                pw1, pw2 = ps[2 * qq], ps[2 * qq + 1]
                w("tensor", s12sm, ld12)
                wait_banks("tensor", [2 * qq, 2 * qq + 1])
                w1_done = w2_done = None
                for which, pps in ((0, pw1), (1, pw2)):
                    for kt in range(DT):
                        ko = which * DT + kt
                        mk = op("tensor", lambda e, pps=pps, slabb=s12b, ko=ko, kt=kt, seq=seq:
                                e.matmul(pps[:, 0:C], slabb[:, ko * 128:(ko + 1) * 128],
                                         xn_fm[:, kt * NPH + seq * C:kt * NPH + seq * C + C],
                                         start=(kt == 0), stop=(kt == DT - 1)),
                                SPE if kt == DT - 1 else None)
                        if kt == DT - 1:
                            if which == 0:
                                w1_done = mk
                            else:
                                w2_done = mk
                xnfm_reader = max(xnfm_reader, SPE.n)
                slab_free[s12slot[0]][s12slot[1]] = (SPE, SPE.n)
                w("scalar", SPE, w1_done)
                w("scalar", SDVE, tsil_reader[0])
                sil = op("scalar", lambda e, pw1=pw1:
                         e.activation(out=tsil[:, 0:C], in_=pw1[:, 0:C],
                                      func=AF.Silu), SACT)
                bank_free[2 * qq] = (SACT, sil)
                w("vector", SACT, sil)
                w("vector", SPE, max(w2_done, Bh1_pe_reader))
                ha = op("vector", lambda e, pw2=pw2, mt=mt: e.tensor_tensor(
                    out=Bh1[:, mt * C:(mt + 1) * C], in0=tsil[:, 0:C],
                    in1=pw2[:, 0:C], op=ALU.mult), SDVE)
                tsil_reader[0] = ha
                bank_free[2 * qq + 1] = (SDVE, ha)
            ha_ready = SDVE.n
            if dbg and phn == 0 and seq == 0:
                w("gpsimd", SDVE, ha_ready)
                op("gpsimd", lambda e: e.dma_start(out=d_ha, in_=Bh1[:, :]), SDBG, 16)

            for mt in range(HT):
                hb, hslot, hsm, ldh = load_slab(h1_s, mt, big=True)
                qq = mt % 2
                pp = ps[qq]
                w("tensor", hsm, ldh)
                w("tensor", SDVE, ha_ready)
                wait_banks("tensor", [qq])
                mm_done = None
                for kt in range(HT):
                    mm_done = op("tensor", lambda e, pp=pp, hb=hb, kt=kt:
                                 e.matmul(pp[:, 0:C], hb[:, kt * 128:(kt + 1) * 128],
                                          Bh1[:, kt * C:(kt + 1) * C],
                                          start=(kt == 0), stop=(kt == HT - 1)),
                                 SPE if kt == HT - 1 else None)
                Bh1_pe_reader = max(Bh1_pe_reader, SPE.n)
                slab_free[hslot[0]][hslot[1]] = (SPE, SPE.n)
                w("scalar", SPE, max(mm_done, Bh2_pe_reader))
                evn = op("scalar", lambda e, pp=pp, mt=mt:
                         e.activation(out=Bh2[:, mt * C:(mt + 1) * C], in_=pp[:, 0:C],
                                      func=AF.Gelu), SACT)
                bank_free[qq] = (SACT, evn)
            hb_ready = SACT.n

            h1_reader_spe = Bh1_pe_reader
            for mt in range(HT):
                hb, hslot, hsm, ldh = load_slab(h2_s, mt, big=True)
                qq = mt % 2
                pp = ps[2 + qq]
                w("tensor", hsm, ldh)
                w("tensor", SACT, hb_ready)
                wait_banks("tensor", [2 + qq])
                mm_done = None
                for kt in range(HT):
                    mm_done = op("tensor", lambda e, pp=pp, hb=hb, kt=kt:
                                 e.matmul(pp[:, 0:C], hb[:, kt * 128:(kt + 1) * 128],
                                          Bh2[:, kt * C:(kt + 1) * C],
                                          start=(kt == 0), stop=(kt == HT - 1)),
                                 SPE if kt == HT - 1 else None)
                Bh2_pe_reader = max(Bh2_pe_reader, SPE.n)
                slab_free[hslot[0]][hslot[1]] = (SPE, SPE.n)
                w("scalar", SPE, max(mm_done, h1_reader_spe))
                evn = op("scalar", lambda e, pp=pp, mt=mt:
                         e.activation(out=Bh1[:, mt * C:(mt + 1) * C], in_=pp[:, 0:C],
                                      func=AF.Gelu), SACT)
                bank_free[2 + qq] = (SACT, evn)
            hc_ready = SACT.n

            for mt in range(DT):
                wob, woslot, wosm, ldo = load_slab(wout_s, mt, big=False)
                w3b, w3slot, w3sm, ld3 = load_slab(w3_s, mt, big=True)
                qq = mt % 2
                pp = ps[qq]
                w("tensor", wosm, ldo)
                w("tensor", w3sm, ld3)
                w("tensor", SDVE, states_ready)
                w("tensor", SACT, hc_ready)
                wait_banks("tensor", [qq])
                mm_done = None
                for kt in range(ET):
                    op("tensor", lambda e, pp=pp, wob=wob, kt=kt, seq=seq:
                       e.matmul(pp[:, 0:C], wob[:, kt * 128:(kt + 1) * 128],
                                states[:, kt * NPH + seq * C:kt * NPH + seq * C + C],
                                start=(kt == 0), stop=False))
                for kt in range(HT):
                    mm_done = op("tensor", lambda e, pp=pp, w3b=w3b, kt=kt:
                                 e.matmul(pp[:, 0:C], w3b[:, kt * 128:(kt + 1) * 128],
                                          Bh1[:, kt * C:(kt + 1) * C],
                                          start=False, stop=(kt == HT - 1)),
                                 SPE if kt == HT - 1 else None)
                states_reader = max(states_reader, SPE.n)
                Bh1_pe_reader = max(Bh1_pe_reader, SPE.n)
                slab_free[woslot[0]][woslot[1]] = (SPE, SPE.n)
                slab_free[w3slot[0]][w3slot[1]] = (SPE, SPE.n)
                w("scalar", SPE, max(mm_done, acc_pe_reader))
                evn = op("scalar", lambda e, pp=pp, mt=mt:
                         e.copy(out=acc[:, mt * C:(mt + 1) * C], in_=pp[:, 0:C]), SACT)
                bank_free[qq] = (SACT, evn)
            acc_ready = SACT.n
            if dbg and phn == 0 and seq == 0:
                w("gpsimd", SACT, acc_ready)
                op("gpsimd", lambda e: e.dma_start(out=d_acc, in_=acc[:, :]), SDBG, 16)

            for ti in range(TI):
                tok0 = ch_f * C + ti * 128
                bq = ti % 2
                b0, b1 = (4, 5) if bq == 0 else (6, 7)
                half = DT // 2
                w("tensor", SACT, acc_ready)
                wait_banks("tensor", [b0, b1])
                tr_done = None
                for dt in range(DT):
                    bank = b0 if dt < half else b1
                    colp = (dt % half) * 128
                    tr_done = op("tensor", lambda e, bank=bank, colp=colp, dt=dt, ti=ti:
                                 e.transpose(ps[bank][:, colp:colp + 128],
                                             acc[:, dt * C + ti * 128:dt * C + ti * 128 + 128],
                                             cI[:, :]),
                                 SPE if dt == DT - 1 else None)
                acc_pe_reader = max(acc_pe_reader, SPE.n)
                xsl, xldn = load_x(seq, tok0)
                osl = ti % 2
                w("vector", SPE, tr_done)
                w("vector", SXT[xsl], xldn)
                w("vector", SO[osl], o_slot_free[osl])
                op("vector", lambda e, osl=osl, xsl=xsl, b0=b0, half=half: e.tensor_tensor(
                    out=obuf[osl][:, 0:half * 128], in0=x_tm[xsl][:, 0:half * 128],
                    in1=ps[b0][:, 0:half * 128], op=ALU.add))
                addn = op("vector", lambda e, osl=osl, xsl=xsl, b1=b1, half=half: e.tensor_tensor(
                    out=obuf[osl][:, half * 128:D], in0=x_tm[xsl][:, half * 128:D],
                    in1=ps[b1][:, 0:(DT - half) * 128], op=ALU.add), SDVE)
                bank_free[b0] = (SDVE, addn)
                bank_free[b1] = (SDVE, addn)
                x_slot_free[xsl] = (SDVE, addn)
                w("gpsimd", SDVE, addn)
                op("gpsimd", lambda e, osl=osl, seq=seq, tok0=tok0:
                   e.dma_start(out=yv[seq * T + tok0:seq * T + tok0 + 128, :],
                               in_=obuf[osl][:, :]),
                   SO[osl], 16)
                o_slot_free[osl] = SO[osl].n

    w("gpsimd", SO[0], SO[0].n)
    w("gpsimd", SO[1], SO[1].n)
    if dbg:
        w("gpsimd", SDBG, SDBG.n)

    with nc.Block() as block:
        @block.sync
        def _(e):
            for f in Q["sync"]:
                f(e)

        @block.scalar
        def _(e):
            for f in Q["scalar"]:
                f(e)

        @block.vector
        def _(e):
            for f in Q["vector"]:
                f(e)

        @block.tensor
        def _(e):
            for f in Q["tensor"]:
                f(e)

        @block.gpsimd
        def _(e):
            for f in Q["gpsimd"]:
                f(e)

    stk.close()
    return nc


def _layout_scale(s, MT):
    s = np.asarray(s, dtype=np.float32)
    return np.ascontiguousarray(s.reshape(MT, 128).T)


def kernel(L0_attn_x, L0_attn_W_hg, L0_attn_W_out, L0_ffn_w1_w, L0_ffn_w1_s,
           L0_ffn_w2_w, L0_ffn_w2_s, L0_ffn_h1_w, L0_ffn_h1_s, L0_ffn_h2_w,
           L0_ffn_h2_s, L0_ffn_w3_w, L0_ffn_w3_s):
    cfg = CFG
    B, T, D, E, H = cfg["B"], cfg["T"], cfg["D"], cfg["E"], cfg["H"]
    DT, HT = D // 128, H // 128

    x = np.ascontiguousarray(np.asarray(L0_attn_x, dtype=np.float32))
    w_hg = np.ascontiguousarray(np.asarray(L0_attn_W_hg, dtype=np.float32))
    w_out = np.ascontiguousarray(np.asarray(L0_attn_W_out, dtype=np.float32))
    w1 = np.ascontiguousarray(np.asarray(L0_ffn_w1_w, dtype=np.float32))
    w2 = np.ascontiguousarray(np.asarray(L0_ffn_w2_w, dtype=np.float32))
    h1 = np.ascontiguousarray(np.asarray(L0_ffn_h1_w, dtype=np.float32))
    h2 = np.ascontiguousarray(np.asarray(L0_ffn_h2_w, dtype=np.float32))
    w3 = np.ascontiguousarray(np.asarray(L0_ffn_w3_w, dtype=np.float32))
    s1 = _layout_scale(L0_ffn_w1_s, HT)
    s2 = _layout_scale(L0_ffn_w2_s, HT)
    h1s = _layout_scale(L0_ffn_h1_s, HT)
    h2s = _layout_scale(L0_ffn_h2_s, HT)
    w3s = _layout_scale(L0_ffn_w3_s, DT)
    cIm = np.eye(128, dtype=np.float32)
    cJm = np.ascontiguousarray(cIm[::-1, :])

    import gc
    nc = _build(cfg)
    gc.collect()
    in_maps = []
    for c in range(N_CORES):
        in_maps.append(dict(
            x=np.ascontiguousarray(x[c * B:(c + 1) * B]),
            w_hg=w_hg, w_out=w_out, w1=w1, w2=w2, h1=h1, h2=h2, w3=w3,
            s1=s1, s2=s2, h1s=h1s, h2s=h2s, w3s=w3s, cI=cIm, cJ=cJm,
        ))
    res = run_bass_kernel_spmd(nc, in_maps, core_ids=list(range(N_CORES)))
    return np.concatenate([r["y"] for r in res.results], axis=0)


# version 06
# speedup vs baseline: 1.2295x; 1.0675x over previous
import numpy as np
import concourse.bass as bass
import concourse.mybir as mybir
from concourse.bass_utils import run_bass_kernel_spmd

F32 = mybir.dt.float32
BF16 = mybir.dt.bfloat16
AF = mybir.ActivationFunctionType
ALU = mybir.AluOpType
AX = mybir.AxisListType

CFG = dict(B=2, T=4096, D=1024, E=1536, H=4096, C=512)
N_CORES = 8


class _S:
    def __init__(self, h):
        self.h = h
        self.n = 0


def _build(cfg):
    B, T, D, E, H, C = cfg["B"], cfg["T"], cfg["D"], cfg["E"], cfg["H"], cfg["C"]
    DT, ET, HT = D // 128, E // 128, H // 128
    PH = T // C
    TI = C // 128
    NPH = B * C
    assert C <= 512 and C % 128 == 0

    nc = bass.Bass("TRN2", target_bir_lowering=False, debug=False,
                   detect_race_conditions=False)

    xd = nc.dram_tensor("x", [B, T, D], F32, kind="ExternalInput").ap()
    yd = nc.dram_tensor("y", [B, T, D], F32, kind="ExternalOutput").ap()
    xv = xd.rearrange("b t d -> (b t) d")
    yv = yd.rearrange("b t d -> (b t) d")

    w_hg = nc.dram_tensor("w_hg", [2 * E, D], F32, kind="ExternalInput").ap()
    w_out = nc.dram_tensor("w_out", [D, E], F32, kind="ExternalInput").ap()
    w1 = nc.dram_tensor("w1", [H, D], F32, kind="ExternalInput").ap()
    w2 = nc.dram_tensor("w2", [H, D], F32, kind="ExternalInput").ap()
    h1 = nc.dram_tensor("h1", [H, H], F32, kind="ExternalInput").ap()
    h2 = nc.dram_tensor("h2", [H, H], F32, kind="ExternalInput").ap()
    w3 = nc.dram_tensor("w3", [D, H], F32, kind="ExternalInput").ap()
    s1d = nc.dram_tensor("s1", [128, HT], F32, kind="ExternalInput").ap()
    s2d = nc.dram_tensor("s2", [128, HT], F32, kind="ExternalInput").ap()
    h1sd = nc.dram_tensor("h1s", [128, HT], F32, kind="ExternalInput").ap()
    h2sd = nc.dram_tensor("h2s", [128, HT], F32, kind="ExternalInput").ap()
    w3sd = nc.dram_tensor("w3s", [128, DT], F32, kind="ExternalInput").ap()
    cId = nc.dram_tensor("cI", [128, 128], F32, kind="ExternalInput").ap()
    cJd = nc.dram_tensor("cJ", [128, 128], F32, kind="ExternalInput").ap()

    dbg = cfg.get("dbg", False)
    if dbg:
        d_xnfm = nc.dram_tensor("d_xnfm", [128, DT * NPH], BF16, kind="ExternalOutput").ap()
        d_xnrev = nc.dram_tensor("d_xnrev", [128, DT * NPH], BF16, kind="ExternalOutput").ap()
        d_states = nc.dram_tensor("d_states", [128, ET * NPH], BF16, kind="ExternalOutput").ap()
        d_ha = nc.dram_tensor("d_ha", [128, HT * C], BF16, kind="ExternalOutput").ap()
        d_acc = nc.dram_tensor("d_acc", [128, DT * C], F32, kind="ExternalOutput").ap()

    def scratch(name, MT, KT):
        t = nc.dram_tensor(name, [MT, 128, KT * 128], BF16, kind="Internal").ap()
        return (t, MT, KT)
    whgp_s = scratch("whgp_s", ET, 2 * DT)
    w12_s = scratch("w12_s", HT, 2 * DT)
    wout_s = scratch("wout_s", DT, ET)
    h1_s = scratch("h1_s", HT, HT)
    h2_s = scratch("h2_s", HT, HT)
    w3_s = scratch("w3_s", DT, HT)

    from contextlib import ExitStack
    stk = ExitStack()
    sb = lambda name, shape, dt=F32: stk.enter_context(nc.sbuf_tensor(name, shape, dt))

    cI = sb("cIt", [128, 128]); cJ = sb("cJt", [128, 128])
    s1t = sb("s1t", [128, HT]); s2t = sb("s2t", [128, HT])
    h1st = sb("h1st", [128, HT]); h2st = sb("h2st", [128, HT])
    w3st = sb("w3st", [128, DT])
    epsT = sb("epsT", [128, 1])
    x_tm = [sb(f"x_tm{i}", [128, D]) for i in range(3)]
    diagb = [sb(f"diagb{i}", [128, 128]) for i in range(2)]
    xn_tm = sb("xn_tm", [128, D])
    stat = sb("stat", [128, 8])
    xn_fm = sb("xn_fm", [128, DT * NPH], BF16)
    xn_rev = sb("xn_rev", [128, DT * NPH], BF16)
    states = sb("states", [128, ET * NPH], BF16)
    carry = sb("carry", [128, 4 * ET])
    zb = sb("zb", [128, C]); cbuf = sb("cbuf", [128, C]); sgb = sb("sgb", [128, C])
    tb = sb("tb", [128, C]); gbuf = sb("gbuf", [128, C])
    vbuf = sgb
    fbA = sb("fbA", [128, C]); fbB = sb("fbB", [128, C])
    Bh1 = sb("Bh1", [128, HT * C], BF16)
    Bh2 = sb("Bh2", [128, HT * C], BF16)
    acc = sb("acc", [128, DT * C])
    obuf = [sb(f"obuf{i}", [128, D]) for i in range(2)]
    slab_big = [sb(f"slabB{i}", [128, HT * 128], BF16) for i in range(2)]
    slab_sml = [sb(f"slabS{i}", [128, max(2 * DT, ET) * 128], BF16) for i in range(3)]
    tsil = sb("tsil", [128, C])

    ps = [stk.enter_context(nc.psum_tensor(f"ps{i}", [128, 512], F32)) for i in range(8)]

    mksem = lambda name: _S(stk.enter_context(nc.semaphore(name)))
    SXC = mksem("SXC")
    SXT = [mksem(f"SXT{i}") for i in range(4)]
    SN = [mksem(f"SN{i}") for i in range(4)]
    SC = [mksem(f"SC{i}") for i in range(2)]
    SBG = [mksem(f"SBG{i}") for i in range(3)]
    SSL = [mksem(f"SSL{i}") for i in range(3)]
    SO = [mksem(f"SO{i}") for i in range(2)]
    SPE = mksem("SPE")
    SACT = mksem("SACT")
    SDVE = mksem("SDVE")
    SDBG = mksem("SDBG")

    Q = dict(sync=[], scalar=[], vector=[], tensor=[], gpsimd=[])
    _last_wait = {}

    def w(eng, sem, val):
        if val <= 0:
            return
        key = (eng, id(sem))
        if _last_wait.get(key, 0) >= val:
            return
        _last_wait[key] = val
        Q[eng].append(lambda e, s=sem, v=val: e.wait_ge(s.h, v))

    def op(eng, f, inc=None, n=1):
        if inc is not None:
            inc.n += n
            Q[eng].append(lambda e, f=f, s=inc, n=n: f(e).then_inc(s.h, n))
            return inc.n
        Q[eng].append(lambda e, f=f: f(e))
        return None

    bank_free = {i: (SPE, 0) for i in range(8)}

    def wait_banks(eng, banks):
        for b in banks:
            sm, v = bank_free[b]
            w(eng, sm, v)

    for dst, src in ((cI, cId), (cJ, cJd), (s1t, s1d), (s2t, s2d),
                     (h1st, h1sd), (h2st, h2sd), (w3st, w3sd)):
        op("sync", lambda e, d=dst, s=src: e.dma_start(out=d[:, :], in_=s), SXC, 16)
    setup_loads = SXC.n
    op("vector", lambda e: e.memset(carry[:, :], 0.0), SDVE)
    op("vector", lambda e: e.memset(epsT[:, :], 1e-5), SDVE)

    maxK = max(D, E, H)
    if HT * C // 2 >= 2 * maxK and 2 * DT * C >= 2 * maxK:
        natw = [Bh1[:].bitcast(F32)[:, 0:maxK], Bh1[:].bitcast(F32)[:, maxK:2 * maxK],
                Bh2[:].bitcast(F32)[:, 0:maxK], Bh2[:].bitcast(F32)[:, maxK:2 * maxK]]
        staging = [acc[:].bitcast(BF16)[:, 0:maxK], acc[:].bitcast(BF16)[:, maxK:2 * maxK]]
    else:
        natw_t = sb("natw_t", [128, 2 * maxK])
        natw_t2 = sb("natw_t2", [128, 2 * maxK])
        staging_t = sb("staging_t", [128, 2 * maxK], BF16)
        natw = [natw_t[:, 0:maxK], natw_t[:, maxK:2 * maxK],
                natw_t2[:, 0:maxK], natw_t2[:, maxK:2 * maxK]]
        staging = [staging_t[:, 0:maxK], staging_t[:, maxK:2 * maxK]]
    rp = dict(pe_done=[0, 0, 0, 0], sc_done=[0, 0], diag_free=[0, 0], i=0)

    w("scalar", SXC, setup_loads)
    w("tensor", SXC, setup_loads)
    w("vector", SXC, setup_loads)

    def repack(wdram, scr3, KTm, scale_t, row0=0, col_off=0):
        scr, MT, KTtot = scr3
        K = KTm * 128
        for mt in range(MT):
            it = rp["i"]
            rp["i"] += 1
            slot = it % 4
            stg = it % 2
            w("sync", SPE, rp["pe_done"][slot])
            w("sync", SC[stg], rp["sc_done"][stg])
            op("sync", lambda e, sl=slot, mt=mt, K=K, wd=wdram, row0=row0:
               e.dma_start(out=natw[sl][:, 0:K],
                           in_=wd[row0 + mt * 128:row0 + (mt + 1) * 128, 0:K]),
               SN[slot], 16)
            ldn = SN[slot].n
            if scale_t is not None:
                dq = it % 2
                w("vector", SPE, rp["diag_free"][dq])
                dgn = op("vector", lambda e, dq=dq, st_=scale_t, mt=mt:
                         e.tensor_scalar(out=diagb[dq][:, :], in0=cI[:, :],
                                         scalar1=st_[:, mt:mt + 1], scalar2=None,
                                         op0=ALU.mult), SDVE)
                w("tensor", SDVE, dgn)
                idop = diagb[dq]
            else:
                idop = cI
            w("tensor", SN[slot], ldn)
            src = natw[slot]
            evs = []
            for kt in range(KTm):
                bq = (kt // 4) % 2
                if kt % 4 == 0:
                    wait_banks("tensor", [bq])
                if scale_t is not None:
                    last = op("tensor", lambda e, kt=kt, bq=bq, src=src, idop=idop:
                              e.matmul(ps[bq][:, (kt % 4) * 128:(kt % 4) * 128 + 128],
                                       src[:, kt * 128:(kt + 1) * 128], idop[:, :],
                                       start=True, stop=True),
                              SPE)
                else:
                    last = op("tensor", lambda e, kt=kt, bq=bq, src=src, idop=idop:
                              e.transpose(ps[bq][:, (kt % 4) * 128:(kt % 4) * 128 + 128],
                                          src[:, kt * 128:(kt + 1) * 128], idop[:, :]),
                              SPE)
                if kt % 4 == 3 or kt == KTm - 1:
                    nt = (kt % 4) + 1
                    w("scalar", SPE, last)
                    if kt < 4:
                        w("scalar", SC[stg], rp["sc_done"][stg])
                    evn = op("scalar", lambda e, sl=stg, kt=kt, bq=bq, nt=nt:
                             e.copy(out=staging[sl][:, (kt - nt + 1) * 128:(kt + 1) * 128],
                                    in_=ps[bq][:, 0:nt * 128]), SACT)
                    bank_free[bq] = (SACT, evn)
                    evs.append(evn)
            rp["pe_done"][slot] = SPE.n
            if scale_t is not None:
                rp["diag_free"][it % 2] = SPE.n
            w("gpsimd", SACT, evs[-1])
            op("gpsimd", lambda e, sl=stg, mt=mt, K=K, scr=scr, co=col_off * 128:
               e.dma_start(out=scr[mt:mt + 1, :, co:co + K].rearrange("m p j -> (m p) j"),
                           in_=staging[sl][:, 0:K]),
               SC[stg], 16)
            rp["sc_done"][stg] = SC[stg].n

    repack(w_hg, whgp_s, DT, None, row0=0, col_off=0)
    repack(w_hg, whgp_s, DT, None, row0=E, col_off=DT)
    repack(w1, w12_s, DT, s1t, col_off=0)
    repack(w2, w12_s, DT, s2t, col_off=DT)
    repack(w_out, wout_s, ET, None)
    repack(h1, h1_s, HT, h1st)
    repack(h2, h2_s, HT, h2st)
    repack(w3, w3_s, HT, w3st)
    w("sync", SC[0], SC[0].n)
    w("sync", SC[1], SC[1].n)

    slab_free = {"big": [(SPE, 0)] * 3, "sml": [(SPE, 0)] * 3}
    slab_idx = {"big": 0, "sml": 0}

    def load_slab(scr3, mt, big, split=1):
        scr, MT, KT = scr3
        kind = "big" if big else "sml"
        pool = slab_big if big else slab_sml
        sgrp = SBG if big else SSL
        sl = slab_idx[kind] % len(pool)
        slab_idx[kind] += 1
        sm, v = slab_free[kind][sl]
        w("sync", sm, v)
        marks = []
        step = KT // split
        for s in range(split):
            c0 = s * step * 128
            c1 = (s + 1) * step * 128 if s < split - 1 else KT * 128
            op("sync", lambda e, sl=sl, mt=mt, scr=scr, pool=pool, c0=c0, c1=c1:
               e.dma_start(out=pool[sl][:, c0:c1],
                           in_=scr[mt:mt + 1, :, c0:c1].rearrange("m p j -> (m p) j")),
               sgrp[sl], 16)
            marks.append(sgrp[sl].n)
        return pool[sl], (kind, sl), sgrp[sl], marks

    x_slot_free = [(SDVE, 0)] * 4
    o_slot_free = [0, 0]
    xnfm_reader = 0
    xnrev_reader = 0
    states_reader = 0
    Bh1_pe_reader = 0
    Bh2_pe_reader = 0
    acc_pe_reader = 0
    tsil_reader = [0, 0]
    prev_scanbufs_free = 0
    prev_stat_done = 0
    prev_xn_tr = 0
    n_groups = 0
    x_rr = dict(i=0)

    def load_x(seq, tok0):
        sl = x_rr["i"] % len(x_tm)
        x_rr["i"] += 1
        sm, v = x_slot_free[sl]
        w("sync", sm, v)
        op("sync", lambda e, sl=sl, seq=seq, tok0=tok0:
           e.dma_start(out=x_tm[sl][:, :],
                       in_=xv[seq * T + tok0:seq * T + tok0 + 128, :]),
           SXT[sl], 16)
        return sl, SXT[sl].n

    for phn in range(PH):
        ch_f, ch_b = phn, PH - 1 - phn

        w("scalar", SPE, max(xnfm_reader, xnrev_reader))
        chunk_ready = {}
        tt = 0
        for seq in range(B):
            for kind, chunk in (("f", ch_f), ("b", ch_b)):
                for ti in range(TI):
                    tok0 = chunk * C + ti * 128
                    sl, ldn = load_x(seq, tok0)
                    w("vector", SXT[sl], ldn)
                    op("vector", lambda e, sl=sl:
                       e.reduce_sum(stat[:, 0:1], x_tm[sl][:, :], axis=AX.X), SDVE)
                    osl = tt % 2
                    w("scalar", SXT[sl], ldn)
                    w("scalar", SO[osl], o_slot_free[osl])
                    w("scalar", SDVE, prev_stat_done)
                    sqn = op("scalar", lambda e, sl=sl, osl=osl:
                             e.activation(out=obuf[osl][:, :], in_=x_tm[sl][:, :],
                                          func=AF.Square), SACT)
                    w("vector", SACT, sqn)
                    s1n = op("vector", lambda e, osl=osl:
                             e.reduce_sum(stat[:, 1:2], obuf[osl][:, :], axis=AX.X), SDVE)
                    w("scalar", SDVE, s1n)
                    op("scalar", lambda e: e.activation(out=stat[:, 2:3], in_=stat[:, 0:1],
                                                        func=AF.Square, scale=1.0 / D), SACT)
                    op("scalar", lambda e: e.activation(out=stat[:, 3:4], in_=stat[:, 0:1],
                                                        func=AF.Copy, scale=1.0 / D), SACT)
                    msn = op("scalar", lambda e: e.activation(out=stat[:, 4:5], in_=stat[:, 1:2],
                                                              func=AF.Copy, scale=1.0 / D), SACT)
                    w("vector", SACT, msn)
                    vn = op("vector", lambda e: e.tensor_tensor(out=stat[:, 5:6], in0=stat[:, 4:5],
                                                                in1=stat[:, 2:3], op=ALU.subtract), SDVE)
                    w("scalar", SDVE, vn)
                    sqn2 = op("scalar", lambda e: e.activation(out=stat[:, 6:7], in_=stat[:, 5:6],
                                                               func=AF.Sqrt, bias=epsT[:, 0:1]), SACT)
                    w("vector", SACT, sqn2)
                    rcn = op("vector", lambda e: e.reciprocal(out=stat[:, 7:8], in_=stat[:, 6:7]), SDVE)
                    w("scalar", SDVE, rcn)
                    rbn = op("scalar", lambda e: e.activation(out=stat[:, 5:6], in_=stat[:, 7:8],
                                                              func=AF.Copy), SACT)
                    w("vector", SACT, rbn)
                    w("vector", SPE, prev_xn_tr)
                    xn_done = op("vector", lambda e, sl=sl: e.tensor_scalar(
                        out=xn_tm[:, :], in0=x_tm[sl][:, :], scalar1=stat[:, 3:4],
                        scalar2=stat[:, 5:6], op0=ALU.subtract, op1=ALU.mult), SDVE)
                    prev_stat_done = xn_done
                    x_slot_free[sl] = (SDVE, xn_done)
                    bq = tt % 2
                    b0, b1 = (4, 5) if bq == 0 else (6, 7)
                    w("tensor", SDVE, xn_done)
                    wait_banks("tensor", [b0, b1])
                    idmat = cI if kind == "f" else cJ
                    half = DT // 2
                    for dt in range(DT):
                        bank = b0 if dt < half else b1
                        colp = (dt % half) * 128
                        op("tensor", lambda e, bank=bank, colp=colp, dt=dt, idmat=idmat:
                           e.transpose(ps[bank][:, colp:colp + 128],
                                       xn_tm[:, dt * 128:(dt + 1) * 128], idmat[:, :]), SPE)
                    tr_done = SPE.n
                    prev_xn_tr = tr_done
                    dst_t = xn_fm if kind == "f" else xn_rev
                    cti = ti if kind == "f" else (TI - 1 - ti)
                    col = seq * C + cti * 128
                    w("scalar", SPE, tr_done)
                    ev1 = op("scalar", lambda e, b0=b0, dst_t=dst_t, col=col, half=half:
                             e.copy(out=dst_t[:, :].rearrange("p (dt n) -> p dt n", n=NPH)[:, 0:half, col:col + 128],
                                    in_=ps[b0][:, 0:half * 128].rearrange("p (dt j) -> p dt j", j=128)),
                             SACT)
                    ev2 = op("scalar", lambda e, b1=b1, dst_t=dst_t, col=col, half=half:
                             e.copy(out=dst_t[:, :].rearrange("p (dt n) -> p dt n", n=NPH)[:, half:DT, col:col + 128],
                                    in_=ps[b1][:, 0:(DT - half) * 128].rearrange("p (dt j) -> p dt j", j=128)),
                             SACT)
                    bank_free[b0] = (SACT, ev1)
                    bank_free[b1] = (SACT, ev2)
                    tt += 1
                chunk_ready[(kind, seq)] = SACT.n
        xn_ready = SACT.n
        if dbg and phn == 0:
            w("gpsimd", SACT, xn_ready)
            op("gpsimd", lambda e: e.dma_start(out=d_xnfm, in_=xn_fm[:, :]), SDBG, 16)
            op("gpsimd", lambda e: e.dma_start(out=d_xnrev, in_=xn_rev[:, :]), SDBG, 16)

        w("vector", SPE, states_reader)
        for pr in range(ET):
            pair_slab, pair_slot, pair_sm, pair_ld = load_slab(whgp_s, pr, big=False)
            for dir_, seq in ((0, 0), (1, 0), (0, 1), (1, 1)):
                qq = n_groups % 2
                p_hid, p_gat = ps[2 * qq], ps[2 * qq + 1]
                src = xn_fm if dir_ == 0 else xn_rev
                w("tensor", pair_sm, pair_ld[-1])
                w("tensor", SACT, chunk_ready[("f" if dir_ == 0 else "b", seq)])
                wait_banks("tensor", [2 * qq, 2 * qq + 1])
                hg_done = None
                for which, pps in ((0, p_hid), (1, p_gat)):
                    for kt in range(DT):
                        is_last = (which == 1 and kt == DT - 1)
                        ko = which * DT + kt
                        hg_done = op("tensor", lambda e, pps=pps, slabb=pair_slab, ko=ko, kt=kt, src=src, seq=seq:
                                     e.matmul(pps[:, 0:C],
                                              slabb[:, ko * 128:(ko + 1) * 128],
                                              src[:, kt * NPH + seq * C:kt * NPH + seq * C + C],
                                              start=(kt == 0), stop=(kt == DT - 1)),
                                     SPE if is_last else None)
                if dir_ == 0:
                    xnfm_reader = max(xnfm_reader, SPE.n)
                else:
                    xnrev_reader = max(xnrev_reader, SPE.n)
                if (dir_, seq) == (1, 1):
                    slab_free[pair_slot[0]][pair_slot[1]] = (SPE, SPE.n)
                w("scalar", SPE, hg_done)
                w("scalar", SDVE, prev_scanbufs_free)
                op("scalar", lambda e, p_gat=p_gat: e.activation(out=zb[:, 0:C], in_=p_gat[:, 0:C],
                                                                 func=AF.Sigmoid), SACT)
                op("scalar", lambda e, p_gat=p_gat: e.activation(out=cbuf[:, 0:C], in_=p_gat[:, 0:C],
                                                                 func=AF.Sigmoid, scale=-1.0), SACT)
                prep = op("scalar", lambda e, p_hid=p_hid: e.activation(out=sgb[:, 0:C], in_=p_hid[:, 0:C],
                                                                        func=AF.Sigmoid), SACT)
                w("vector", SACT, prep)
                tbn = op("vector", lambda e, p_hid=p_hid: e.scalar_tensor_tensor(
                    out=tb[:, 0:C], in0=p_hid[:, 0:C], scalar=0.5, in1=sgb[:, 0:C],
                    op0=ALU.add, op1=ALU.subtract), SDVE)
                bank_free[2 * qq] = (SDVE, tbn)
                bank_free[2 * qq + 1] = (SDVE, tbn)
                op("vector", lambda e: e.scalar_tensor_tensor(
                    out=gbuf[:, 0:C], in0=tb[:, 0:C], scalar=0.0, in1=sgb[:, 0:C],
                    op0=ALU.max, op1=ALU.add))
                op("vector", lambda e: e.tensor_tensor(out=vbuf[:, 0:C], in0=zb[:, 0:C],
                                                       in1=gbuf[:, 0:C], op=ALU.mult))
                fb = fbA if dir_ == 0 else fbB
                cc = (dir_ * B + seq) * ET + pr
                scn = op("vector", lambda e, fb=fb, cc=cc: e.tensor_tensor_scan(
                    out=fb[:, 0:C], data0=cbuf[:, 0:C], data1=vbuf[:, 0:C],
                    initial=carry[:, cc:cc + 1], op0=ALU.mult, op1=ALU.add), SDVE)
                prev_scanbufs_free = scn
                if dir_ == 1:
                    op("vector", lambda e, pr=pr, seq=seq: e.tensor_tensor(
                        out=states[:, pr * NPH + seq * C:pr * NPH + seq * C + C],
                        in0=fbA[:, 0:C], in1=fbB[:, 0:C], op=ALU.add), SDVE)
                    ccf = (0 * B + seq) * ET + pr
                    op("vector", lambda e, ccf=ccf:
                       e.tensor_copy(out=carry[:, ccf:ccf + 1], in_=fbA[:, C - 1:C]))
                    op("vector", lambda e, cc=cc:
                       e.tensor_copy(out=carry[:, cc:cc + 1], in_=fbB[:, C - 1:C]))
                n_groups += 1
        states_ready = SDVE.n
        if dbg and phn == 0:
            w("gpsimd", SDVE, states_ready)
            op("gpsimd", lambda e: e.dma_start(out=d_states, in_=states[:, :]), SDBG, 16)

        for seq in range(B):
            for mt in range(HT):
                s12b, s12slot, s12sm, ld12 = load_slab(w12_s, mt, big=False)
                qq = mt % 2
                pw1, pw2 = ps[2 * qq], ps[2 * qq + 1]
                w("tensor", s12sm, ld12[-1])
                wait_banks("tensor", [2 * qq, 2 * qq + 1])
                w1_done = w2_done = None
                for which, pps in ((0, pw1), (1, pw2)):
                    for kt in range(DT):
                        ko = which * DT + kt
                        mk = op("tensor", lambda e, pps=pps, slabb=s12b, ko=ko, kt=kt, seq=seq:
                                e.matmul(pps[:, 0:C], slabb[:, ko * 128:(ko + 1) * 128],
                                         xn_fm[:, kt * NPH + seq * C:kt * NPH + seq * C + C],
                                         start=(kt == 0), stop=(kt == DT - 1)),
                                SPE if kt == DT - 1 else None)
                        if kt == DT - 1:
                            if which == 0:
                                w1_done = mk
                            else:
                                w2_done = mk
                xnfm_reader = max(xnfm_reader, SPE.n)
                slab_free[s12slot[0]][s12slot[1]] = (SPE, SPE.n)
                w("scalar", SPE, w1_done)
                w("scalar", SDVE, tsil_reader[0])
                sil = op("scalar", lambda e, pw1=pw1:
                         e.activation(out=tsil[:, 0:C], in_=pw1[:, 0:C],
                                      func=AF.Silu), SACT)
                bank_free[2 * qq] = (SACT, sil)
                w("vector", SACT, sil)
                w("vector", SPE, max(w2_done, Bh1_pe_reader))
                ha = op("vector", lambda e, pw2=pw2, mt=mt: e.tensor_tensor(
                    out=Bh1[:, mt * C:(mt + 1) * C], in0=tsil[:, 0:C],
                    in1=pw2[:, 0:C], op=ALU.mult), SDVE)
                tsil_reader[0] = ha
                bank_free[2 * qq + 1] = (SDVE, ha)
            ha_ready = SDVE.n
            if dbg and phn == 0 and seq == 0:
                w("gpsimd", SDVE, ha_ready)
                op("gpsimd", lambda e: e.dma_start(out=d_ha, in_=Bh1[:, :]), SDBG, 16)

            for mt in range(HT):
                hb, hslot, hsm, ldh = load_slab(h1_s, mt, big=True, split=2)
                qq = mt % 2
                pp = ps[qq]
                w("tensor", hsm, ldh[0])
                w("tensor", SDVE, ha_ready)
                wait_banks("tensor", [qq])
                mm_done = None
                for kt in range(HT):
                    if kt == HT // 2:
                        w("tensor", hsm, ldh[1])
                    mm_done = op("tensor", lambda e, pp=pp, hb=hb, kt=kt:
                                 e.matmul(pp[:, 0:C], hb[:, kt * 128:(kt + 1) * 128],
                                          Bh1[:, kt * C:(kt + 1) * C],
                                          start=(kt == 0), stop=(kt == HT - 1)),
                                 SPE if kt == HT - 1 else None)
                Bh1_pe_reader = max(Bh1_pe_reader, SPE.n)
                slab_free[hslot[0]][hslot[1]] = (SPE, SPE.n)
                w("scalar", SPE, max(mm_done, Bh2_pe_reader))
                evn = op("scalar", lambda e, pp=pp, mt=mt:
                         e.activation(out=Bh2[:, mt * C:(mt + 1) * C], in_=pp[:, 0:C],
                                      func=AF.Gelu), SACT)
                bank_free[qq] = (SACT, evn)
            hb_ready = SACT.n

            h1_reader_spe = Bh1_pe_reader
            for mt in range(HT):
                hb, hslot, hsm, ldh = load_slab(h2_s, mt, big=True, split=2)
                qq = mt % 2
                pp = ps[2 + qq]
                w("tensor", hsm, ldh[0])
                w("tensor", SACT, hb_ready)
                wait_banks("tensor", [2 + qq])
                mm_done = None
                for kt in range(HT):
                    if kt == HT // 2:
                        w("tensor", hsm, ldh[1])
                    mm_done = op("tensor", lambda e, pp=pp, hb=hb, kt=kt:
                                 e.matmul(pp[:, 0:C], hb[:, kt * 128:(kt + 1) * 128],
                                          Bh2[:, kt * C:(kt + 1) * C],
                                          start=(kt == 0), stop=(kt == HT - 1)),
                                 SPE if kt == HT - 1 else None)
                Bh2_pe_reader = max(Bh2_pe_reader, SPE.n)
                slab_free[hslot[0]][hslot[1]] = (SPE, SPE.n)
                w("scalar", SPE, max(mm_done, h1_reader_spe))
                evn = op("scalar", lambda e, pp=pp, mt=mt:
                         e.activation(out=Bh1[:, mt * C:(mt + 1) * C], in_=pp[:, 0:C],
                                      func=AF.Gelu), SACT)
                bank_free[2 + qq] = (SACT, evn)
            hc_ready = SACT.n

            for mt in range(DT):
                wob, woslot, wosm, ldo = load_slab(wout_s, mt, big=False)
                w3b, w3slot, w3sm, ld3 = load_slab(w3_s, mt, big=True, split=2)
                qq = mt % 2
                pp = ps[qq]
                w("tensor", wosm, ldo[-1])
                w("tensor", w3sm, ld3[0])
                w("tensor", SDVE, states_ready)
                w("tensor", SACT, hc_ready)
                wait_banks("tensor", [qq])
                mm_done = None
                for kt in range(ET):
                    op("tensor", lambda e, pp=pp, wob=wob, kt=kt, seq=seq:
                       e.matmul(pp[:, 0:C], wob[:, kt * 128:(kt + 1) * 128],
                                states[:, kt * NPH + seq * C:kt * NPH + seq * C + C],
                                start=(kt == 0), stop=False))
                for kt in range(HT):
                    if kt == HT // 2:
                        w("tensor", w3sm, ld3[1])
                    mm_done = op("tensor", lambda e, pp=pp, w3b=w3b, kt=kt:
                                 e.matmul(pp[:, 0:C], w3b[:, kt * 128:(kt + 1) * 128],
                                          Bh1[:, kt * C:(kt + 1) * C],
                                          start=False, stop=(kt == HT - 1)),
                                 SPE if kt == HT - 1 else None)
                states_reader = max(states_reader, SPE.n)
                Bh1_pe_reader = max(Bh1_pe_reader, SPE.n)
                slab_free[woslot[0]][woslot[1]] = (SPE, SPE.n)
                slab_free[w3slot[0]][w3slot[1]] = (SPE, SPE.n)
                w("scalar", SPE, max(mm_done, acc_pe_reader))
                evn = op("scalar", lambda e, pp=pp, mt=mt:
                         e.copy(out=acc[:, mt * C:(mt + 1) * C], in_=pp[:, 0:C]), SACT)
                bank_free[qq] = (SACT, evn)
            acc_ready = SACT.n
            if dbg and phn == 0 and seq == 0:
                w("gpsimd", SACT, acc_ready)
                op("gpsimd", lambda e: e.dma_start(out=d_acc, in_=acc[:, :]), SDBG, 16)

            for ti in range(TI):
                tok0 = ch_f * C + ti * 128
                bq = ti % 2
                b0, b1 = (4, 5) if bq == 0 else (6, 7)
                half = DT // 2
                w("tensor", SACT, acc_ready)
                wait_banks("tensor", [b0, b1])
                tr_done = None
                for dt in range(DT):
                    bank = b0 if dt < half else b1
                    colp = (dt % half) * 128
                    tr_done = op("tensor", lambda e, bank=bank, colp=colp, dt=dt, ti=ti:
                                 e.transpose(ps[bank][:, colp:colp + 128],
                                             acc[:, dt * C + ti * 128:dt * C + ti * 128 + 128],
                                             cI[:, :]),
                                 SPE if dt == DT - 1 else None)
                acc_pe_reader = max(acc_pe_reader, SPE.n)
                xsl, xldn = load_x(seq, tok0)
                osl = ti % 2
                w("vector", SPE, tr_done)
                w("vector", SXT[xsl], xldn)
                w("vector", SO[osl], o_slot_free[osl])
                op("vector", lambda e, osl=osl, xsl=xsl, b0=b0, half=half: e.tensor_tensor(
                    out=obuf[osl][:, 0:half * 128], in0=x_tm[xsl][:, 0:half * 128],
                    in1=ps[b0][:, 0:half * 128], op=ALU.add))
                addn = op("vector", lambda e, osl=osl, xsl=xsl, b1=b1, half=half: e.tensor_tensor(
                    out=obuf[osl][:, half * 128:D], in0=x_tm[xsl][:, half * 128:D],
                    in1=ps[b1][:, 0:(DT - half) * 128], op=ALU.add), SDVE)
                bank_free[b0] = (SDVE, addn)
                bank_free[b1] = (SDVE, addn)
                x_slot_free[xsl] = (SDVE, addn)
                w("gpsimd", SDVE, addn)
                op("gpsimd", lambda e, osl=osl, seq=seq, tok0=tok0:
                   e.dma_start(out=yv[seq * T + tok0:seq * T + tok0 + 128, :],
                               in_=obuf[osl][:, :]),
                   SO[osl], 16)
                o_slot_free[osl] = SO[osl].n

    w("gpsimd", SO[0], SO[0].n)
    w("gpsimd", SO[1], SO[1].n)
    if dbg:
        w("gpsimd", SDBG, SDBG.n)

    with nc.Block() as block:
        @block.sync
        def _(e):
            for f in Q["sync"]:
                f(e)

        @block.scalar
        def _(e):
            for f in Q["scalar"]:
                f(e)

        @block.vector
        def _(e):
            for f in Q["vector"]:
                f(e)

        @block.tensor
        def _(e):
            for f in Q["tensor"]:
                f(e)

        @block.gpsimd
        def _(e):
            for f in Q["gpsimd"]:
                f(e)

    stk.close()
    return nc


def _layout_scale(s, MT):
    s = np.asarray(s, dtype=np.float32)
    return np.ascontiguousarray(s.reshape(MT, 128).T)


def kernel(L0_attn_x, L0_attn_W_hg, L0_attn_W_out, L0_ffn_w1_w, L0_ffn_w1_s,
           L0_ffn_w2_w, L0_ffn_w2_s, L0_ffn_h1_w, L0_ffn_h1_s, L0_ffn_h2_w,
           L0_ffn_h2_s, L0_ffn_w3_w, L0_ffn_w3_s):
    cfg = CFG
    B, T, D, E, H = cfg["B"], cfg["T"], cfg["D"], cfg["E"], cfg["H"]
    DT, HT = D // 128, H // 128

    x = np.ascontiguousarray(np.asarray(L0_attn_x, dtype=np.float32))
    w_hg = np.ascontiguousarray(np.asarray(L0_attn_W_hg, dtype=np.float32))
    w_out = np.ascontiguousarray(np.asarray(L0_attn_W_out, dtype=np.float32))
    w1 = np.ascontiguousarray(np.asarray(L0_ffn_w1_w, dtype=np.float32))
    w2 = np.ascontiguousarray(np.asarray(L0_ffn_w2_w, dtype=np.float32))
    h1 = np.ascontiguousarray(np.asarray(L0_ffn_h1_w, dtype=np.float32))
    h2 = np.ascontiguousarray(np.asarray(L0_ffn_h2_w, dtype=np.float32))
    w3 = np.ascontiguousarray(np.asarray(L0_ffn_w3_w, dtype=np.float32))
    s1 = _layout_scale(L0_ffn_w1_s, HT)
    s2 = _layout_scale(L0_ffn_w2_s, HT)
    h1s = _layout_scale(L0_ffn_h1_s, HT)
    h2s = _layout_scale(L0_ffn_h2_s, HT)
    w3s = _layout_scale(L0_ffn_w3_s, DT)
    cIm = np.eye(128, dtype=np.float32)
    cJm = np.ascontiguousarray(cIm[::-1, :])

    import gc
    nc = _build(cfg)
    gc.collect()
    in_maps = []
    for c in range(N_CORES):
        in_maps.append(dict(
            x=np.ascontiguousarray(x[c * B:(c + 1) * B]),
            w_hg=w_hg, w_out=w_out, w1=w1, w2=w2, h1=h1, h2=h2, w3=w3,
            s1=s1, s2=s2, h1s=h1s, h2s=h2s, w3s=w3s, cI=cIm, cJ=cJm,
        ))
    res = run_bass_kernel_spmd(nc, in_maps, core_ids=list(range(N_CORES)))
    return np.concatenate([r["y"] for r in res.results], axis=0)


# version 08
# speedup vs baseline: 1.3076x; 1.0636x over previous
import numpy as np
import concourse.bass as bass
import concourse.mybir as mybir
from concourse.bass_utils import run_bass_kernel_spmd

F32 = mybir.dt.float32
BF16 = mybir.dt.bfloat16
AF = mybir.ActivationFunctionType
ALU = mybir.AluOpType
AX = mybir.AxisListType

CFG = dict(B=2, T=4096, D=1024, E=1536, H=4096, C=512)
N_CORES = 8


class _S:
    def __init__(self, h):
        self.h = h
        self.n = 0


def _build(cfg):
    B, T, D, E, H, C = cfg["B"], cfg["T"], cfg["D"], cfg["E"], cfg["H"], cfg["C"]
    DT, ET, HT = D // 128, E // 128, H // 128
    PH = T // C
    TI = C // 128
    NPH = B * C
    assert C <= 512 and C % 128 == 0

    nc = bass.Bass("TRN2", target_bir_lowering=False, debug=False,
                   detect_race_conditions=False)

    xd = nc.dram_tensor("x", [B, T, D], F32, kind="ExternalInput").ap()
    yd = nc.dram_tensor("y", [B, T, D], F32, kind="ExternalOutput").ap()
    xv = xd.rearrange("b t d -> (b t) d")
    yv = yd.rearrange("b t d -> (b t) d")

    w_hg = nc.dram_tensor("w_hg", [2 * E, D], F32, kind="ExternalInput").ap()
    w_out = nc.dram_tensor("w_out", [D, E], F32, kind="ExternalInput").ap()
    w1 = nc.dram_tensor("w1", [H, D], F32, kind="ExternalInput").ap()
    w2 = nc.dram_tensor("w2", [H, D], F32, kind="ExternalInput").ap()
    h1 = nc.dram_tensor("h1", [H, H], F32, kind="ExternalInput").ap()
    h2 = nc.dram_tensor("h2", [H, H], F32, kind="ExternalInput").ap()
    w3 = nc.dram_tensor("w3", [D, H], F32, kind="ExternalInput").ap()
    s1d = nc.dram_tensor("s1", [128, HT], F32, kind="ExternalInput").ap()
    s2d = nc.dram_tensor("s2", [128, HT], F32, kind="ExternalInput").ap()
    h1sd = nc.dram_tensor("h1s", [128, HT], F32, kind="ExternalInput").ap()
    h2sd = nc.dram_tensor("h2s", [128, HT], F32, kind="ExternalInput").ap()
    w3sd = nc.dram_tensor("w3s", [128, DT], F32, kind="ExternalInput").ap()
    cId = nc.dram_tensor("cI", [128, 128], F32, kind="ExternalInput").ap()
    cJd = nc.dram_tensor("cJ", [128, 128], F32, kind="ExternalInput").ap()

    dbg = cfg.get("dbg", False)
    if dbg:
        d_xnfm = nc.dram_tensor("d_xnfm", [128, DT * NPH], BF16, kind="ExternalOutput").ap()
        d_xnrev = nc.dram_tensor("d_xnrev", [128, DT * NPH], BF16, kind="ExternalOutput").ap()
        d_states = nc.dram_tensor("d_states", [128, ET * NPH], BF16, kind="ExternalOutput").ap()
        d_ha = nc.dram_tensor("d_ha", [128, HT * C], BF16, kind="ExternalOutput").ap()
        d_acc = nc.dram_tensor("d_acc", [128, DT * C], F32, kind="ExternalOutput").ap()

    def scratch(name, MT, KT):
        t = nc.dram_tensor(name, [MT, 128, KT * 128], BF16, kind="Internal").ap()
        return (t, MT, KT)
    whgp_s = scratch("whgp_s", ET, 2 * DT)
    w12_s = scratch("w12_s", HT, 2 * DT)
    wout_s = scratch("wout_s", DT, ET)
    h1_s = scratch("h1_s", HT, HT)
    h2_s = scratch("h2_s", HT, HT)
    w3_s = scratch("w3_s", DT, HT)

    from contextlib import ExitStack
    stk = ExitStack()
    sb = lambda name, shape, dt=F32: stk.enter_context(nc.sbuf_tensor(name, shape, dt))

    cI = sb("cIt", [128, 128]); cJ = sb("cJt", [128, 128])
    s1t = sb("s1t", [128, HT]); s2t = sb("s2t", [128, HT])
    h1st = sb("h1st", [128, HT]); h2st = sb("h2st", [128, HT])
    w3st = sb("w3st", [128, DT])
    epsT = sb("epsT", [128, 1])
    x_tm = [sb(f"x_tm{i}", [128, D]) for i in range(3)]
    diagb = [sb(f"diagb{i}", [128, 128]) for i in range(2)]
    xn_tm = sb("xn_tm", [128, D])
    stat = sb("stat", [128, 8])
    xn_fm = sb("xn_fm", [128, DT * NPH], BF16)
    xn_rev = sb("xn_rev", [128, DT * NPH], BF16)
    states = sb("states", [128, ET * NPH], BF16)
    carry = sb("carry", [128, 4 * ET])
    zb = sb("zb", [128, C]); cbuf = sb("cbuf", [128, C]); sgb = sb("sgb", [128, C])
    tb = sb("tb", [128, C]); gbuf = sb("gbuf", [128, C])
    vbuf = sgb
    fbA = sb("fbA", [128, C]); fbB = sb("fbB", [128, C])
    Bh1 = sb("Bh1", [128, HT * C], BF16)
    Bh2 = sb("Bh2", [128, HT * C], BF16)
    acc = sb("acc", [128, DT * C])
    obuf = [sb(f"obuf{i}", [128, D]) for i in range(2)]
    slab_big = [sb(f"slabB{i}", [128, HT * 128], BF16) for i in range(2)]
    slab_sml = [sb(f"slabS{i}", [128, max(2 * DT, ET) * 128], BF16) for i in range(3)]
    tsil = sb("tsil", [128, C])

    ps = [stk.enter_context(nc.psum_tensor(f"ps{i}", [128, 512], F32)) for i in range(8)]

    mksem = lambda name: _S(stk.enter_context(nc.semaphore(name)))
    SXC = mksem("SXC")
    SXT = [mksem(f"SXT{i}") for i in range(4)]
    SN = [mksem(f"SN{i}") for i in range(4)]
    SC = [mksem(f"SC{i}") for i in range(2)]
    SBG = [mksem(f"SBG{i}") for i in range(3)]
    SSL = [mksem(f"SSL{i}") for i in range(3)]
    SO = [mksem(f"SO{i}") for i in range(2)]
    SPE = mksem("SPE")
    SACT = mksem("SACT")
    SDVE = mksem("SDVE")
    SDBG = mksem("SDBG")

    Q = dict(sync=[], scalar=[], vector=[], tensor=[], gpsimd=[])
    _last_wait = {}

    def w(eng, sem, val):
        if val <= 0:
            return
        key = (eng, id(sem))
        if _last_wait.get(key, 0) >= val:
            return
        _last_wait[key] = val
        Q[eng].append(lambda e, s=sem, v=val: e.wait_ge(s.h, v))

    def op(eng, f, inc=None, n=1):
        if inc is not None:
            inc.n += n
            Q[eng].append(lambda e, f=f, s=inc, n=n: f(e).then_inc(s.h, n))
            return inc.n
        Q[eng].append(lambda e, f=f: f(e))
        return None

    bank_free = {i: (SPE, 0) for i in range(8)}

    def wait_banks(eng, banks):
        for b in banks:
            sm, v = bank_free[b]
            w(eng, sm, v)

    for dst, src in ((cI, cId), (cJ, cJd), (s1t, s1d), (s2t, s2d),
                     (h1st, h1sd), (h2st, h2sd), (w3st, w3sd)):
        op("sync", lambda e, d=dst, s=src: e.dma_start(out=d[:, :], in_=s), SXC, 16)
    setup_loads = SXC.n
    op("vector", lambda e: e.memset(carry[:, :], 0.0), SDVE)
    op("vector", lambda e: e.memset(epsT[:, :], 1e-5), SDVE)

    maxK = max(D, E, H)
    if HT * C // 2 >= 2 * maxK and 2 * DT * C >= 2 * maxK:
        natw = [Bh1[:].bitcast(F32)[:, 0:maxK], Bh1[:].bitcast(F32)[:, maxK:2 * maxK],
                Bh2[:].bitcast(F32)[:, 0:maxK], Bh2[:].bitcast(F32)[:, maxK:2 * maxK]]
        staging = [acc[:].bitcast(BF16)[:, 0:maxK], acc[:].bitcast(BF16)[:, maxK:2 * maxK]]
    else:
        natw_t = sb("natw_t", [128, 2 * maxK])
        natw_t2 = sb("natw_t2", [128, 2 * maxK])
        staging_t = sb("staging_t", [128, 2 * maxK], BF16)
        natw = [natw_t[:, 0:maxK], natw_t[:, maxK:2 * maxK],
                natw_t2[:, 0:maxK], natw_t2[:, maxK:2 * maxK]]
        staging = [staging_t[:, 0:maxK], staging_t[:, maxK:2 * maxK]]
    rp = dict(pe_done=[0, 0, 0, 0], sc_done=[0, 0], diag_free=[0, 0], i=0)

    w("scalar", SXC, setup_loads)
    w("tensor", SXC, setup_loads)
    w("vector", SXC, setup_loads)

    def repack(wdram, scr3, KTm, scale_t, row0=0, col_off=0):
        scr, MT, KTtot = scr3
        K = KTm * 128
        for mt in range(MT):
            it = rp["i"]
            rp["i"] += 1
            slot = it % 4
            stg = it % 2
            w("sync", SPE, rp["pe_done"][slot])
            w("sync", SC[stg], rp["sc_done"][stg])
            op("sync", lambda e, sl=slot, mt=mt, K=K, wd=wdram, row0=row0:
               e.dma_start(out=natw[sl][:, 0:K],
                           in_=wd[row0 + mt * 128:row0 + (mt + 1) * 128, 0:K]),
               SN[slot], 16)
            ldn = SN[slot].n
            if scale_t is not None:
                dq = it % 2
                w("vector", SPE, rp["diag_free"][dq])
                dgn = op("vector", lambda e, dq=dq, st_=scale_t, mt=mt:
                         e.tensor_scalar(out=diagb[dq][:, :], in0=cI[:, :],
                                         scalar1=st_[:, mt:mt + 1], scalar2=None,
                                         op0=ALU.mult), SDVE)
                w("tensor", SDVE, dgn)
                idop = diagb[dq]
            else:
                idop = cI
            w("tensor", SN[slot], ldn)
            src = natw[slot]
            evs = []
            for kt in range(KTm):
                bq = (kt // 4) % 2
                if kt % 4 == 0:
                    wait_banks("tensor", [bq])
                if scale_t is not None:
                    last = op("tensor", lambda e, kt=kt, bq=bq, src=src, idop=idop:
                              e.matmul(ps[bq][:, (kt % 4) * 128:(kt % 4) * 128 + 128],
                                       src[:, kt * 128:(kt + 1) * 128], idop[:, :],
                                       start=True, stop=True),
                              SPE)
                else:
                    last = op("tensor", lambda e, kt=kt, bq=bq, src=src, idop=idop:
                              e.transpose(ps[bq][:, (kt % 4) * 128:(kt % 4) * 128 + 128],
                                          src[:, kt * 128:(kt + 1) * 128], idop[:, :]),
                              SPE)
                if kt % 4 == 3 or kt == KTm - 1:
                    nt = (kt % 4) + 1
                    w("scalar", SPE, last)
                    if kt < 4:
                        w("scalar", SC[stg], rp["sc_done"][stg])
                    evn = op("scalar", lambda e, sl=stg, kt=kt, bq=bq, nt=nt:
                             e.copy(out=staging[sl][:, (kt - nt + 1) * 128:(kt + 1) * 128],
                                    in_=ps[bq][:, 0:nt * 128]), SACT)
                    bank_free[bq] = (SACT, evn)
                    evs.append(evn)
            rp["pe_done"][slot] = SPE.n
            if scale_t is not None:
                rp["diag_free"][it % 2] = SPE.n
            w("gpsimd", SACT, evs[-1])
            op("gpsimd", lambda e, sl=stg, mt=mt, K=K, scr=scr, co=col_off * 128:
               e.dma_start(out=scr[mt:mt + 1, :, co:co + K].rearrange("m p j -> (m p) j"),
                           in_=staging[sl][:, 0:K]),
               SC[stg], 16)
            rp["sc_done"][stg] = SC[stg].n

    repack(w_hg, whgp_s, DT, None, row0=0, col_off=0)
    repack(w_hg, whgp_s, DT, None, row0=E, col_off=DT)
    repack(w1, w12_s, DT, s1t, col_off=0)
    repack(w2, w12_s, DT, s2t, col_off=DT)
    repack(w_out, wout_s, ET, None)
    repack(h1, h1_s, HT, h1st)
    repack(h2, h2_s, HT, h2st)
    repack(w3, w3_s, HT, w3st)
    w("sync", SC[0], SC[0].n)
    w("sync", SC[1], SC[1].n)

    slab_free = {"big": [(SPE, 0)] * 3, "sml": [(SPE, 0)] * 3}
    slab_idx = {"big": 0, "sml": 0}

    def load_slab(scr3, mt, big, split=1):
        scr, MT, KT = scr3
        kind = "big" if big else "sml"
        pool = slab_big if big else slab_sml
        sgrp = SBG if big else SSL
        sl = slab_idx[kind] % len(pool)
        slab_idx[kind] += 1
        sm, v = slab_free[kind][sl]
        w("sync", sm, v)
        marks = []
        step = KT // split
        for s in range(split):
            c0 = s * step * 128
            c1 = (s + 1) * step * 128 if s < split - 1 else KT * 128
            op("sync", lambda e, sl=sl, mt=mt, scr=scr, pool=pool, c0=c0, c1=c1:
               e.dma_start(out=pool[sl][:, c0:c1],
                           in_=scr[mt:mt + 1, :, c0:c1].rearrange("m p j -> (m p) j")),
               sgrp[sl], 16)
            marks.append(sgrp[sl].n)
        return pool[sl], (kind, sl), sgrp[sl], marks

    x_slot_free = [(SDVE, 0)] * 4
    o_slot_free = [0, 0]
    xnfm_reader = 0
    xnrev_reader = 0
    states_reader = 0
    Bh1_pe_reader = 0
    Bh2_pe_reader = 0
    acc_pe_reader = 0
    tsil_reader = [0, 0]
    prev_scanbufs_free = 0
    prev_stat_done = 0
    prev_xn_tr = 0
    n_groups = 0
    x_rr = dict(i=0)

    def load_x(seq, tok0):
        sl = x_rr["i"] % len(x_tm)
        x_rr["i"] += 1
        sm, v = x_slot_free[sl]
        w("sync", sm, v)
        op("sync", lambda e, sl=sl, seq=seq, tok0=tok0:
           e.dma_start(out=x_tm[sl][:, :],
                       in_=xv[seq * T + tok0:seq * T + tok0 + 128, :]),
           SXT[sl], 16)
        return sl, SXT[sl].n

    misc = dict(prev_stat_done=0, prev_xn_tr=0, xnfm_reader=0, xnrev_reader=0,
                states_reader=0, Bh1_pe=0, Bh2_pe=0, acc_pe=0, tsil_rd=0,
                scanbufs_free=0, tt=0, tt_b=0, pending_xn=0, n_groups=0)

    def emit_T_ln(seq, kind, chunk, ti, cr, first):
        if first["v"]:
            w("scalar", SPE, max(misc["xnfm_reader"], misc["xnrev_reader"]))
            first["v"] = False
        tok0 = chunk * C + ti * 128
        sl, ldn = load_x(seq, tok0)
        tt = misc["tt"]
        w("vector", SXT[sl], ldn)
        op("vector", lambda e, sl=sl:
           e.reduce_sum(stat[:, 0:1], x_tm[sl][:, :], axis=AX.X), SDVE)
        osl = tt % 2
        w("scalar", SXT[sl], ldn)
        w("scalar", SO[osl], o_slot_free[osl])
        w("scalar", SDVE, misc["prev_stat_done"])
        sqn = op("scalar", lambda e, sl=sl, osl=osl:
                 e.activation(out=obuf[osl][:, :], in_=x_tm[sl][:, :],
                              func=AF.Square), SACT)
        w("vector", SACT, sqn)
        s1n = op("vector", lambda e, osl=osl:
                 e.reduce_sum(stat[:, 1:2], obuf[osl][:, :], axis=AX.X), SDVE)
        w("scalar", SDVE, s1n)
        op("scalar", lambda e: e.activation(out=stat[:, 2:3], in_=stat[:, 0:1],
                                            func=AF.Square, scale=1.0 / D), SACT)
        op("scalar", lambda e: e.activation(out=stat[:, 3:4], in_=stat[:, 0:1],
                                            func=AF.Copy, scale=1.0 / D), SACT)
        msn = op("scalar", lambda e: e.activation(out=stat[:, 4:5], in_=stat[:, 1:2],
                                                  func=AF.Copy, scale=1.0 / D), SACT)
        w("vector", SACT, msn)
        vn = op("vector", lambda e: e.tensor_tensor(out=stat[:, 5:6], in0=stat[:, 4:5],
                                                    in1=stat[:, 2:3], op=ALU.subtract), SDVE)
        w("scalar", SDVE, vn)
        op("scalar", lambda e: e.activation(out=stat[:, 6:7], in_=stat[:, 5:6],
                                            func=AF.Sqrt, bias=epsT[:, 0:1]), SACT)
        w("scalar" if False else "vector", SACT, SACT.n)
        rcn = op("vector", lambda e: e.reciprocal(out=stat[:, 7:8], in_=stat[:, 6:7]), SDVE)
        w("scalar", SDVE, rcn)
        rbn = op("scalar", lambda e: e.activation(out=stat[:, 5:6], in_=stat[:, 7:8],
                                                  func=AF.Copy), SACT)
        w("vector", SACT, rbn)
        w("vector", SPE, misc["prev_xn_tr"])
        xn_done = op("vector", lambda e, sl=sl: e.tensor_scalar(
            out=xn_tm[:, :], in0=x_tm[sl][:, :], scalar1=stat[:, 3:4],
            scalar2=stat[:, 5:6], op0=ALU.subtract, op1=ALU.mult), SDVE)
        misc["prev_stat_done"] = xn_done
        x_slot_free[sl] = (SDVE, xn_done)
        misc["pending_xn"] = xn_done
        misc["tt"] = tt + 1

    def emit_T_tr(seq, kind, ti, cr):
        tt = misc["tt_b"]
        xn_done = misc["pending_xn"]
        bq = tt % 2
        b0, b1 = (4, 5) if bq == 0 else (6, 7)
        w("tensor", SDVE, xn_done)
        wait_banks("tensor", [b0, b1])
        idmat = cI if kind == "f" else cJ
        half = DT // 2
        for dt in range(DT):
            bank = b0 if dt < half else b1
            colp = (dt % half) * 128
            op("tensor", lambda e, bank=bank, colp=colp, dt=dt, idmat=idmat:
               e.transpose(ps[bank][:, colp:colp + 128],
                           xn_tm[:, dt * 128:(dt + 1) * 128], idmat[:, :]), SPE)
        tr_done = SPE.n
        misc["prev_xn_tr"] = tr_done
        dst_t = xn_fm if kind == "f" else xn_rev
        cti = ti if kind == "f" else (TI - 1 - ti)
        col = seq * C + cti * 128
        w("scalar", SPE, tr_done)
        ev1 = op("scalar", lambda e, b0=b0, dst_t=dst_t, col=col, half=half:
                 e.copy(out=dst_t[:, :].rearrange("p (dt n) -> p dt n", n=NPH)[:, 0:half, col:col + 128],
                        in_=ps[b0][:, 0:half * 128].rearrange("p (dt j) -> p dt j", j=128)),
                 SACT)
        ev2 = op("scalar", lambda e, b1=b1, dst_t=dst_t, col=col, half=half:
                 e.copy(out=dst_t[:, :].rearrange("p (dt n) -> p dt n", n=NPH)[:, half:DT, col:col + 128],
                        in_=ps[b1][:, 0:(DT - half) * 128].rearrange("p (dt j) -> p dt j", j=128)),
                 SACT)
        bank_free[b0] = (SACT, ev1)
        bank_free[b1] = (SACT, ev2)
        misc["tt_b"] = tt + 1
        cr[(kind, seq)] = SACT.n

    def make_T_emitters(phn):
        cr = {}
        first = dict(v=True)
        ems = []
        ch_f, ch_b = phn, PH - 1 - phn
        for seq in range(B):
            for kind, chunk in (("f", ch_f), ("b", ch_b)):
                for ti in range(TI):
                    ems.append(lambda seq=seq, kind=kind, chunk=chunk, ti=ti:
                               emit_T_ln(seq, kind, chunk, ti, cr, first))
                    ems.append(lambda seq=seq, kind=kind, ti=ti:
                               emit_T_tr(seq, kind, ti, cr))
        return ems, cr

    def stage_G(phn, chunk_ready):
        w("vector", SPE, misc["states_reader"])
        for pr in range(ET):
            pair_slab, pair_slot, pair_sm, pair_ld = load_slab(whgp_s, pr, big=False)
            for dir_, seq in ((0, 0), (1, 0), (0, 1), (1, 1)):
                qq = misc["n_groups"] % 4
                p_hid, p_gat = ps[2 * qq], ps[2 * qq + 1]
                src = xn_fm if dir_ == 0 else xn_rev
                w("tensor", pair_sm, pair_ld[-1])
                w("tensor", SACT, chunk_ready[("f" if dir_ == 0 else "b", seq)])
                wait_banks("tensor", [2 * qq, 2 * qq + 1])
                hg_done = None
                for which, pps in ((0, p_hid), (1, p_gat)):
                    for kt in range(DT):
                        is_last = (which == 1 and kt == DT - 1)
                        ko = which * DT + kt
                        hg_done = op("tensor", lambda e, pps=pps, slabb=pair_slab, ko=ko, kt=kt, src=src, seq=seq:
                                     e.matmul(pps[:, 0:C],
                                              slabb[:, ko * 128:(ko + 1) * 128],
                                              src[:, kt * NPH + seq * C:kt * NPH + seq * C + C],
                                              start=(kt == 0), stop=(kt == DT - 1)),
                                     SPE if is_last else None)
                if dir_ == 0:
                    misc["xnfm_reader"] = max(misc["xnfm_reader"], SPE.n)
                else:
                    misc["xnrev_reader"] = max(misc["xnrev_reader"], SPE.n)
                if (dir_, seq) == (1, 1):
                    slab_free[pair_slot[0]][pair_slot[1]] = (SPE, SPE.n)
                w("scalar", SPE, hg_done)
                w("scalar", SDVE, misc["scanbufs_free"])
                sgn = op("scalar", lambda e, p_hid=p_hid: e.activation(out=sgb[:, 0:C], in_=p_hid[:, 0:C],
                                                                       func=AF.Sigmoid), SACT)
                zbn = op("scalar", lambda e, p_gat=p_gat: e.activation(out=zb[:, 0:C], in_=p_gat[:, 0:C],
                                                                       func=AF.Sigmoid), SACT)
                cbn = op("scalar", lambda e, p_gat=p_gat: e.activation(out=cbuf[:, 0:C], in_=p_gat[:, 0:C],
                                                                       func=AF.Sigmoid, scale=-1.0), SACT)
                bank_free[2 * qq + 1] = (SACT, cbn)
                w("vector", SACT, sgn)
                tbn = op("vector", lambda e, p_hid=p_hid: e.scalar_tensor_tensor(
                    out=tb[:, 0:C], in0=p_hid[:, 0:C], scalar=0.5, in1=sgb[:, 0:C],
                    op0=ALU.add, op1=ALU.subtract), SDVE)
                bank_free[2 * qq] = (SDVE, tbn)
                op("vector", lambda e: e.scalar_tensor_tensor(
                    out=gbuf[:, 0:C], in0=tb[:, 0:C], scalar=0.0, in1=sgb[:, 0:C],
                    op0=ALU.max, op1=ALU.add))
                w("vector", SACT, zbn)
                op("vector", lambda e: e.tensor_tensor(out=vbuf[:, 0:C], in0=zb[:, 0:C],
                                                       in1=gbuf[:, 0:C], op=ALU.mult))
                fb = fbA if dir_ == 0 else fbB
                cc = (dir_ * B + seq) * ET + pr
                w("vector", SACT, cbn)
                scn = op("vector", lambda e, fb=fb, cc=cc: e.tensor_tensor_scan(
                    out=fb[:, 0:C], data0=cbuf[:, 0:C], data1=vbuf[:, 0:C],
                    initial=carry[:, cc:cc + 1], op0=ALU.mult, op1=ALU.add), SDVE)
                misc["scanbufs_free"] = scn
                if dir_ == 1:
                    op("vector", lambda e, pr=pr, seq=seq: e.tensor_tensor(
                        out=states[:, pr * NPH + seq * C:pr * NPH + seq * C + C],
                        in0=fbA[:, 0:C], in1=fbB[:, 0:C], op=ALU.add), SDVE)
                    ccf = (0 * B + seq) * ET + pr
                    op("vector", lambda e, ccf=ccf:
                       e.tensor_copy(out=carry[:, ccf:ccf + 1], in_=fbA[:, C - 1:C]))
                    op("vector", lambda e, cc=cc:
                       e.tensor_copy(out=carry[:, cc:cc + 1], in_=fbB[:, C - 1:C]))
                misc["n_groups"] += 1
        return SDVE.n

    def stage_F(phn, states_ready, pump):
        ch_f = phn
        for seq in range(B):
            for mt in range(HT):
                s12b, s12slot, s12sm, ld12 = load_slab(w12_s, mt, big=False)
                qq = mt % 2
                pw1, pw2 = ps[2 * qq], ps[2 * qq + 1]
                w("tensor", s12sm, ld12[-1])
                wait_banks("tensor", [2 * qq, 2 * qq + 1])
                w1_done = w2_done = None
                for which, pps in ((0, pw1), (1, pw2)):
                    for kt in range(DT):
                        ko = which * DT + kt
                        mk = op("tensor", lambda e, pps=pps, slabb=s12b, ko=ko, kt=kt, seq=seq:
                                e.matmul(pps[:, 0:C], slabb[:, ko * 128:(ko + 1) * 128],
                                         xn_fm[:, kt * NPH + seq * C:kt * NPH + seq * C + C],
                                         start=(kt == 0), stop=(kt == DT - 1)),
                                SPE if kt == DT - 1 else None)
                        if kt == DT - 1:
                            if which == 0:
                                w1_done = mk
                            else:
                                w2_done = mk
                misc["xnfm_reader"] = max(misc["xnfm_reader"], SPE.n)
                slab_free[s12slot[0]][s12slot[1]] = (SPE, SPE.n)
                w("scalar", SPE, w1_done)
                w("scalar", SDVE, misc["tsil_rd"])
                sil = op("scalar", lambda e, pw1=pw1:
                         e.activation(out=tsil[:, 0:C], in_=pw1[:, 0:C],
                                      func=AF.Silu), SACT)
                bank_free[2 * qq] = (SACT, sil)
                w("vector", SACT, sil)
                w("vector", SPE, max(w2_done, misc["Bh1_pe"]))
                ha = op("vector", lambda e, pw2=pw2, mt=mt: e.tensor_tensor(
                    out=Bh1[:, mt * C:(mt + 1) * C], in0=tsil[:, 0:C],
                    in1=pw2[:, 0:C], op=ALU.mult), SDVE)
                misc["tsil_rd"] = ha
                bank_free[2 * qq + 1] = (SDVE, ha)
            ha_ready = SDVE.n
            if dbg and phn == 0 and seq == 0:
                w("gpsimd", SDVE, ha_ready)
                op("gpsimd", lambda e: e.dma_start(out=d_ha, in_=Bh1[:, :]), SDBG, 16)

            for mt in range(HT):
                if seq == 1 and pump:
                    pump.pop(0)()
                hb, hslot, hsm, ldh = load_slab(h1_s, mt, big=True, split=2)
                qq = mt % 2
                pp = ps[qq]
                w("tensor", hsm, ldh[0])
                w("tensor", SDVE, ha_ready)
                wait_banks("tensor", [qq])
                mm_done = None
                for kt in range(HT):
                    if kt == HT // 2:
                        w("tensor", hsm, ldh[1])
                    mm_done = op("tensor", lambda e, pp=pp, hb=hb, kt=kt:
                                 e.matmul(pp[:, 0:C], hb[:, kt * 128:(kt + 1) * 128],
                                          Bh1[:, kt * C:(kt + 1) * C],
                                          start=(kt == 0), stop=(kt == HT - 1)),
                                 SPE if kt == HT - 1 else None)
                misc["Bh1_pe"] = max(misc["Bh1_pe"], SPE.n)
                slab_free[hslot[0]][hslot[1]] = (SPE, SPE.n)
                w("scalar", SPE, max(mm_done, misc["Bh2_pe"]))
                evn = op("scalar", lambda e, pp=pp, mt=mt:
                         e.activation(out=Bh2[:, mt * C:(mt + 1) * C], in_=pp[:, 0:C],
                                      func=AF.Gelu), SACT)
                bank_free[qq] = (SACT, evn)
            hb_ready = SACT.n

            h1_reader_spe = misc["Bh1_pe"]
            for mt in range(HT):
                if seq == 1 and pump:
                    pump.pop(0)()
                hb, hslot, hsm, ldh = load_slab(h2_s, mt, big=True, split=2)
                qq = mt % 2
                pp = ps[2 + qq]
                w("tensor", hsm, ldh[0])
                w("tensor", SACT, hb_ready)
                wait_banks("tensor", [2 + qq])
                mm_done = None
                for kt in range(HT):
                    if kt == HT // 2:
                        w("tensor", hsm, ldh[1])
                    mm_done = op("tensor", lambda e, pp=pp, hb=hb, kt=kt:
                                 e.matmul(pp[:, 0:C], hb[:, kt * 128:(kt + 1) * 128],
                                          Bh2[:, kt * C:(kt + 1) * C],
                                          start=(kt == 0), stop=(kt == HT - 1)),
                                 SPE if kt == HT - 1 else None)
                misc["Bh2_pe"] = max(misc["Bh2_pe"], SPE.n)
                slab_free[hslot[0]][hslot[1]] = (SPE, SPE.n)
                w("scalar", SPE, max(mm_done, h1_reader_spe))
                evn = op("scalar", lambda e, pp=pp, mt=mt:
                         e.activation(out=Bh1[:, mt * C:(mt + 1) * C], in_=pp[:, 0:C],
                                      func=AF.Gelu), SACT)
                bank_free[2 + qq] = (SACT, evn)
            hc_ready = SACT.n

            for mt in range(DT):
                while seq == 1 and pump:
                    pump.pop(0)()
                wob, woslot, wosm, ldo = load_slab(wout_s, mt, big=False)
                w3b, w3slot, w3sm, ld3 = load_slab(w3_s, mt, big=True, split=2)
                qq = mt % 2
                pp = ps[qq]
                w("tensor", wosm, ldo[-1])
                w("tensor", w3sm, ld3[0])
                w("tensor", SDVE, states_ready)
                w("tensor", SACT, hc_ready)
                wait_banks("tensor", [qq])
                mm_done = None
                for kt in range(ET):
                    op("tensor", lambda e, pp=pp, wob=wob, kt=kt, seq=seq:
                       e.matmul(pp[:, 0:C], wob[:, kt * 128:(kt + 1) * 128],
                                states[:, kt * NPH + seq * C:kt * NPH + seq * C + C],
                                start=(kt == 0), stop=False))
                for kt in range(HT):
                    if kt == HT // 2:
                        w("tensor", w3sm, ld3[1])
                    mm_done = op("tensor", lambda e, pp=pp, w3b=w3b, kt=kt:
                                 e.matmul(pp[:, 0:C], w3b[:, kt * 128:(kt + 1) * 128],
                                          Bh1[:, kt * C:(kt + 1) * C],
                                          start=False, stop=(kt == HT - 1)),
                                 SPE if kt == HT - 1 else None)
                misc["states_reader"] = max(misc["states_reader"], SPE.n)
                misc["Bh1_pe"] = max(misc["Bh1_pe"], SPE.n)
                slab_free[woslot[0]][woslot[1]] = (SPE, SPE.n)
                slab_free[w3slot[0]][w3slot[1]] = (SPE, SPE.n)
                w("scalar", SPE, max(mm_done, misc["acc_pe"]))
                evn = op("scalar", lambda e, pp=pp, mt=mt:
                         e.copy(out=acc[:, mt * C:(mt + 1) * C], in_=pp[:, 0:C]), SACT)
                bank_free[qq] = (SACT, evn)
            acc_ready = SACT.n
            if dbg and phn == 0 and seq == 0:
                w("gpsimd", SACT, acc_ready)
                op("gpsimd", lambda e: e.dma_start(out=d_acc, in_=acc[:, :]), SDBG, 16)

            for ti in range(TI):
                tok0 = ch_f * C + ti * 128
                bq = ti % 2
                b0, b1 = (4, 5) if bq == 0 else (6, 7)
                half = DT // 2
                w("tensor", SACT, acc_ready)
                wait_banks("tensor", [b0, b1])
                tr_done = None
                for dt in range(DT):
                    bank = b0 if dt < half else b1
                    colp = (dt % half) * 128
                    tr_done = op("tensor", lambda e, bank=bank, colp=colp, dt=dt, ti=ti:
                                 e.transpose(ps[bank][:, colp:colp + 128],
                                             acc[:, dt * C + ti * 128:dt * C + ti * 128 + 128],
                                             cI[:, :]),
                                 SPE if dt == DT - 1 else None)
                misc["acc_pe"] = max(misc["acc_pe"], SPE.n)
                xsl, xldn = load_x(seq, tok0)
                osl = ti % 2
                w("vector", SPE, tr_done)
                w("vector", SXT[xsl], xldn)
                w("vector", SO[osl], o_slot_free[osl])
                op("vector", lambda e, osl=osl, xsl=xsl, b0=b0, half=half: e.tensor_tensor(
                    out=obuf[osl][:, 0:half * 128], in0=x_tm[xsl][:, 0:half * 128],
                    in1=ps[b0][:, 0:half * 128], op=ALU.add))
                addn = op("vector", lambda e, osl=osl, xsl=xsl, b1=b1, half=half: e.tensor_tensor(
                    out=obuf[osl][:, half * 128:D], in0=x_tm[xsl][:, half * 128:D],
                    in1=ps[b1][:, 0:(DT - half) * 128], op=ALU.add), SDVE)
                bank_free[b0] = (SDVE, addn)
                bank_free[b1] = (SDVE, addn)
                x_slot_free[xsl] = (SDVE, addn)
                w("gpsimd", SDVE, addn)
                op("gpsimd", lambda e, osl=osl, seq=seq, tok0=tok0:
                   e.dma_start(out=yv[seq * T + tok0:seq * T + tok0 + 128, :],
                               in_=obuf[osl][:, :]),
                   SO[osl], 16)
                o_slot_free[osl] = SO[osl].n

    ems, cur_cr = make_T_emitters(0)
    for em in ems:
        em()
    if dbg:
        w("gpsimd", SACT, SACT.n)
        op("gpsimd", lambda e: e.dma_start(out=d_xnfm, in_=xn_fm[:, :]), SDBG, 16)
        op("gpsimd", lambda e: e.dma_start(out=d_xnrev, in_=xn_rev[:, :]), SDBG, 16)
    for phn in range(PH):
        states_ready = stage_G(phn, cur_cr)
        if dbg and phn == 0:
            w("gpsimd", SDVE, states_ready)
            op("gpsimd", lambda e: e.dma_start(out=d_states, in_=states[:, :]), SDBG, 16)
        if phn + 1 < PH:
            nems, ncr = make_T_emitters(phn + 1)
        else:
            nems, ncr = [], None
        stage_F(phn, states_ready, nems)
        for em in nems:
            em()
        cur_cr = ncr

    w("gpsimd", SO[0], SO[0].n)
    w("gpsimd", SO[1], SO[1].n)
    if dbg:
        w("gpsimd", SDBG, SDBG.n)

    with nc.Block() as block:
        @block.sync
        def _(e):
            for f in Q["sync"]:
                f(e)

        @block.scalar
        def _(e):
            for f in Q["scalar"]:
                f(e)

        @block.vector
        def _(e):
            for f in Q["vector"]:
                f(e)

        @block.tensor
        def _(e):
            for f in Q["tensor"]:
                f(e)

        @block.gpsimd
        def _(e):
            for f in Q["gpsimd"]:
                f(e)

    stk.close()
    return nc


def _layout_scale(s, MT):
    s = np.asarray(s, dtype=np.float32)
    return np.ascontiguousarray(s.reshape(MT, 128).T)


def kernel(L0_attn_x, L0_attn_W_hg, L0_attn_W_out, L0_ffn_w1_w, L0_ffn_w1_s,
           L0_ffn_w2_w, L0_ffn_w2_s, L0_ffn_h1_w, L0_ffn_h1_s, L0_ffn_h2_w,
           L0_ffn_h2_s, L0_ffn_w3_w, L0_ffn_w3_s):
    cfg = CFG
    B, T, D, E, H = cfg["B"], cfg["T"], cfg["D"], cfg["E"], cfg["H"]
    DT, HT = D // 128, H // 128

    x = np.ascontiguousarray(np.asarray(L0_attn_x, dtype=np.float32))
    w_hg = np.ascontiguousarray(np.asarray(L0_attn_W_hg, dtype=np.float32))
    w_out = np.ascontiguousarray(np.asarray(L0_attn_W_out, dtype=np.float32))
    w1 = np.ascontiguousarray(np.asarray(L0_ffn_w1_w, dtype=np.float32))
    w2 = np.ascontiguousarray(np.asarray(L0_ffn_w2_w, dtype=np.float32))
    h1 = np.ascontiguousarray(np.asarray(L0_ffn_h1_w, dtype=np.float32))
    h2 = np.ascontiguousarray(np.asarray(L0_ffn_h2_w, dtype=np.float32))
    w3 = np.ascontiguousarray(np.asarray(L0_ffn_w3_w, dtype=np.float32))
    s1 = _layout_scale(L0_ffn_w1_s, HT)
    s2 = _layout_scale(L0_ffn_w2_s, HT)
    h1s = _layout_scale(L0_ffn_h1_s, HT)
    h2s = _layout_scale(L0_ffn_h2_s, HT)
    w3s = _layout_scale(L0_ffn_w3_s, DT)
    cIm = np.eye(128, dtype=np.float32)
    cJm = np.ascontiguousarray(cIm[::-1, :])

    import gc
    nc = _build(cfg)
    gc.collect()
    in_maps = []
    for c in range(N_CORES):
        in_maps.append(dict(
            x=np.ascontiguousarray(x[c * B:(c + 1) * B]),
            w_hg=w_hg, w_out=w_out, w1=w1, w2=w2, h1=h1, h2=h2, w3=w3,
            s1=s1, s2=s2, h1s=h1s, h2s=h2s, w3s=w3s, cI=cIm, cJ=cJm,
        ))
    res = run_bass_kernel_spmd(nc, in_maps, core_ids=list(range(N_CORES)))
    return np.concatenate([r["y"] for r in res.results], axis=0)


# version 09
# speedup vs baseline: 1.3869x; 1.0607x over previous
import numpy as np
import concourse.bass as bass
import concourse.mybir as mybir
from concourse.bass_utils import run_bass_kernel_spmd

F32 = mybir.dt.float32
BF16 = mybir.dt.bfloat16
AF = mybir.ActivationFunctionType
ALU = mybir.AluOpType
AX = mybir.AxisListType

CFG = dict(B=2, T=4096, D=1024, E=1536, H=4096, C=512)
N_CORES = 8


class _S:
    def __init__(self, h):
        self.h = h
        self.n = 0


def _build(cfg):
    B, T, D, E, H, C = cfg["B"], cfg["T"], cfg["D"], cfg["E"], cfg["H"], cfg["C"]
    DT, ET, HT = D // 128, E // 128, H // 128
    PH = T // C
    TI = C // 128
    NPH = B * C
    assert C <= 512 and C % 128 == 0

    nc = bass.Bass("TRN2", target_bir_lowering=False, debug=False,
                   detect_race_conditions=False)

    xd = nc.dram_tensor("x", [B, T, D], F32, kind="ExternalInput").ap()
    yd = nc.dram_tensor("y", [B, T, D], F32, kind="ExternalOutput").ap()
    xv = xd.rearrange("b t d -> (b t) d")
    yv = yd.rearrange("b t d -> (b t) d")

    w_hg = nc.dram_tensor("w_hg", [2 * E, D], F32, kind="ExternalInput").ap()
    w_out = nc.dram_tensor("w_out", [D, E], F32, kind="ExternalInput").ap()
    w1 = nc.dram_tensor("w1", [H, D], F32, kind="ExternalInput").ap()
    w2 = nc.dram_tensor("w2", [H, D], F32, kind="ExternalInput").ap()
    h1 = nc.dram_tensor("h1", [H, H], F32, kind="ExternalInput").ap()
    h2 = nc.dram_tensor("h2", [H, H], F32, kind="ExternalInput").ap()
    w3 = nc.dram_tensor("w3", [D, H], F32, kind="ExternalInput").ap()
    s1d = nc.dram_tensor("s1", [128, HT], F32, kind="ExternalInput").ap()
    s2d = nc.dram_tensor("s2", [128, HT], F32, kind="ExternalInput").ap()
    h1sd = nc.dram_tensor("h1s", [128, HT], F32, kind="ExternalInput").ap()
    h2sd = nc.dram_tensor("h2s", [128, HT], F32, kind="ExternalInput").ap()
    w3sd = nc.dram_tensor("w3s", [128, DT], F32, kind="ExternalInput").ap()
    cId = nc.dram_tensor("cI", [128, 128], F32, kind="ExternalInput").ap()
    cJd = nc.dram_tensor("cJ", [128, 128], F32, kind="ExternalInput").ap()

    dbg = cfg.get("dbg", False)
    if dbg:
        d_xnfm = nc.dram_tensor("d_xnfm", [128, DT * NPH], BF16, kind="ExternalOutput").ap()
        d_xnrev = nc.dram_tensor("d_xnrev", [128, DT * NPH], BF16, kind="ExternalOutput").ap()
        d_states = nc.dram_tensor("d_states", [128, ET * NPH], BF16, kind="ExternalOutput").ap()
        d_ha = nc.dram_tensor("d_ha", [128, HT * C], BF16, kind="ExternalOutput").ap()
        d_acc = nc.dram_tensor("d_acc", [128, DT * C], F32, kind="ExternalOutput").ap()

    def scratch(name, MT, KT):
        t = nc.dram_tensor(name, [MT, 128, KT * 128], BF16, kind="Internal").ap()
        return (t, MT, KT)
    whgp_s = scratch("whgp_s", ET, 2 * DT)
    w12_s = scratch("w12_s", HT, 2 * DT)
    wout_s = scratch("wout_s", DT, ET)
    h1_s = scratch("h1_s", HT, HT)
    h2_s = scratch("h2_s", HT, HT)
    w3_s = scratch("w3_s", DT, HT)

    from contextlib import ExitStack
    stk = ExitStack()
    sb = lambda name, shape, dt=F32: stk.enter_context(nc.sbuf_tensor(name, shape, dt))

    cI = sb("cIt", [128, 128]); cJ = sb("cJt", [128, 128])
    s1t = sb("s1t", [128, HT]); s2t = sb("s2t", [128, HT])
    h1st = sb("h1st", [128, HT]); h2st = sb("h2st", [128, HT])
    w3st = sb("w3st", [128, DT])
    epsT = sb("epsT", [128, 1])
    x_tm = [sb(f"x_tm{i}", [128, D]) for i in range(3)]
    diagb = [sb(f"diagb{i}", [128, 128], BF16) for i in range(2)]
    cIb = sb("cIb", [128, 128], BF16)
    xn_tm = sb("xn_tm", [128, D])
    stat = sb("stat", [128, 8])
    xn_fm = sb("xn_fm", [128, DT * NPH], BF16)
    xn_rev = sb("xn_rev", [128, DT * NPH], BF16)
    states = sb("states", [128, ET * NPH], BF16)
    carry = sb("carry", [128, 4 * ET])
    zb = sb("zb", [128, C]); cbuf = sb("cbuf", [128, C]); sgb = sb("sgb", [128, C])
    tb = sb("tb", [128, C]); gbuf = sb("gbuf", [128, C])
    vbuf = sgb
    fbA = sb("fbA", [128, C]); fbB = sb("fbB", [128, C])
    Bh1 = sb("Bh1", [128, HT * C], BF16)
    Bh2 = sb("Bh2", [128, HT * C], BF16)
    acc = sb("acc", [128, DT * C])
    obuf = [sb(f"obuf{i}", [128, D]) for i in range(2)]
    slab_big = [sb(f"slabB{i}", [128, HT * 128], BF16) for i in range(2)]
    slab_sml = [sb(f"slabS{i}", [128, max(2 * DT, ET) * 128], BF16) for i in range(3)]
    tsil = sb("tsil", [128, C])

    ps = [stk.enter_context(nc.psum_tensor(f"ps{i}", [128, 512], F32)) for i in range(8)]

    mksem = lambda name: _S(stk.enter_context(nc.semaphore(name)))
    SXC = mksem("SXC")
    SXT = [mksem(f"SXT{i}") for i in range(4)]
    SN = [mksem(f"SN{i}") for i in range(4)]
    SC = [mksem(f"SC{i}") for i in range(2)]
    SBG = [mksem(f"SBG{i}") for i in range(3)]
    SSL = [mksem(f"SSL{i}") for i in range(3)]
    SO = [mksem(f"SO{i}") for i in range(2)]
    SPE = mksem("SPE")
    SACT = mksem("SACT")
    SDVE = mksem("SDVE")
    SDBG = mksem("SDBG")

    Q = dict(sync=[], scalar=[], vector=[], tensor=[], gpsimd=[])
    _last_wait = {}

    def w(eng, sem, val):
        if val <= 0:
            return
        key = (eng, id(sem))
        if _last_wait.get(key, 0) >= val:
            return
        _last_wait[key] = val
        Q[eng].append(lambda e, s=sem, v=val: e.wait_ge(s.h, v))

    def op(eng, f, inc=None, n=1):
        if inc is not None:
            inc.n += n
            Q[eng].append(lambda e, f=f, s=inc, n=n: f(e).then_inc(s.h, n))
            return inc.n
        Q[eng].append(lambda e, f=f: f(e))
        return None

    bank_free = {i: (SPE, 0) for i in range(8)}

    def wait_banks(eng, banks):
        for b in banks:
            sm, v = bank_free[b]
            w(eng, sm, v)

    for dst, src in ((cI, cId), (cJ, cJd), (s1t, s1d), (s2t, s2d),
                     (h1st, h1sd), (h2st, h2sd), (w3st, w3sd)):
        op("sync", lambda e, d=dst, s=src: e.dma_start(out=d[:, :], in_=s), SXC, 16)
    setup_loads = SXC.n
    op("vector", lambda e: e.memset(carry[:, :], 0.0), SDVE)
    w("vector", SXC, SXC.n)
    op("vector", lambda e: e.tensor_copy(out=cIb[:, :], in_=cI[:, :]), SDVE)
    op("vector", lambda e: e.memset(epsT[:, :], 1e-5), SDVE)

    maxK = max(D, E, H)
    if HT * C // 2 >= 2 * maxK and 2 * DT * C >= 2 * maxK:
        natw = [Bh1[:].bitcast(F32)[:, 0:maxK], Bh1[:].bitcast(F32)[:, maxK:2 * maxK]]
        natwb = [Bh2[:, i * maxK:(i + 1) * maxK] for i in range(4)]
        staging = [acc[:].bitcast(BF16)[:, 0:maxK], acc[:].bitcast(BF16)[:, maxK:2 * maxK]]
    else:
        natw_t = sb("natw_t", [128, 2 * maxK])
        natwb_t = sb("natwb_t", [128, 4 * maxK], BF16)
        staging_t = sb("staging_t", [128, 2 * maxK], BF16)
        natw = [natw_t[:, 0:maxK], natw_t[:, maxK:2 * maxK]]
        natwb = [natwb_t[:, i * maxK:(i + 1) * maxK] for i in range(4)]
        staging = [staging_t[:, 0:maxK], staging_t[:, maxK:2 * maxK]]
    rp = dict(pe_done=[0, 0, 0, 0], cast_done=[0, 0], sc_done=[0, 0],
              diag_free=[0, 0], i=0)

    w("scalar", SXC, setup_loads)
    w("tensor", SXC, setup_loads)
    w("vector", SXC, setup_loads)

    def repack(wdram, scr3, KTm, scale_t, row0=0, col_off=0):
        scr, MT, KTtot = scr3
        K = KTm * 128
        for mt in range(MT):
            it = rp["i"]
            rp["i"] += 1
            slot = it % 2
            bslot = it % 4
            stg = it % 2
            w("sync", SACT, rp["cast_done"][slot])
            w("sync", SC[stg], rp["sc_done"][stg])
            op("sync", lambda e, sl=slot, mt=mt, K=K, wd=wdram, row0=row0:
               e.dma_start(out=natw[sl][:, 0:K],
                           in_=wd[row0 + mt * 128:row0 + (mt + 1) * 128, 0:K]),
               SN[slot], 16)
            ldn = SN[slot].n
            w("scalar", SN[slot], ldn)
            w("scalar", SPE, rp["pe_done"][bslot])
            cst = op("scalar", lambda e, sl=slot, bs=bslot, K=K:
                     e.copy(out=natwb[bs][:, 0:K], in_=natw[sl][:, 0:K]), SACT)
            rp["cast_done"][slot] = cst
            if scale_t is not None:
                dq = it % 2
                w("vector", SPE, rp["diag_free"][dq])
                dgn = op("vector", lambda e, dq=dq, st_=scale_t, mt=mt:
                         e.tensor_scalar(out=diagb[dq][:, :], in0=cI[:, :],
                                         scalar1=st_[:, mt:mt + 1], scalar2=None,
                                         op0=ALU.mult), SDVE)
                w("tensor", SDVE, dgn)
                idop = diagb[dq]
            else:
                idop = cIb
            w("tensor", SACT, cst)
            srcb = natwb[bslot]
            evs = []
            for kt in range(KTm):
                bq = (kt // 4) % 2
                if kt % 4 == 0:
                    wait_banks("tensor", [bq])
                last = op("tensor", lambda e, kt=kt, bq=bq, srcb=srcb, idop=idop:
                          e.matmul(ps[bq][:, (kt % 4) * 128:(kt % 4) * 128 + 128],
                                   srcb[:, kt * 128:(kt + 1) * 128], idop[:, :],
                                   start=True, stop=True),
                          SPE)
                if kt % 4 == 3 or kt == KTm - 1:
                    nt = (kt % 4) + 1
                    w("scalar", SPE, last)
                    if kt < 4:
                        w("scalar", SC[stg], rp["sc_done"][stg])
                    evn = op("scalar", lambda e, sl=stg, kt=kt, bq=bq, nt=nt:
                             e.copy(out=staging[sl][:, (kt - nt + 1) * 128:(kt + 1) * 128],
                                    in_=ps[bq][:, 0:nt * 128]), SACT)
                    bank_free[bq] = (SACT, evn)
                    evs.append(evn)
            rp["pe_done"][bslot] = SPE.n
            if scale_t is not None:
                rp["diag_free"][it % 2] = SPE.n
            w("gpsimd", SACT, evs[-1])
            op("gpsimd", lambda e, sl=stg, mt=mt, K=K, scr=scr, co=col_off * 128:
               e.dma_start(out=scr[mt:mt + 1, :, co:co + K].rearrange("m p j -> (m p) j"),
                           in_=staging[sl][:, 0:K]),
               SC[stg], 16)
            rp["sc_done"][stg] = SC[stg].n

    repack(w_hg, whgp_s, DT, None, row0=0, col_off=0)
    repack(w_hg, whgp_s, DT, None, row0=E, col_off=DT)
    w("sync", SC[0], SC[0].n)
    w("sync", SC[1], SC[1].n)

    def repack_rest():
        repack(w1, w12_s, DT, s1t, col_off=0)
        repack(w2, w12_s, DT, s2t, col_off=DT)
        repack(w_out, wout_s, ET, None)
        repack(w3, w3_s, HT, w3st)
        repack(h1, h1_s, HT, h1st)
        repack(h2, h2_s, HT, h2st)
        w("sync", SC[0], SC[0].n)
        w("sync", SC[1], SC[1].n)

    slab_free = {"big": [(SPE, 0)] * 3, "sml": [(SPE, 0)] * 3}
    slab_idx = {"big": 0, "sml": 0}

    def load_slab(scr3, mt, big, split=1):
        scr, MT, KT = scr3
        kind = "big" if big else "sml"
        pool = slab_big if big else slab_sml
        sgrp = SBG if big else SSL
        sl = slab_idx[kind] % len(pool)
        slab_idx[kind] += 1
        sm, v = slab_free[kind][sl]
        w("sync", sm, v)
        marks = []
        step = KT // split
        for s in range(split):
            c0 = s * step * 128
            c1 = (s + 1) * step * 128 if s < split - 1 else KT * 128
            op("sync", lambda e, sl=sl, mt=mt, scr=scr, pool=pool, c0=c0, c1=c1:
               e.dma_start(out=pool[sl][:, c0:c1],
                           in_=scr[mt:mt + 1, :, c0:c1].rearrange("m p j -> (m p) j")),
               sgrp[sl], 16)
            marks.append(sgrp[sl].n)
        return pool[sl], (kind, sl), sgrp[sl], marks

    x_slot_free = [(SDVE, 0)] * 4
    o_slot_free = [0, 0]
    xnfm_reader = 0
    xnrev_reader = 0
    states_reader = 0
    Bh1_pe_reader = 0
    Bh2_pe_reader = 0
    acc_pe_reader = 0
    tsil_reader = [0, 0]
    prev_scanbufs_free = 0
    prev_stat_done = 0
    prev_xn_tr = 0
    n_groups = 0
    x_rr = dict(i=0)

    def load_x(seq, tok0):
        sl = x_rr["i"] % len(x_tm)
        x_rr["i"] += 1
        sm, v = x_slot_free[sl]
        w("sync", sm, v)
        op("sync", lambda e, sl=sl, seq=seq, tok0=tok0:
           e.dma_start(out=x_tm[sl][:, :],
                       in_=xv[seq * T + tok0:seq * T + tok0 + 128, :]),
           SXT[sl], 16)
        return sl, SXT[sl].n

    misc = dict(prev_stat_done=0, prev_xn_tr=0, xnfm_reader=0, xnrev_reader=0,
                states_reader=0, Bh1_pe=0, Bh2_pe=0, acc_pe=0, tsil_rd=0,
                scanbufs_free=0, tt=0, tt_b=0, pending_xn=0, n_groups=0)

    def emit_T_ln(seq, kind, chunk, ti, cr, first):
        if first["v"]:
            w("scalar", SPE, max(misc["xnfm_reader"], misc["xnrev_reader"]))
            first["v"] = False
        tok0 = chunk * C + ti * 128
        sl, ldn = load_x(seq, tok0)
        tt = misc["tt"]
        w("vector", SXT[sl], ldn)
        op("vector", lambda e, sl=sl:
           e.reduce_sum(stat[:, 0:1], x_tm[sl][:, :], axis=AX.X), SDVE)
        osl = tt % 2
        w("scalar", SXT[sl], ldn)
        w("scalar", SO[osl], o_slot_free[osl])
        w("scalar", SDVE, misc["prev_stat_done"])
        sqn = op("scalar", lambda e, sl=sl, osl=osl:
                 e.activation(out=obuf[osl][:, :], in_=x_tm[sl][:, :],
                              func=AF.Square), SACT)
        w("vector", SACT, sqn)
        s1n = op("vector", lambda e, osl=osl:
                 e.reduce_sum(stat[:, 1:2], obuf[osl][:, :], axis=AX.X), SDVE)
        w("scalar", SDVE, s1n)
        op("scalar", lambda e: e.activation(out=stat[:, 2:3], in_=stat[:, 0:1],
                                            func=AF.Square, scale=1.0 / D), SACT)
        op("scalar", lambda e: e.activation(out=stat[:, 3:4], in_=stat[:, 0:1],
                                            func=AF.Copy, scale=1.0 / D), SACT)
        msn = op("scalar", lambda e: e.activation(out=stat[:, 4:5], in_=stat[:, 1:2],
                                                  func=AF.Copy, scale=1.0 / D), SACT)
        w("vector", SACT, msn)
        vn = op("vector", lambda e: e.tensor_tensor(out=stat[:, 5:6], in0=stat[:, 4:5],
                                                    in1=stat[:, 2:3], op=ALU.subtract), SDVE)
        w("scalar", SDVE, vn)
        op("scalar", lambda e: e.activation(out=stat[:, 6:7], in_=stat[:, 5:6],
                                            func=AF.Sqrt, bias=epsT[:, 0:1]), SACT)
        w("scalar" if False else "vector", SACT, SACT.n)
        rcn = op("vector", lambda e: e.reciprocal(out=stat[:, 7:8], in_=stat[:, 6:7]), SDVE)
        w("scalar", SDVE, rcn)
        rbn = op("scalar", lambda e: e.activation(out=stat[:, 5:6], in_=stat[:, 7:8],
                                                  func=AF.Copy), SACT)
        w("vector", SACT, rbn)
        w("vector", SPE, misc["prev_xn_tr"])
        xn_done = op("vector", lambda e, sl=sl: e.tensor_scalar(
            out=xn_tm[:, :], in0=x_tm[sl][:, :], scalar1=stat[:, 3:4],
            scalar2=stat[:, 5:6], op0=ALU.subtract, op1=ALU.mult), SDVE)
        misc["prev_stat_done"] = xn_done
        x_slot_free[sl] = (SDVE, xn_done)
        misc["pending_xn"] = xn_done
        misc["tt"] = tt + 1

    def emit_T_tr(seq, kind, ti, cr):
        tt = misc["tt_b"]
        xn_done = misc["pending_xn"]
        bq = tt % 2
        b0, b1 = (4, 5) if bq == 0 else (6, 7)
        w("tensor", SDVE, xn_done)
        wait_banks("tensor", [b0, b1])
        idmat = cI if kind == "f" else cJ
        half = DT // 2
        for dt in range(DT):
            bank = b0 if dt < half else b1
            colp = (dt % half) * 128
            op("tensor", lambda e, bank=bank, colp=colp, dt=dt, idmat=idmat:
               e.transpose(ps[bank][:, colp:colp + 128],
                           xn_tm[:, dt * 128:(dt + 1) * 128], idmat[:, :]), SPE)
        tr_done = SPE.n
        misc["prev_xn_tr"] = tr_done
        dst_t = xn_fm if kind == "f" else xn_rev
        cti = ti if kind == "f" else (TI - 1 - ti)
        col = seq * C + cti * 128
        w("scalar", SPE, tr_done)
        ev1 = op("scalar", lambda e, b0=b0, dst_t=dst_t, col=col, half=half:
                 e.copy(out=dst_t[:, :].rearrange("p (dt n) -> p dt n", n=NPH)[:, 0:half, col:col + 128],
                        in_=ps[b0][:, 0:half * 128].rearrange("p (dt j) -> p dt j", j=128)),
                 SACT)
        ev2 = op("scalar", lambda e, b1=b1, dst_t=dst_t, col=col, half=half:
                 e.copy(out=dst_t[:, :].rearrange("p (dt n) -> p dt n", n=NPH)[:, half:DT, col:col + 128],
                        in_=ps[b1][:, 0:(DT - half) * 128].rearrange("p (dt j) -> p dt j", j=128)),
                 SACT)
        bank_free[b0] = (SACT, ev1)
        bank_free[b1] = (SACT, ev2)
        misc["tt_b"] = tt + 1
        cr[(kind, seq)] = SACT.n

    def make_T_emitters(phn):
        cr = {}
        first = dict(v=True)
        ems = []
        ch_f, ch_b = phn, PH - 1 - phn
        for seq in range(B):
            for kind, chunk in (("f", ch_f), ("b", ch_b)):
                for ti in range(TI):
                    ems.append(lambda seq=seq, kind=kind, chunk=chunk, ti=ti:
                               emit_T_ln(seq, kind, chunk, ti, cr, first))
                    ems.append(lambda seq=seq, kind=kind, ti=ti:
                               emit_T_tr(seq, kind, ti, cr))
        return ems, cr

    def stage_G(phn, chunk_ready):
        w("vector", SPE, misc["states_reader"])
        for pr in range(ET):
            pair_slab, pair_slot, pair_sm, pair_ld = load_slab(whgp_s, pr, big=False)
            for dir_, seq in ((0, 0), (1, 0), (0, 1), (1, 1)):
                qq = misc["n_groups"] % 4
                p_hid, p_gat = ps[2 * qq], ps[2 * qq + 1]
                src = xn_fm if dir_ == 0 else xn_rev
                w("tensor", pair_sm, pair_ld[-1])
                w("tensor", SACT, chunk_ready[("f" if dir_ == 0 else "b", seq)])
                wait_banks("tensor", [2 * qq, 2 * qq + 1])
                hg_done = None
                for which, pps in ((0, p_hid), (1, p_gat)):
                    for kt in range(DT):
                        is_last = (which == 1 and kt == DT - 1)
                        ko = which * DT + kt
                        hg_done = op("tensor", lambda e, pps=pps, slabb=pair_slab, ko=ko, kt=kt, src=src, seq=seq:
                                     e.matmul(pps[:, 0:C],
                                              slabb[:, ko * 128:(ko + 1) * 128],
                                              src[:, kt * NPH + seq * C:kt * NPH + seq * C + C],
                                              start=(kt == 0), stop=(kt == DT - 1)),
                                     SPE if is_last else None)
                if dir_ == 0:
                    misc["xnfm_reader"] = max(misc["xnfm_reader"], SPE.n)
                else:
                    misc["xnrev_reader"] = max(misc["xnrev_reader"], SPE.n)
                if (dir_, seq) == (1, 1):
                    slab_free[pair_slot[0]][pair_slot[1]] = (SPE, SPE.n)
                w("scalar", SPE, hg_done)
                w("scalar", SDVE, misc["scanbufs_free"])
                sgn = op("scalar", lambda e, p_hid=p_hid: e.activation(out=sgb[:, 0:C], in_=p_hid[:, 0:C],
                                                                       func=AF.Sigmoid), SACT)
                zbn = op("scalar", lambda e, p_gat=p_gat: e.activation(out=zb[:, 0:C], in_=p_gat[:, 0:C],
                                                                       func=AF.Sigmoid), SACT)
                cbn = op("scalar", lambda e, p_gat=p_gat: e.activation(out=cbuf[:, 0:C], in_=p_gat[:, 0:C],
                                                                       func=AF.Sigmoid, scale=-1.0), SACT)
                bank_free[2 * qq + 1] = (SACT, cbn)
                w("vector", SACT, sgn)
                tbn = op("vector", lambda e, p_hid=p_hid: e.scalar_tensor_tensor(
                    out=tb[:, 0:C], in0=p_hid[:, 0:C], scalar=0.5, in1=sgb[:, 0:C],
                    op0=ALU.add, op1=ALU.subtract), SDVE)
                bank_free[2 * qq] = (SDVE, tbn)
                op("vector", lambda e: e.scalar_tensor_tensor(
                    out=gbuf[:, 0:C], in0=tb[:, 0:C], scalar=0.0, in1=sgb[:, 0:C],
                    op0=ALU.max, op1=ALU.add))
                w("vector", SACT, zbn)
                op("vector", lambda e: e.tensor_tensor(out=vbuf[:, 0:C], in0=zb[:, 0:C],
                                                       in1=gbuf[:, 0:C], op=ALU.mult))
                fb = fbA if dir_ == 0 else fbB
                cc = (dir_ * B + seq) * ET + pr
                w("vector", SACT, cbn)
                scn = op("vector", lambda e, fb=fb, cc=cc: e.tensor_tensor_scan(
                    out=fb[:, 0:C], data0=cbuf[:, 0:C], data1=vbuf[:, 0:C],
                    initial=carry[:, cc:cc + 1], op0=ALU.mult, op1=ALU.add), SDVE)
                misc["scanbufs_free"] = scn
                if dir_ == 1:
                    op("vector", lambda e, pr=pr, seq=seq: e.tensor_tensor(
                        out=states[:, pr * NPH + seq * C:pr * NPH + seq * C + C],
                        in0=fbA[:, 0:C], in1=fbB[:, 0:C], op=ALU.add), SDVE)
                    ccf = (0 * B + seq) * ET + pr
                    op("vector", lambda e, ccf=ccf:
                       e.tensor_copy(out=carry[:, ccf:ccf + 1], in_=fbA[:, C - 1:C]))
                    op("vector", lambda e, cc=cc:
                       e.tensor_copy(out=carry[:, cc:cc + 1], in_=fbB[:, C - 1:C]))
                misc["n_groups"] += 1
        return SDVE.n

    def stage_F(phn, states_ready, pump):
        ch_f = phn
        for seq in range(B):
            for mt in range(HT):
                s12b, s12slot, s12sm, ld12 = load_slab(w12_s, mt, big=False)
                qq = mt % 2
                pw1, pw2 = ps[2 * qq], ps[2 * qq + 1]
                w("tensor", s12sm, ld12[-1])
                wait_banks("tensor", [2 * qq, 2 * qq + 1])
                w1_done = w2_done = None
                for which, pps in ((0, pw1), (1, pw2)):
                    for kt in range(DT):
                        ko = which * DT + kt
                        mk = op("tensor", lambda e, pps=pps, slabb=s12b, ko=ko, kt=kt, seq=seq:
                                e.matmul(pps[:, 0:C], slabb[:, ko * 128:(ko + 1) * 128],
                                         xn_fm[:, kt * NPH + seq * C:kt * NPH + seq * C + C],
                                         start=(kt == 0), stop=(kt == DT - 1)),
                                SPE if kt == DT - 1 else None)
                        if kt == DT - 1:
                            if which == 0:
                                w1_done = mk
                            else:
                                w2_done = mk
                misc["xnfm_reader"] = max(misc["xnfm_reader"], SPE.n)
                slab_free[s12slot[0]][s12slot[1]] = (SPE, SPE.n)
                w("scalar", SPE, w1_done)
                w("scalar", SDVE, misc["tsil_rd"])
                sil = op("scalar", lambda e, pw1=pw1:
                         e.activation(out=tsil[:, 0:C], in_=pw1[:, 0:C],
                                      func=AF.Silu), SACT)
                bank_free[2 * qq] = (SACT, sil)
                w("vector", SACT, sil)
                w("vector", SPE, max(w2_done, misc["Bh1_pe"]))
                ha = op("vector", lambda e, pw2=pw2, mt=mt: e.tensor_tensor(
                    out=Bh1[:, mt * C:(mt + 1) * C], in0=tsil[:, 0:C],
                    in1=pw2[:, 0:C], op=ALU.mult), SDVE)
                misc["tsil_rd"] = ha
                bank_free[2 * qq + 1] = (SDVE, ha)
            ha_ready = SDVE.n
            if dbg and phn == 0 and seq == 0:
                w("gpsimd", SDVE, ha_ready)
                op("gpsimd", lambda e: e.dma_start(out=d_ha, in_=Bh1[:, :]), SDBG, 16)

            for mt in range(HT):
                if seq == 1 and pump:
                    pump.pop(0)()
                hb, hslot, hsm, ldh = load_slab(h1_s, mt, big=True, split=2)
                qq = mt % 2
                pp = ps[qq]
                w("tensor", hsm, ldh[0])
                w("tensor", SDVE, ha_ready)
                wait_banks("tensor", [qq])
                mm_done = None
                for kt in range(HT):
                    if kt == HT // 2:
                        w("tensor", hsm, ldh[1])
                    mm_done = op("tensor", lambda e, pp=pp, hb=hb, kt=kt:
                                 e.matmul(pp[:, 0:C], hb[:, kt * 128:(kt + 1) * 128],
                                          Bh1[:, kt * C:(kt + 1) * C],
                                          start=(kt == 0), stop=(kt == HT - 1)),
                                 SPE if kt == HT - 1 else None)
                misc["Bh1_pe"] = max(misc["Bh1_pe"], SPE.n)
                slab_free[hslot[0]][hslot[1]] = (SPE, SPE.n)
                w("scalar", SPE, max(mm_done, misc["Bh2_pe"]))
                evn = op("scalar", lambda e, pp=pp, mt=mt:
                         e.activation(out=Bh2[:, mt * C:(mt + 1) * C], in_=pp[:, 0:C],
                                      func=AF.Gelu), SACT)
                bank_free[qq] = (SACT, evn)
            hb_ready = SACT.n

            h1_reader_spe = misc["Bh1_pe"]
            for mt in range(HT):
                if seq == 1 and pump:
                    pump.pop(0)()
                hb, hslot, hsm, ldh = load_slab(h2_s, mt, big=True, split=2)
                qq = mt % 2
                pp = ps[2 + qq]
                w("tensor", hsm, ldh[0])
                w("tensor", SACT, hb_ready)
                wait_banks("tensor", [2 + qq])
                mm_done = None
                for kt in range(HT):
                    if kt == HT // 2:
                        w("tensor", hsm, ldh[1])
                    mm_done = op("tensor", lambda e, pp=pp, hb=hb, kt=kt:
                                 e.matmul(pp[:, 0:C], hb[:, kt * 128:(kt + 1) * 128],
                                          Bh2[:, kt * C:(kt + 1) * C],
                                          start=(kt == 0), stop=(kt == HT - 1)),
                                 SPE if kt == HT - 1 else None)
                misc["Bh2_pe"] = max(misc["Bh2_pe"], SPE.n)
                slab_free[hslot[0]][hslot[1]] = (SPE, SPE.n)
                w("scalar", SPE, max(mm_done, h1_reader_spe))
                evn = op("scalar", lambda e, pp=pp, mt=mt:
                         e.activation(out=Bh1[:, mt * C:(mt + 1) * C], in_=pp[:, 0:C],
                                      func=AF.Gelu), SACT)
                bank_free[2 + qq] = (SACT, evn)
            hc_ready = SACT.n

            if phn == 0:
                w("scalar", SC[0], SC[0].n)
                w("scalar", SC[1], SC[1].n)
            for mt in range(DT):
                while seq == 1 and pump:
                    pump.pop(0)()
                wob, woslot, wosm, ldo = load_slab(wout_s, mt, big=False)
                w3b, w3slot, w3sm, ld3 = load_slab(w3_s, mt, big=True, split=2)
                qq = mt % 2
                pp = ps[qq]
                w("tensor", wosm, ldo[-1])
                w("tensor", w3sm, ld3[0])
                w("tensor", SDVE, states_ready)
                w("tensor", SACT, hc_ready)
                wait_banks("tensor", [qq])
                mm_done = None
                for kt in range(ET):
                    op("tensor", lambda e, pp=pp, wob=wob, kt=kt, seq=seq:
                       e.matmul(pp[:, 0:C], wob[:, kt * 128:(kt + 1) * 128],
                                states[:, kt * NPH + seq * C:kt * NPH + seq * C + C],
                                start=(kt == 0), stop=False))
                for kt in range(HT):
                    if kt == HT // 2:
                        w("tensor", w3sm, ld3[1])
                    mm_done = op("tensor", lambda e, pp=pp, w3b=w3b, kt=kt:
                                 e.matmul(pp[:, 0:C], w3b[:, kt * 128:(kt + 1) * 128],
                                          Bh1[:, kt * C:(kt + 1) * C],
                                          start=False, stop=(kt == HT - 1)),
                                 SPE if kt == HT - 1 else None)
                misc["states_reader"] = max(misc["states_reader"], SPE.n)
                misc["Bh1_pe"] = max(misc["Bh1_pe"], SPE.n)
                slab_free[woslot[0]][woslot[1]] = (SPE, SPE.n)
                slab_free[w3slot[0]][w3slot[1]] = (SPE, SPE.n)
                w("scalar", SPE, max(mm_done, misc["acc_pe"]))
                evn = op("scalar", lambda e, pp=pp, mt=mt:
                         e.copy(out=acc[:, mt * C:(mt + 1) * C], in_=pp[:, 0:C]), SACT)
                bank_free[qq] = (SACT, evn)
            acc_ready = SACT.n
            if dbg and phn == 0 and seq == 0:
                w("gpsimd", SACT, acc_ready)
                op("gpsimd", lambda e: e.dma_start(out=d_acc, in_=acc[:, :]), SDBG, 16)

            for ti in range(TI):
                tok0 = ch_f * C + ti * 128
                bq = ti % 2
                b0, b1 = (4, 5) if bq == 0 else (6, 7)
                half = DT // 2
                w("tensor", SACT, acc_ready)
                wait_banks("tensor", [b0, b1])
                tr_done = None
                for dt in range(DT):
                    bank = b0 if dt < half else b1
                    colp = (dt % half) * 128
                    tr_done = op("tensor", lambda e, bank=bank, colp=colp, dt=dt, ti=ti:
                                 e.transpose(ps[bank][:, colp:colp + 128],
                                             acc[:, dt * C + ti * 128:dt * C + ti * 128 + 128],
                                             cI[:, :]),
                                 SPE if dt == DT - 1 else None)
                misc["acc_pe"] = max(misc["acc_pe"], SPE.n)
                xsl, xldn = load_x(seq, tok0)
                osl = ti % 2
                w("vector", SPE, tr_done)
                w("vector", SXT[xsl], xldn)
                w("vector", SO[osl], o_slot_free[osl])
                op("vector", lambda e, osl=osl, xsl=xsl, b0=b0, half=half: e.tensor_tensor(
                    out=obuf[osl][:, 0:half * 128], in0=x_tm[xsl][:, 0:half * 128],
                    in1=ps[b0][:, 0:half * 128], op=ALU.add))
                addn = op("vector", lambda e, osl=osl, xsl=xsl, b1=b1, half=half: e.tensor_tensor(
                    out=obuf[osl][:, half * 128:D], in0=x_tm[xsl][:, half * 128:D],
                    in1=ps[b1][:, 0:(DT - half) * 128], op=ALU.add), SDVE)
                bank_free[b0] = (SDVE, addn)
                bank_free[b1] = (SDVE, addn)
                x_slot_free[xsl] = (SDVE, addn)
                w("gpsimd", SDVE, addn)
                op("gpsimd", lambda e, osl=osl, seq=seq, tok0=tok0:
                   e.dma_start(out=yv[seq * T + tok0:seq * T + tok0 + 128, :],
                               in_=obuf[osl][:, :]),
                   SO[osl], 16)
                o_slot_free[osl] = SO[osl].n

    ems, cur_cr = make_T_emitters(0)
    for em in ems:
        em()
    if dbg:
        w("gpsimd", SACT, SACT.n)
        op("gpsimd", lambda e: e.dma_start(out=d_xnfm, in_=xn_fm[:, :]), SDBG, 16)
        op("gpsimd", lambda e: e.dma_start(out=d_xnrev, in_=xn_rev[:, :]), SDBG, 16)
    for phn in range(PH):
        states_ready = stage_G(phn, cur_cr)
        if phn == 0:
            repack_rest()
        if dbg and phn == 0:
            w("gpsimd", SDVE, states_ready)
            op("gpsimd", lambda e: e.dma_start(out=d_states, in_=states[:, :]), SDBG, 16)
        if phn + 1 < PH:
            nems, ncr = make_T_emitters(phn + 1)
        else:
            nems, ncr = [], None
        stage_F(phn, states_ready, nems)
        for em in nems:
            em()
        cur_cr = ncr

    w("gpsimd", SO[0], SO[0].n)
    w("gpsimd", SO[1], SO[1].n)
    if dbg:
        w("gpsimd", SDBG, SDBG.n)

    with nc.Block() as block:
        @block.sync
        def _(e):
            for f in Q["sync"]:
                f(e)

        @block.scalar
        def _(e):
            for f in Q["scalar"]:
                f(e)

        @block.vector
        def _(e):
            for f in Q["vector"]:
                f(e)

        @block.tensor
        def _(e):
            for f in Q["tensor"]:
                f(e)

        @block.gpsimd
        def _(e):
            for f in Q["gpsimd"]:
                f(e)

    stk.close()
    return nc


def _layout_scale(s, MT):
    s = np.asarray(s, dtype=np.float32)
    return np.ascontiguousarray(s.reshape(MT, 128).T)


def kernel(L0_attn_x, L0_attn_W_hg, L0_attn_W_out, L0_ffn_w1_w, L0_ffn_w1_s,
           L0_ffn_w2_w, L0_ffn_w2_s, L0_ffn_h1_w, L0_ffn_h1_s, L0_ffn_h2_w,
           L0_ffn_h2_s, L0_ffn_w3_w, L0_ffn_w3_s):
    cfg = CFG
    B, T, D, E, H = cfg["B"], cfg["T"], cfg["D"], cfg["E"], cfg["H"]
    DT, HT = D // 128, H // 128

    x = np.ascontiguousarray(np.asarray(L0_attn_x, dtype=np.float32))
    w_hg = np.ascontiguousarray(np.asarray(L0_attn_W_hg, dtype=np.float32))
    w_out = np.ascontiguousarray(np.asarray(L0_attn_W_out, dtype=np.float32))
    w1 = np.ascontiguousarray(np.asarray(L0_ffn_w1_w, dtype=np.float32))
    w2 = np.ascontiguousarray(np.asarray(L0_ffn_w2_w, dtype=np.float32))
    h1 = np.ascontiguousarray(np.asarray(L0_ffn_h1_w, dtype=np.float32))
    h2 = np.ascontiguousarray(np.asarray(L0_ffn_h2_w, dtype=np.float32))
    w3 = np.ascontiguousarray(np.asarray(L0_ffn_w3_w, dtype=np.float32))
    s1 = _layout_scale(L0_ffn_w1_s, HT)
    s2 = _layout_scale(L0_ffn_w2_s, HT)
    h1s = _layout_scale(L0_ffn_h1_s, HT)
    h2s = _layout_scale(L0_ffn_h2_s, HT)
    w3s = _layout_scale(L0_ffn_w3_s, DT)
    cIm = np.eye(128, dtype=np.float32)
    cJm = np.ascontiguousarray(cIm[::-1, :])

    import gc
    nc = _build(cfg)
    gc.collect()
    in_maps = []
    for c in range(N_CORES):
        in_maps.append(dict(
            x=np.ascontiguousarray(x[c * B:(c + 1) * B]),
            w_hg=w_hg, w_out=w_out, w1=w1, w2=w2, h1=h1, h2=h2, w3=w3,
            s1=s1, s2=s2, h1s=h1s, h2s=h2s, w3s=w3s, cI=cIm, cJ=cJm,
        ))
    res = run_bass_kernel_spmd(nc, in_maps, core_ids=list(range(N_CORES)))
    return np.concatenate([r["y"] for r in res.results], axis=0)


# version 10 (final)
# speedup vs baseline: 1.4972x; 1.0795x over previous
import numpy as np
import concourse.bass as bass
import concourse.mybir as mybir
from concourse.bass_utils import run_bass_kernel_spmd

F32 = mybir.dt.float32
BF16 = mybir.dt.bfloat16
AF = mybir.ActivationFunctionType
ALU = mybir.AluOpType
AX = mybir.AxisListType

CFG = dict(B=2, T=4096, D=1024, E=1536, H=4096, C=512)
N_CORES = 8


class _S:
    def __init__(self, h):
        self.h = h
        self.n = 0


def _build(cfg):
    B, T, D, E, H, C = cfg["B"], cfg["T"], cfg["D"], cfg["E"], cfg["H"], cfg["C"]
    DT, ET, HT = D // 128, E // 128, H // 128
    PH = T // C
    TI = C // 128
    NPH = B * C
    assert C <= 512 and C % 128 == 0

    nc = bass.Bass("TRN2", target_bir_lowering=False, debug=False,
                   detect_race_conditions=False)

    xd = nc.dram_tensor("x", [B, T, D], F32, kind="ExternalInput").ap()
    yd = nc.dram_tensor("y", [B, T, D], F32, kind="ExternalOutput").ap()
    xv = xd.rearrange("b t d -> (b t) d")
    yv = yd.rearrange("b t d -> (b t) d")

    w_hg = nc.dram_tensor("w_hg", [2 * E, D], F32, kind="ExternalInput").ap()
    w_out = nc.dram_tensor("w_out", [D, E], F32, kind="ExternalInput").ap()
    w1 = nc.dram_tensor("w1", [H, D], F32, kind="ExternalInput").ap()
    w2 = nc.dram_tensor("w2", [H, D], F32, kind="ExternalInput").ap()
    h1 = nc.dram_tensor("h1", [H, H], F32, kind="ExternalInput").ap()
    h2 = nc.dram_tensor("h2", [H, H], F32, kind="ExternalInput").ap()
    w3 = nc.dram_tensor("w3", [D, H], F32, kind="ExternalInput").ap()
    s1d = nc.dram_tensor("s1", [128, HT], F32, kind="ExternalInput").ap()
    s2d = nc.dram_tensor("s2", [128, HT], F32, kind="ExternalInput").ap()
    h1sd = nc.dram_tensor("h1s", [128, HT], F32, kind="ExternalInput").ap()
    h2sd = nc.dram_tensor("h2s", [128, HT], F32, kind="ExternalInput").ap()
    w3sd = nc.dram_tensor("w3s", [128, DT], F32, kind="ExternalInput").ap()
    cId = nc.dram_tensor("cI", [128, 128], F32, kind="ExternalInput").ap()
    cJd = nc.dram_tensor("cJ", [128, 128], F32, kind="ExternalInput").ap()

    dbg = cfg.get("dbg", False)
    if dbg:
        d_xnfm = nc.dram_tensor("d_xnfm", [128, DT * NPH], BF16, kind="ExternalOutput").ap()
        d_xnrev = nc.dram_tensor("d_xnrev", [128, DT * NPH], BF16, kind="ExternalOutput").ap()
        d_states = nc.dram_tensor("d_states", [128, ET * NPH], BF16, kind="ExternalOutput").ap()
        d_ha = nc.dram_tensor("d_ha", [128, HT * C], BF16, kind="ExternalOutput").ap()
        d_acc = nc.dram_tensor("d_acc", [128, DT * C], F32, kind="ExternalOutput").ap()

    def scratch(name, MT, KT):
        t = nc.dram_tensor(name, [MT, 128, KT * 128], BF16, kind="Internal").ap()
        return (t, MT, KT)
    whgp_s = scratch("whgp_s", ET, 2 * DT)
    w12_s = scratch("w12_s", HT, 2 * DT)
    wout_s = scratch("wout_s", DT, ET)
    h1_s = scratch("h1_s", HT, HT)
    h2_s = scratch("h2_s", HT, HT)
    w3_s = scratch("w3_s", DT, HT)

    from contextlib import ExitStack
    stk = ExitStack()
    sb = lambda name, shape, dt=F32: stk.enter_context(nc.sbuf_tensor(name, shape, dt))

    cI = sb("cIt", [128, 128]); cJ = sb("cJt", [128, 128])
    s1t = sb("s1t", [128, HT]); s2t = sb("s2t", [128, HT])
    h1st = sb("h1st", [128, HT]); h2st = sb("h2st", [128, HT])
    w3st = sb("w3st", [128, DT])
    epsT = sb("epsT", [128, 1])
    x_tm = [sb(f"x_tm{i}", [128, D]) for i in range(3)]
    diagb = [sb(f"diagb{i}", [128, 128], BF16) for i in range(2)]
    cIb = sb("cIb", [128, 128], BF16)
    xn_tm = sb("xn_tm", [128, D])
    stat = sb("stat", [128, 8])
    xn_fm = sb("xn_fm", [128, DT * NPH], BF16)
    xn_rev = sb("xn_rev", [128, DT * NPH], BF16)
    states = sb("states", [128, ET * NPH], BF16)
    carry = sb("carry", [128, 4 * ET])
    zb = sb("zb", [128, C]); cbuf = sb("cbuf", [128, C]); sgb = sb("sgb", [128, C])
    tb = sb("tb", [128, C]); gbuf = sb("gbuf", [128, C])
    vbuf = sgb
    fbA = sb("fbA", [128, C]); fbB = sb("fbB", [128, C])
    Bh1 = sb("Bh1", [128, HT * C], BF16)
    Bh2 = sb("Bh2", [128, HT * C], BF16)
    acc = sb("acc", [128, DT * C])
    obuf = [sb(f"obuf{i}", [128, D]) for i in range(2)]
    slab_big = [sb(f"slabB{i}", [128, HT * 128], BF16) for i in range(2)]
    slab_sml = [sb(f"slabS{i}", [128, max(2 * DT, ET) * 128], BF16) for i in range(3)]
    tsil = sb("tsil", [128, C])

    ps = [stk.enter_context(nc.psum_tensor(f"ps{i}", [128, 512], F32)) for i in range(8)]

    mksem = lambda name: _S(stk.enter_context(nc.semaphore(name)))
    SXC = mksem("SXC")
    SXT = [mksem(f"SXT{i}") for i in range(4)]
    SN = [mksem(f"SN{i}") for i in range(4)]
    SC = [mksem(f"SC{i}") for i in range(2)]
    SBG = [mksem(f"SBG{i}") for i in range(3)]
    SSL = [mksem(f"SSL{i}") for i in range(3)]
    SO = [mksem(f"SO{i}") for i in range(2)]
    SPE = mksem("SPE")
    SACT = mksem("SACT")
    SDVE = mksem("SDVE")
    SDBG = mksem("SDBG")

    Q = dict(sync=[], scalar=[], vector=[], tensor=[], gpsimd=[])
    _last_wait = {}

    def w(eng, sem, val):
        if val <= 0:
            return
        key = (eng, id(sem))
        if _last_wait.get(key, 0) >= val:
            return
        _last_wait[key] = val
        Q[eng].append(lambda e, s=sem, v=val: e.wait_ge(s.h, v))

    def op(eng, f, inc=None, n=1):
        if inc is not None:
            inc.n += n
            Q[eng].append(lambda e, f=f, s=inc, n=n: f(e).then_inc(s.h, n))
            return inc.n
        Q[eng].append(lambda e, f=f: f(e))
        return None

    bank_free = {i: (SPE, 0) for i in range(8)}

    def wait_banks(eng, banks):
        for b in banks:
            sm, v = bank_free[b]
            w(eng, sm, v)

    for dst, src in ((cI, cId), (cJ, cJd), (s1t, s1d), (s2t, s2d),
                     (h1st, h1sd), (h2st, h2sd), (w3st, w3sd)):
        op("sync", lambda e, d=dst, s=src: e.dma_start(out=d[:, :], in_=s), SXC, 16)
    setup_loads = SXC.n
    op("vector", lambda e: e.memset(carry[:, :], 0.0), SDVE)
    w("vector", SXC, SXC.n)
    op("vector", lambda e: e.tensor_copy(out=cIb[:, :], in_=cI[:, :]), SDVE)
    op("vector", lambda e: e.memset(epsT[:, :], 1e-5), SDVE)

    maxK = max(D, E, H)
    if HT * C // 2 >= 2 * maxK and 2 * DT * C >= 2 * maxK:
        natw = [Bh1[:].bitcast(F32)[:, 0:maxK], Bh1[:].bitcast(F32)[:, maxK:2 * maxK]]
        natwb = [Bh2[:, i * maxK:(i + 1) * maxK] for i in range(4)]
        staging = [acc[:].bitcast(BF16)[:, 0:maxK], acc[:].bitcast(BF16)[:, maxK:2 * maxK]]
    else:
        natw_t = sb("natw_t", [128, 2 * maxK])
        natwb_t = sb("natwb_t", [128, 4 * maxK], BF16)
        staging_t = sb("staging_t", [128, 2 * maxK], BF16)
        natw = [natw_t[:, 0:maxK], natw_t[:, maxK:2 * maxK]]
        natwb = [natwb_t[:, i * maxK:(i + 1) * maxK] for i in range(4)]
        staging = [staging_t[:, 0:maxK], staging_t[:, maxK:2 * maxK]]
    rp = dict(pe_done=[0, 0, 0, 0], cast_done=[0, 0], sc_done=[0, 0],
              diag_free=[0, 0], i=0)

    w("scalar", SXC, setup_loads)
    w("tensor", SXC, setup_loads)
    w("vector", SXC, setup_loads)

    scr_marks = {}

    def repack_mt(wdram, scr3, KTm, scale_t, row0, col_off, mt):
        scr, MT, KTtot = scr3
        K = KTm * 128
        if True:
            it = rp["i"]
            rp["i"] += 1
            slot = it % 2
            bslot = it % 4
            stg = it % 2
            w("sync", SACT, rp["cast_done"][slot])
            w("sync", SC[stg], rp["sc_done"][stg])
            op("sync", lambda e, sl=slot, mt=mt, K=K, wd=wdram, row0=row0:
               e.dma_start(out=natw[sl][:, 0:K],
                           in_=wd[row0 + mt * 128:row0 + (mt + 1) * 128, 0:K]),
               SN[slot], 16)
            ldn = SN[slot].n
            w("scalar", SN[slot], ldn)
            w("scalar", SPE, rp["pe_done"][bslot])
            cst = op("scalar", lambda e, sl=slot, bs=bslot, K=K:
                     e.copy(out=natwb[bs][:, 0:K], in_=natw[sl][:, 0:K]), SACT)
            rp["cast_done"][slot] = cst
            if scale_t is not None:
                dq = it % 2
                w("vector", SPE, rp["diag_free"][dq])
                dgn = op("vector", lambda e, dq=dq, st_=scale_t, mt=mt:
                         e.tensor_scalar(out=diagb[dq][:, :], in0=cI[:, :],
                                         scalar1=st_[:, mt:mt + 1], scalar2=None,
                                         op0=ALU.mult), SDVE)
                w("tensor", SDVE, dgn)
                idop = diagb[dq]
            else:
                idop = cIb
            w("tensor", SACT, cst)
            srcb = natwb[bslot]
            evs = []
            for kt in range(KTm):
                bq = (kt // 4) % 2
                if kt % 4 == 0:
                    wait_banks("tensor", [bq])
                last = op("tensor", lambda e, kt=kt, bq=bq, srcb=srcb, idop=idop:
                          e.matmul(ps[bq][:, (kt % 4) * 128:(kt % 4) * 128 + 128],
                                   srcb[:, kt * 128:(kt + 1) * 128], idop[:, :],
                                   start=True, stop=True),
                          SPE)
                if kt % 4 == 3 or kt == KTm - 1:
                    nt = (kt % 4) + 1
                    w("scalar", SPE, last)
                    if kt < 4:
                        w("scalar", SC[stg], rp["sc_done"][stg])
                    evn = op("scalar", lambda e, sl=stg, kt=kt, bq=bq, nt=nt:
                             e.copy(out=staging[sl][:, (kt - nt + 1) * 128:(kt + 1) * 128],
                                    in_=ps[bq][:, 0:nt * 128]), SACT)
                    bank_free[bq] = (SACT, evn)
                    evs.append(evn)
            rp["pe_done"][bslot] = SPE.n
            if scale_t is not None:
                rp["diag_free"][it % 2] = SPE.n
            w("gpsimd", SACT, evs[-1])
            op("gpsimd", lambda e, sl=stg, mt=mt, K=K, scr=scr, co=col_off * 128:
               e.dma_start(out=scr[mt:mt + 1, :, co:co + K].rearrange("m p j -> (m p) j"),
                           in_=staging[sl][:, 0:K]),
               SC[stg], 16)
            rp["sc_done"][stg] = SC[stg].n
            scr_marks.setdefault((id(scr3[0]), mt), []).append((SC[stg], SC[stg].n))

    def repack(wdram, scr3, KTm, scale_t, row0=0, col_off=0):
        for mt in range(scr3[1]):
            repack_mt(wdram, scr3, KTm, scale_t, row0, col_off, mt)

    repack(w_hg, whgp_s, DT, None, row0=0, col_off=0)
    repack(w_hg, whgp_s, DT, None, row0=E, col_off=DT)

    def make_rp_emitters():
        ems = []
        for (wd, scr3, KTm, st_, r0, co) in (
                (w1, w12_s, DT, s1t, 0, 0),
                (w2, w12_s, DT, s2t, 0, DT),
                (w_out, wout_s, ET, None, 0, 0),
                (h1, h1_s, HT, h1st, 0, 0),
                (h2, h2_s, HT, h2st, 0, 0),
                (w3, w3_s, HT, w3st, 0, 0)):
            for mt in range(scr3[1]):
                ems.append(lambda wd=wd, scr3=scr3, KTm=KTm, st_=st_, r0=r0, co=co, mt=mt:
                           repack_mt(wd, scr3, KTm, st_, r0, co, mt))
        return ems

    slab_free = {"big": [(SPE, 0)] * 3, "sml": [(SPE, 0)] * 3}
    slab_idx = {"big": 0, "sml": 0}

    def load_slab(scr3, mt, big, split=1):
        scr, MT, KT = scr3
        kind = "big" if big else "sml"
        pool = slab_big if big else slab_sml
        sgrp = SBG if big else SSL
        sl = slab_idx[kind] % len(pool)
        slab_idx[kind] += 1
        sm, v = slab_free[kind][sl]
        w("sync", sm, v)
        for msem, mval in scr_marks.get((id(scr), mt), ()):
            w("sync", msem, mval)
        marks = []
        step = KT // split
        for s in range(split):
            c0 = s * step * 128
            c1 = (s + 1) * step * 128 if s < split - 1 else KT * 128
            op("sync", lambda e, sl=sl, mt=mt, scr=scr, pool=pool, c0=c0, c1=c1:
               e.dma_start(out=pool[sl][:, c0:c1],
                           in_=scr[mt:mt + 1, :, c0:c1].rearrange("m p j -> (m p) j")),
               sgrp[sl], 16)
            marks.append(sgrp[sl].n)
        return pool[sl], (kind, sl), sgrp[sl], marks

    x_slot_free = [(SDVE, 0)] * 4
    o_slot_free = [0, 0]
    xnfm_reader = 0
    xnrev_reader = 0
    states_reader = 0
    Bh1_pe_reader = 0
    Bh2_pe_reader = 0
    acc_pe_reader = 0
    tsil_reader = [0, 0]
    prev_scanbufs_free = 0
    prev_stat_done = 0
    prev_xn_tr = 0
    n_groups = 0
    x_rr = dict(i=0)

    def load_x(seq, tok0):
        sl = x_rr["i"] % len(x_tm)
        x_rr["i"] += 1
        sm, v = x_slot_free[sl]
        w("sync", sm, v)
        op("sync", lambda e, sl=sl, seq=seq, tok0=tok0:
           e.dma_start(out=x_tm[sl][:, :],
                       in_=xv[seq * T + tok0:seq * T + tok0 + 128, :]),
           SXT[sl], 16)
        return sl, SXT[sl].n

    misc = dict(prev_stat_done=0, prev_xn_tr=0, xnfm_reader=0, xnrev_reader=0,
                states_reader=0, Bh1_pe=0, Bh2_pe=0, acc_pe=0, tsil_rd=0,
                scanbufs_free=0, tt=0, tt_b=0, pending_xn=0, n_groups=0)

    def emit_T_ln(seq, kind, chunk, ti, cr, first):
        if first["v"]:
            w("scalar", SPE, max(misc["xnfm_reader"], misc["xnrev_reader"]))
            first["v"] = False
        tok0 = chunk * C + ti * 128
        sl, ldn = load_x(seq, tok0)
        tt = misc["tt"]
        w("vector", SXT[sl], ldn)
        op("vector", lambda e, sl=sl:
           e.reduce_sum(stat[:, 0:1], x_tm[sl][:, :], axis=AX.X), SDVE)
        osl = tt % 2
        w("scalar", SXT[sl], ldn)
        w("scalar", SO[osl], o_slot_free[osl])
        w("scalar", SDVE, misc["prev_stat_done"])
        sqn = op("scalar", lambda e, sl=sl, osl=osl:
                 e.activation(out=obuf[osl][:, :], in_=x_tm[sl][:, :],
                              func=AF.Square), SACT)
        w("vector", SACT, sqn)
        s1n = op("vector", lambda e, osl=osl:
                 e.reduce_sum(stat[:, 1:2], obuf[osl][:, :], axis=AX.X), SDVE)
        w("scalar", SDVE, s1n)
        op("scalar", lambda e: e.activation(out=stat[:, 2:3], in_=stat[:, 0:1],
                                            func=AF.Square, scale=1.0 / D), SACT)
        op("scalar", lambda e: e.activation(out=stat[:, 3:4], in_=stat[:, 0:1],
                                            func=AF.Copy, scale=1.0 / D), SACT)
        msn = op("scalar", lambda e: e.activation(out=stat[:, 4:5], in_=stat[:, 1:2],
                                                  func=AF.Copy, scale=1.0 / D), SACT)
        w("vector", SACT, msn)
        vn = op("vector", lambda e: e.tensor_tensor(out=stat[:, 5:6], in0=stat[:, 4:5],
                                                    in1=stat[:, 2:3], op=ALU.subtract), SDVE)
        w("scalar", SDVE, vn)
        op("scalar", lambda e: e.activation(out=stat[:, 6:7], in_=stat[:, 5:6],
                                            func=AF.Sqrt, bias=epsT[:, 0:1]), SACT)
        w("scalar" if False else "vector", SACT, SACT.n)
        rcn = op("vector", lambda e: e.reciprocal(out=stat[:, 7:8], in_=stat[:, 6:7]), SDVE)
        w("scalar", SDVE, rcn)
        rbn = op("scalar", lambda e: e.activation(out=stat[:, 5:6], in_=stat[:, 7:8],
                                                  func=AF.Copy), SACT)
        w("vector", SACT, rbn)
        w("vector", SPE, misc["prev_xn_tr"])
        xn_done = op("vector", lambda e, sl=sl: e.tensor_scalar(
            out=xn_tm[:, :], in0=x_tm[sl][:, :], scalar1=stat[:, 3:4],
            scalar2=stat[:, 5:6], op0=ALU.subtract, op1=ALU.mult), SDVE)
        misc["prev_stat_done"] = xn_done
        x_slot_free[sl] = (SDVE, xn_done)
        misc["pending_xn"] = xn_done
        misc["tt"] = tt + 1

    def emit_T_tr(seq, kind, ti, cr):
        tt = misc["tt_b"]
        xn_done = misc["pending_xn"]
        bq = tt % 2
        b0, b1 = (4, 5) if bq == 0 else (6, 7)
        w("tensor", SDVE, xn_done)
        wait_banks("tensor", [b0, b1])
        idmat = cI if kind == "f" else cJ
        half = DT // 2
        for dt in range(DT):
            bank = b0 if dt < half else b1
            colp = (dt % half) * 128
            op("tensor", lambda e, bank=bank, colp=colp, dt=dt, idmat=idmat:
               e.transpose(ps[bank][:, colp:colp + 128],
                           xn_tm[:, dt * 128:(dt + 1) * 128], idmat[:, :]), SPE)
        tr_done = SPE.n
        misc["prev_xn_tr"] = tr_done
        dst_t = xn_fm if kind == "f" else xn_rev
        cti = ti if kind == "f" else (TI - 1 - ti)
        col = seq * C + cti * 128
        w("scalar", SPE, tr_done)
        ev1 = op("scalar", lambda e, b0=b0, dst_t=dst_t, col=col, half=half:
                 e.copy(out=dst_t[:, :].rearrange("p (dt n) -> p dt n", n=NPH)[:, 0:half, col:col + 128],
                        in_=ps[b0][:, 0:half * 128].rearrange("p (dt j) -> p dt j", j=128)),
                 SACT)
        ev2 = op("scalar", lambda e, b1=b1, dst_t=dst_t, col=col, half=half:
                 e.copy(out=dst_t[:, :].rearrange("p (dt n) -> p dt n", n=NPH)[:, half:DT, col:col + 128],
                        in_=ps[b1][:, 0:(DT - half) * 128].rearrange("p (dt j) -> p dt j", j=128)),
                 SACT)
        bank_free[b0] = (SACT, ev1)
        bank_free[b1] = (SACT, ev2)
        misc["tt_b"] = tt + 1
        cr[(kind, seq)] = SACT.n

    def make_T_emitters(phn):
        cr = {}
        first = dict(v=True)
        ems = []
        ch_f, ch_b = phn, PH - 1 - phn
        for seq in range(B):
            for kind, chunk in (("f", ch_f), ("b", ch_b)):
                for ti in range(TI):
                    ems.append(lambda seq=seq, kind=kind, chunk=chunk, ti=ti:
                               emit_T_ln(seq, kind, chunk, ti, cr, first))
                    ems.append(lambda seq=seq, kind=kind, ti=ti:
                               emit_T_tr(seq, kind, ti, cr))
        return ems, cr

    def stage_G(phn, chunk_ready, pump_rp=None):
        w("vector", SPE, misc["states_reader"])
        for pr in range(ET):
            pair_slab, pair_slot, pair_sm, pair_ld = load_slab(whgp_s, pr, big=False)
            for dir_, seq in ((0, 0), (1, 0), (0, 1), (1, 1)):
                qq = misc["n_groups"] % 4
                p_hid, p_gat = ps[2 * qq], ps[2 * qq + 1]
                src = xn_fm if dir_ == 0 else xn_rev
                w("tensor", pair_sm, pair_ld[-1])
                w("tensor", SACT, chunk_ready[("f" if dir_ == 0 else "b", seq)])
                wait_banks("tensor", [2 * qq, 2 * qq + 1])
                hg_done = None
                for which, pps in ((0, p_hid), (1, p_gat)):
                    for kt in range(DT):
                        is_last = (which == 1 and kt == DT - 1)
                        ko = which * DT + kt
                        hg_done = op("tensor", lambda e, pps=pps, slabb=pair_slab, ko=ko, kt=kt, src=src, seq=seq:
                                     e.matmul(pps[:, 0:C],
                                              slabb[:, ko * 128:(ko + 1) * 128],
                                              src[:, kt * NPH + seq * C:kt * NPH + seq * C + C],
                                              start=(kt == 0), stop=(kt == DT - 1)),
                                     SPE if is_last else None)
                if dir_ == 0:
                    misc["xnfm_reader"] = max(misc["xnfm_reader"], SPE.n)
                else:
                    misc["xnrev_reader"] = max(misc["xnrev_reader"], SPE.n)
                if (dir_, seq) == (1, 1):
                    slab_free[pair_slot[0]][pair_slot[1]] = (SPE, SPE.n)
                w("scalar", SPE, hg_done)
                w("scalar", SDVE, misc["scanbufs_free"])
                sgn = op("scalar", lambda e, p_hid=p_hid: e.activation(out=sgb[:, 0:C], in_=p_hid[:, 0:C],
                                                                       func=AF.Sigmoid), SACT)
                zbn = op("scalar", lambda e, p_gat=p_gat: e.activation(out=zb[:, 0:C], in_=p_gat[:, 0:C],
                                                                       func=AF.Sigmoid), SACT)
                cbn = op("scalar", lambda e, p_gat=p_gat: e.activation(out=cbuf[:, 0:C], in_=p_gat[:, 0:C],
                                                                       func=AF.Sigmoid, scale=-1.0), SACT)
                bank_free[2 * qq + 1] = (SACT, cbn)
                w("vector", SACT, sgn)
                tbn = op("vector", lambda e, p_hid=p_hid: e.scalar_tensor_tensor(
                    out=tb[:, 0:C], in0=p_hid[:, 0:C], scalar=0.5, in1=sgb[:, 0:C],
                    op0=ALU.add, op1=ALU.subtract), SDVE)
                bank_free[2 * qq] = (SDVE, tbn)
                op("vector", lambda e: e.scalar_tensor_tensor(
                    out=gbuf[:, 0:C], in0=tb[:, 0:C], scalar=0.0, in1=sgb[:, 0:C],
                    op0=ALU.max, op1=ALU.add))
                w("vector", SACT, zbn)
                op("vector", lambda e: e.tensor_tensor(out=vbuf[:, 0:C], in0=zb[:, 0:C],
                                                       in1=gbuf[:, 0:C], op=ALU.mult))
                fb = fbA if dir_ == 0 else fbB
                cc = (dir_ * B + seq) * ET + pr
                w("vector", SACT, cbn)
                scn = op("vector", lambda e, fb=fb, cc=cc: e.tensor_tensor_scan(
                    out=fb[:, 0:C], data0=cbuf[:, 0:C], data1=vbuf[:, 0:C],
                    initial=carry[:, cc:cc + 1], op0=ALU.mult, op1=ALU.add), SDVE)
                misc["scanbufs_free"] = scn
                if dir_ == 1:
                    op("vector", lambda e, pr=pr, seq=seq: e.tensor_tensor(
                        out=states[:, pr * NPH + seq * C:pr * NPH + seq * C + C],
                        in0=fbA[:, 0:C], in1=fbB[:, 0:C], op=ALU.add), SDVE)
                    ccf = (0 * B + seq) * ET + pr
                    op("vector", lambda e, ccf=ccf:
                       e.tensor_copy(out=carry[:, ccf:ccf + 1], in_=fbA[:, C - 1:C]))
                    op("vector", lambda e, cc=cc:
                       e.tensor_copy(out=carry[:, cc:cc + 1], in_=fbB[:, C - 1:C]))
                misc["n_groups"] += 1
                if pump_rp:
                    for _ in range(3):
                        if pump_rp:
                            pump_rp.pop(0)()
        return SDVE.n

    def stage_F(phn, states_ready, pump):
        ch_f = phn
        for seq in range(B):
            for mt in range(HT):
                s12b, s12slot, s12sm, ld12 = load_slab(w12_s, mt, big=False)
                qq = mt % 2
                pw1, pw2 = ps[2 * qq], ps[2 * qq + 1]
                w("tensor", s12sm, ld12[-1])
                wait_banks("tensor", [2 * qq, 2 * qq + 1])
                w1_done = w2_done = None
                for which, pps in ((0, pw1), (1, pw2)):
                    for kt in range(DT):
                        ko = which * DT + kt
                        mk = op("tensor", lambda e, pps=pps, slabb=s12b, ko=ko, kt=kt, seq=seq:
                                e.matmul(pps[:, 0:C], slabb[:, ko * 128:(ko + 1) * 128],
                                         xn_fm[:, kt * NPH + seq * C:kt * NPH + seq * C + C],
                                         start=(kt == 0), stop=(kt == DT - 1)),
                                SPE if kt == DT - 1 else None)
                        if kt == DT - 1:
                            if which == 0:
                                w1_done = mk
                            else:
                                w2_done = mk
                misc["xnfm_reader"] = max(misc["xnfm_reader"], SPE.n)
                slab_free[s12slot[0]][s12slot[1]] = (SPE, SPE.n)
                w("scalar", SPE, w1_done)
                w("scalar", SDVE, misc["tsil_rd"])
                sil = op("scalar", lambda e, pw1=pw1:
                         e.activation(out=tsil[:, 0:C], in_=pw1[:, 0:C],
                                      func=AF.Silu), SACT)
                bank_free[2 * qq] = (SACT, sil)
                w("vector", SACT, sil)
                w("vector", SPE, max(w2_done, misc["Bh1_pe"]))
                ha = op("vector", lambda e, pw2=pw2, mt=mt: e.tensor_tensor(
                    out=Bh1[:, mt * C:(mt + 1) * C], in0=tsil[:, 0:C],
                    in1=pw2[:, 0:C], op=ALU.mult), SDVE)
                misc["tsil_rd"] = ha
                bank_free[2 * qq + 1] = (SDVE, ha)
            ha_ready = SDVE.n
            if dbg and phn == 0 and seq == 0:
                w("gpsimd", SDVE, ha_ready)
                op("gpsimd", lambda e: e.dma_start(out=d_ha, in_=Bh1[:, :]), SDBG, 16)

            for mt in range(HT):
                if seq == 1 and pump:
                    pump.pop(0)()
                hb, hslot, hsm, ldh = load_slab(h1_s, mt, big=True, split=2)
                qq = mt % 2
                pp = ps[qq]
                w("tensor", hsm, ldh[0])
                w("tensor", SDVE, ha_ready)
                wait_banks("tensor", [qq])
                mm_done = None
                for kt in range(HT):
                    if kt == HT // 2:
                        w("tensor", hsm, ldh[1])
                    mm_done = op("tensor", lambda e, pp=pp, hb=hb, kt=kt:
                                 e.matmul(pp[:, 0:C], hb[:, kt * 128:(kt + 1) * 128],
                                          Bh1[:, kt * C:(kt + 1) * C],
                                          start=(kt == 0), stop=(kt == HT - 1)),
                                 SPE if kt == HT - 1 else None)
                misc["Bh1_pe"] = max(misc["Bh1_pe"], SPE.n)
                slab_free[hslot[0]][hslot[1]] = (SPE, SPE.n)
                w("scalar", SPE, max(mm_done, misc["Bh2_pe"]))
                evn = op("scalar", lambda e, pp=pp, mt=mt:
                         e.activation(out=Bh2[:, mt * C:(mt + 1) * C], in_=pp[:, 0:C],
                                      func=AF.Gelu), SACT)
                bank_free[qq] = (SACT, evn)
            hb_ready = SACT.n

            h1_reader_spe = misc["Bh1_pe"]
            for mt in range(HT):
                if seq == 1 and pump:
                    pump.pop(0)()
                hb, hslot, hsm, ldh = load_slab(h2_s, mt, big=True, split=2)
                qq = mt % 2
                pp = ps[2 + qq]
                w("tensor", hsm, ldh[0])
                w("tensor", SACT, hb_ready)
                wait_banks("tensor", [2 + qq])
                mm_done = None
                for kt in range(HT):
                    if kt == HT // 2:
                        w("tensor", hsm, ldh[1])
                    mm_done = op("tensor", lambda e, pp=pp, hb=hb, kt=kt:
                                 e.matmul(pp[:, 0:C], hb[:, kt * 128:(kt + 1) * 128],
                                          Bh2[:, kt * C:(kt + 1) * C],
                                          start=(kt == 0), stop=(kt == HT - 1)),
                                 SPE if kt == HT - 1 else None)
                misc["Bh2_pe"] = max(misc["Bh2_pe"], SPE.n)
                slab_free[hslot[0]][hslot[1]] = (SPE, SPE.n)
                w("scalar", SPE, max(mm_done, h1_reader_spe))
                evn = op("scalar", lambda e, pp=pp, mt=mt:
                         e.activation(out=Bh1[:, mt * C:(mt + 1) * C], in_=pp[:, 0:C],
                                      func=AF.Gelu), SACT)
                bank_free[2 + qq] = (SACT, evn)
            hc_ready = SACT.n

            if phn == 0:
                w("scalar", SC[0], SC[0].n)
                w("scalar", SC[1], SC[1].n)
            for mt in range(DT):
                while seq == 1 and pump:
                    pump.pop(0)()
                wob, woslot, wosm, ldo = load_slab(wout_s, mt, big=False)
                w3b, w3slot, w3sm, ld3 = load_slab(w3_s, mt, big=True, split=2)
                qq = mt % 2
                pp = ps[qq]
                w("tensor", wosm, ldo[-1])
                w("tensor", w3sm, ld3[0])
                w("tensor", SDVE, states_ready)
                w("tensor", SACT, hc_ready)
                wait_banks("tensor", [qq])
                mm_done = None
                for kt in range(ET):
                    op("tensor", lambda e, pp=pp, wob=wob, kt=kt, seq=seq:
                       e.matmul(pp[:, 0:C], wob[:, kt * 128:(kt + 1) * 128],
                                states[:, kt * NPH + seq * C:kt * NPH + seq * C + C],
                                start=(kt == 0), stop=False))
                for kt in range(HT):
                    if kt == HT // 2:
                        w("tensor", w3sm, ld3[1])
                    mm_done = op("tensor", lambda e, pp=pp, w3b=w3b, kt=kt:
                                 e.matmul(pp[:, 0:C], w3b[:, kt * 128:(kt + 1) * 128],
                                          Bh1[:, kt * C:(kt + 1) * C],
                                          start=False, stop=(kt == HT - 1)),
                                 SPE if kt == HT - 1 else None)
                misc["states_reader"] = max(misc["states_reader"], SPE.n)
                misc["Bh1_pe"] = max(misc["Bh1_pe"], SPE.n)
                slab_free[woslot[0]][woslot[1]] = (SPE, SPE.n)
                slab_free[w3slot[0]][w3slot[1]] = (SPE, SPE.n)
                w("scalar", SPE, max(mm_done, misc["acc_pe"]))
                evn = op("scalar", lambda e, pp=pp, mt=mt:
                         e.copy(out=acc[:, mt * C:(mt + 1) * C], in_=pp[:, 0:C]), SACT)
                bank_free[qq] = (SACT, evn)
            acc_ready = SACT.n
            if dbg and phn == 0 and seq == 0:
                w("gpsimd", SACT, acc_ready)
                op("gpsimd", lambda e: e.dma_start(out=d_acc, in_=acc[:, :]), SDBG, 16)

            for ti in range(TI):
                tok0 = ch_f * C + ti * 128
                bq = ti % 2
                b0, b1 = (4, 5) if bq == 0 else (6, 7)
                half = DT // 2
                w("tensor", SACT, acc_ready)
                wait_banks("tensor", [b0, b1])
                tr_done = None
                for dt in range(DT):
                    bank = b0 if dt < half else b1
                    colp = (dt % half) * 128
                    tr_done = op("tensor", lambda e, bank=bank, colp=colp, dt=dt, ti=ti:
                                 e.transpose(ps[bank][:, colp:colp + 128],
                                             acc[:, dt * C + ti * 128:dt * C + ti * 128 + 128],
                                             cI[:, :]),
                                 SPE if dt == DT - 1 else None)
                misc["acc_pe"] = max(misc["acc_pe"], SPE.n)
                xsl, xldn = load_x(seq, tok0)
                osl = ti % 2
                w("vector", SPE, tr_done)
                w("vector", SXT[xsl], xldn)
                w("vector", SO[osl], o_slot_free[osl])
                op("vector", lambda e, osl=osl, xsl=xsl, b0=b0, half=half: e.tensor_tensor(
                    out=obuf[osl][:, 0:half * 128], in0=x_tm[xsl][:, 0:half * 128],
                    in1=ps[b0][:, 0:half * 128], op=ALU.add))
                addn = op("vector", lambda e, osl=osl, xsl=xsl, b1=b1, half=half: e.tensor_tensor(
                    out=obuf[osl][:, half * 128:D], in0=x_tm[xsl][:, half * 128:D],
                    in1=ps[b1][:, 0:(DT - half) * 128], op=ALU.add), SDVE)
                bank_free[b0] = (SDVE, addn)
                bank_free[b1] = (SDVE, addn)
                x_slot_free[xsl] = (SDVE, addn)
                w("gpsimd", SDVE, addn)
                op("gpsimd", lambda e, osl=osl, seq=seq, tok0=tok0:
                   e.dma_start(out=yv[seq * T + tok0:seq * T + tok0 + 128, :],
                               in_=obuf[osl][:, :]),
                   SO[osl], 16)
                o_slot_free[osl] = SO[osl].n

    ems, cur_cr = make_T_emitters(0)
    for em in ems:
        em()
    if dbg:
        w("gpsimd", SACT, SACT.n)
        op("gpsimd", lambda e: e.dma_start(out=d_xnfm, in_=xn_fm[:, :]), SDBG, 16)
        op("gpsimd", lambda e: e.dma_start(out=d_xnrev, in_=xn_rev[:, :]), SDBG, 16)
    rp_ems = make_rp_emitters()
    for phn in range(PH):
        states_ready = stage_G(phn, cur_cr, pump_rp=rp_ems if phn == 0 else None)
        if phn == 0:
            for em in rp_ems:
                em()
            rp_ems = []
        if dbg and phn == 0:
            w("gpsimd", SDVE, states_ready)
            op("gpsimd", lambda e: e.dma_start(out=d_states, in_=states[:, :]), SDBG, 16)
        if phn + 1 < PH:
            nems, ncr = make_T_emitters(phn + 1)
        else:
            nems, ncr = [], None
        stage_F(phn, states_ready, nems)
        for em in nems:
            em()
        cur_cr = ncr

    w("gpsimd", SO[0], SO[0].n)
    w("gpsimd", SO[1], SO[1].n)
    if dbg:
        w("gpsimd", SDBG, SDBG.n)

    with nc.Block() as block:
        @block.sync
        def _(e):
            for f in Q["sync"]:
                f(e)

        @block.scalar
        def _(e):
            for f in Q["scalar"]:
                f(e)

        @block.vector
        def _(e):
            for f in Q["vector"]:
                f(e)

        @block.tensor
        def _(e):
            for f in Q["tensor"]:
                f(e)

        @block.gpsimd
        def _(e):
            for f in Q["gpsimd"]:
                f(e)

    stk.close()
    return nc


def _layout_scale(s, MT):
    s = np.asarray(s, dtype=np.float32)
    return np.ascontiguousarray(s.reshape(MT, 128).T)


def kernel(L0_attn_x, L0_attn_W_hg, L0_attn_W_out, L0_ffn_w1_w, L0_ffn_w1_s,
           L0_ffn_w2_w, L0_ffn_w2_s, L0_ffn_h1_w, L0_ffn_h1_s, L0_ffn_h2_w,
           L0_ffn_h2_s, L0_ffn_w3_w, L0_ffn_w3_s):
    cfg = CFG
    B, T, D, E, H = cfg["B"], cfg["T"], cfg["D"], cfg["E"], cfg["H"]
    DT, HT = D // 128, H // 128

    x = np.ascontiguousarray(np.asarray(L0_attn_x, dtype=np.float32))
    w_hg = np.ascontiguousarray(np.asarray(L0_attn_W_hg, dtype=np.float32))
    w_out = np.ascontiguousarray(np.asarray(L0_attn_W_out, dtype=np.float32))
    w1 = np.ascontiguousarray(np.asarray(L0_ffn_w1_w, dtype=np.float32))
    w2 = np.ascontiguousarray(np.asarray(L0_ffn_w2_w, dtype=np.float32))
    h1 = np.ascontiguousarray(np.asarray(L0_ffn_h1_w, dtype=np.float32))
    h2 = np.ascontiguousarray(np.asarray(L0_ffn_h2_w, dtype=np.float32))
    w3 = np.ascontiguousarray(np.asarray(L0_ffn_w3_w, dtype=np.float32))
    s1 = _layout_scale(L0_ffn_w1_s, HT)
    s2 = _layout_scale(L0_ffn_w2_s, HT)
    h1s = _layout_scale(L0_ffn_h1_s, HT)
    h2s = _layout_scale(L0_ffn_h2_s, HT)
    w3s = _layout_scale(L0_ffn_w3_s, DT)
    cIm = np.eye(128, dtype=np.float32)
    cJm = np.ascontiguousarray(cIm[::-1, :])

    import gc
    nc = _build(cfg)
    gc.collect()
    in_maps = []
    for c in range(N_CORES):
        in_maps.append(dict(
            x=np.ascontiguousarray(x[c * B:(c + 1) * B]),
            w_hg=w_hg, w_out=w_out, w1=w1, w2=w2, h1=h1, h2=h2, w3=w3,
            s1=s1, s2=s2, h1s=h1s, h2s=h2s, w3s=w3s, cI=cIm, cJ=cJm,
        ))
    res = run_bass_kernel_spmd(nc, in_maps, core_ids=list(range(N_CORES)))
    return np.concatenate([r["y"] for r in res.results], axis=0)
